# version 25
# baseline (speedup 1.0000x reference)
import numpy as np
import ml_dtypes
import concourse.bass as bass
import concourse.mybir as mybir
from concourse.bass_utils import run_bass_kernel_spmd
from contextlib import ExitStack

F32 = mybir.dt.float32
BF16 = mybir.dt.bfloat16
ALU = mybir.AluOpType
ACT = mybir.ActivationFunctionType
AX = mybir.AxisListType

FP8_MAX = 448.0
FP4_MAX = 6.0
GSC = FP8_MAX * FP4_MAX
EXP_BIAS = -8.0

ENGINES = ("sync", "tensor", "vector", "scalar", "gpsimd")


class Script:

    def __init__(self):
        self.ops = {e: [] for e in ENGINES}
        self.counts = {}

    def emit(self, eng, fn, wait=(), inc=None, inc_by=1, drain=False):
        self.ops[eng].append((fn, tuple(wait), inc, inc_by, drain))
        if inc is not None:
            self.counts[inc] = self.counts.get(inc, 0) + inc_by
            return self.counts[inc]
        return None

    def c(self, sem):
        return self.counts.get(sem, 0)

    def barrier(self):
        snap = tuple((s, v) for s, v in self.counts.items() if v > 0)
        for e in ENGINES:
            self.emit(e, None, wait=snap)


def build_nc(seq=4096, drains=False, dbg=False):
    SEQ = seq
    OWN = SEQ // 2
    NT = SEQ // 128
    NTO = OWN // 128
    D = 1024
    CC = 8
    NP = 8
    I = 4096
    NI = I // 128
    NISL = I // 512
    CHT = min(2, NTO)
    NCH = NTO // CHT
    NQT = max(OWN // 512, 1)
    QTW = min(512, OWN)
    GRP = 4 if NT % 4 == 0 else 1

    nc = bass.Bass("TRN2", target_bir_lowering=False)

    def din(name, shape, dt=F32):
        return nc.dram_tensor(name, shape, dt, kind="ExternalInput").ap()

    x_full = din("x_full", [SEQ, D])
    wqT = din("wqT", [D, D]); wkT = din("wkT", [D, D]); wvT = din("wvT", [D, D])
    bias_b = din("bias_b", [128, 3 * D])
    gate_w = din("gate_w", [I, D]); up_w = din("up_w", [I, D]); down_w = din("down_w", [D, I])
    s_in_b = din("s_in_b", [128, 1]); s_ind_b = din("s_ind_b", [128, 1])
    lepe_w = din("lepe_w", [128, NP * 3]); lepe_b = din("lepe_b", [128, NP])
    sin_t = din("sin_t", [128, NT * 32]); cos_t = din("cos_t", [128, NT * 32])
    masks = din("masks", [128, 2])
    ident_f = din("ident_f", [128, 128]); ident_b = din("ident_b", [128, 128], BF16)
    ones_in = din("ones_in", [128, 128])

    out_t = nc.dram_tensor("out_s", [OWN, D], F32, kind="ExternalOutput").ap()

    KIND = "ExternalOutput" if dbg else "Internal"
    wgq_d = nc.dram_tensor("wgq_d", [128, CC * I], BF16, kind=KIND).ap()
    wuq_d = nc.dram_tensor("wuq_d", [128, CC * I], BF16, kind=KIND).ap()
    wdq_d = nc.dram_tensor("wdq_d", [128, NI * D], BF16, kind=KIND).ap()
    xqT_d = nc.dram_tensor("xqT_d", [128, CC * OWN], BF16, kind=KIND).ap()
    xnT_d = nc.dram_tensor("xnT_d", [128, CC * SEQ], F32, kind="Internal").ap()
    s_d = nc.dram_tensor("s_d", [OWN, D], F32, kind=KIND).ap()
    g_d = nc.dram_tensor("g_d", [OWN, D], F32, kind=KIND).ap()
    if dbg:
        dbg_xnt = nc.dram_tensor("dbg_xnt", [128, CC * SEQ], F32, kind="ExternalOutput").ap()

    ctx = ExitStack()
    with ctx:
        def sbuf(name, shape, dt=F32):
            return ctx.enter_context(nc.sbuf_tensor(name, shape, dt))

        sinc = sbuf("sinc", [128, NT * 32]); cosc = sbuf("cosc", [128, NT * 32])
        lepw = sbuf("lepw", [128, NP * 3]); lepb = sbuf("lepb", [128, NP])
        maskc = sbuf("maskc", [128, 2])
        idf = sbuf("idf", [128, 128]); idb = sbuf("idb", [128, 128], BF16)
        onesc = sbuf("onesc", [128, 128])
        scal = sbuf("scal", [128, 32])
        expbuf = sbuf("expbuf", [128, 1024])
        biasg = sbuf("biasg", [128, 384])
        rcpr = sbuf("rcpr", [128, 512])

        rem = nc.sbuf_bytes_remaining
        acols = (rem // 4 // 128) * 128 - 128
        arena = sbuf("arena", [128, acols])
        arena_b = arena[:].bitcast(BF16)

        def psum(name, dt=F32, cols=512):
            return ctx.enter_context(nc.psum_tensor(name, [128, cols], dt))
        psS0 = psum("psS0"); psS1 = psum("psS1")
        psA0 = psum("psA0"); psA1 = psum("psA1")
        psP0 = psum("psP0"); psP1 = psum("psP1")
        psT = psum("psT")
        psB = psum("psB", BF16, 1024)

        sem_names = (["TE", "VE", "SE", "GE"] + [f"L{i}" for i in range(8)]
                     + [f"S{i}" for i in range(4)])
        sems = {s: ctx.enter_context(nc.semaphore(s)) for s in sem_names}

        S = Script()

        rr = {"ld": 0, "st": 0}

        def ld(dst_ap, src_ap, wait=()):
            sem = f"L{rr['ld'] % 8}"
            rr["ld"] += 1
            w = tuple(wait)
            prev = S.c(sem)
            if prev:
                w = w + ((sem, prev),)
            v = S.emit("sync", lambda e, d=dst_ap, s=src_ap: e.dma_start(out=d, in_=s),
                       wait=w, inc=sem, inc_by=16)
            return (sem, v)

        def ld_all():
            return tuple((f"L{i}", S.c(f"L{i}")) for i in range(8) if S.c(f"L{i}"))

        def stq(dst_ap, src_ap, wait=()):
            sem = f"S{rr['st'] % 4}"
            rr["st"] += 1
            w = tuple(wait)
            prev = S.c(sem)
            if prev:
                w = w + ((sem, prev),)
            v = S.emit("gpsimd", lambda e, d=dst_ap, s=src_ap: e.dma_start(out=d, in_=s),
                       wait=w, inc=sem, inc_by=16)
            return (sem, v)

        def st_all():
            return tuple((f"S{i}", S.c(f"S{i}")) for i in range(4) if S.c(f"S{i}"))

        dr_v = drains in (True, "all")
        dr_s = drains in (True, "all", "scalar")

        def vop(fn, wait=(), inc="VE", drain=None):
            return S.emit("vector", fn, wait=wait, inc=inc,
                          drain=dr_v if drain is None else drain)

        def sop(fn, wait=(), inc="SE", drain=None):
            return S.emit("scalar", fn, wait=wait, inc=inc,
                          drain=dr_s if drain is None else drain)

        def top(fn, wait=(), inc="TE"):
            return S.emit("tensor", fn, wait=wait, inc=inc)

        def gop(fn, wait=(), inc="GE", drain=None):
            return S.emit("gpsimd", fn, wait=wait, inc=inc,
                          drain=dr_v if drain is None else drain)

        class Cursor:
            def __init__(self, off=0):
                self.off = off
            def f32(self, n):
                assert self.off + n <= acols, f"arena overflow: {self.off}+{n} > {acols}"
                ap = arena[:, self.off:self.off + n]
                self.off += n
                return ap
            def bf16(self, n):
                assert n % 2 == 0
                assert self.off + n // 2 <= acols, "arena overflow (bf16)"
                ap = arena_b[:, 2 * self.off:2 * self.off + n]
                self.off += n // 2
                return ap

        ld(sinc[:], sin_t[:]); ld(cosc[:], cos_t[:])
        ld(lepw[:], lepe_w[:]); ld(lepb[:], lepe_b[:])
        ld(maskc[:], masks[:]); ld(idf[:], ident_f[:]); ld(idb[:], ident_b[:])
        ld(onesc[:], ones_in[:])
        ld(scal[:, 0:1], s_in_b[:]); ld(scal[:, 1:2], s_ind_b[:])
        c0all = ld_all()
        vop(lambda e: e.tensor_scalar(scal[:, 2:3], scal[:, 0:1], 1.0 / 6.0, None, ALU.mult),
            wait=c0all)
        vop(lambda e: e.tensor_scalar(scal[:, 3:4], scal[:, 0:1], 2.0, None, ALU.mult))
        vop(lambda e: e.tensor_scalar(scal[:, 4:5], scal[:, 1:2], 1.0 / 6.0, None, ALU.mult))
        vop(lambda e: e.tensor_scalar(scal[:, 5:6], scal[:, 1:2], 2.0, None, ALU.mult))
        vop(lambda e: e.memset(scal[:, 18:19], 1e-5))
        vop(lambda e: e.memset(scal[:, 19:20], EXP_BIAS))
        c_consts = ld_all()
        for _e in ("tensor", "scalar", "gpsimd", "vector"):
            S.emit(_e, None, wait=c_consts)

        def quant_block_chain(cur, am, gs6_ap, gs2_ap, nb):
            r2 = cur.f32(nb); s8h = cur.f32(nb)
            sc = cur.f32(nb); p = cur.f32(nb); pinv = cur.f32(nb)
            t1 = cur.f32(nb); t2 = cur.f32(nb); fl = cur.f32(nb)
            vop(lambda e: e.tensor_scalar(sc, am, gs6_ap, 448.0, ALU.mult, ALU.min))
            vop(lambda e: e.tensor_scalar(t1, sc, 4.0, None, ALU.is_ge))
            vop(lambda e: e.tensor_scalar(p, t1, 255.0, 1.0, ALU.mult, ALU.add))
            vop(lambda e: e.tensor_scalar(pinv, t1, (2.0 ** -8) - 1.0, 1.0, ALU.mult, ALU.add))
            for mul, thr in ((15.0, 0.25), (3.0, 2.0 ** -4), (1.0, 2.0 ** -5)):
                vop(lambda e, th=thr: e.tensor_scalar(t2, p, th, None, ALU.mult))
                vop(lambda e: e.tensor_tensor(t1, sc, t2, ALU.is_ge))
                vop(lambda e, m=mul: e.tensor_scalar(t2, t1, m, 1.0, ALU.mult, ALU.add))
                vop(lambda e: e.tensor_tensor(p, p, t2, ALU.mult))
                vop(lambda e, m=mul: e.tensor_scalar(t2, t1, 1.0 / (m + 1.0) - 1.0, 1.0,
                                                     ALU.mult, ALU.add))
                vop(lambda e: e.tensor_tensor(pinv, pinv, t2, ALU.mult))
            vop(lambda e: e.scalar_tensor_tensor(t1, sc, 512.0, pinv, ALU.mult, ALU.mult))
            vop(lambda e: e.tensor_scalar(t1, t1, 0.5, None, ALU.add))
            vop(lambda e: e.tensor_scalar(t2, t1, 16.0, 16.0, ALU.is_ge, ALU.mult))
            vop(lambda e: e.tensor_tensor(t1, t1, t2, ALU.subtract))
            vop(lambda e: e.tensor_copy(fl, t2))
            vop(lambda e: e.tensor_scalar(t2, t1, 8.0, 8.0, ALU.is_ge, ALU.mult))
            vop(lambda e: e.tensor_tensor(t1, t1, t2, ALU.subtract))
            vop(lambda e: e.tensor_tensor(fl, fl, t2, ALU.add))
            vop(lambda e: e.tensor_scalar(t2, t1, 4.0, 4.0, ALU.is_ge, ALU.mult))
            vop(lambda e: e.tensor_tensor(t1, t1, t2, ALU.subtract))
            vop(lambda e: e.tensor_tensor(fl, fl, t2, ALU.add))
            vop(lambda e: e.tensor_scalar(t2, t1, 2.0, 2.0, ALU.is_ge, ALU.mult))
            vop(lambda e: e.tensor_tensor(t1, t1, t2, ALU.subtract))
            vop(lambda e: e.tensor_tensor(fl, fl, t2, ALU.add))
            vop(lambda e: e.scalar_tensor_tensor(fl, t1, 1.0, fl, ALU.is_ge, ALU.add))
            vop(lambda e: e.scalar_tensor_tensor(sc, fl, 2.0 ** -9, p, ALU.mult, ALU.mult))
            vop(lambda e: e.tensor_scalar(s8h, sc, 0.5, None, ALU.mult))
            vop(lambda e: e.tensor_single_scalar(t1, sc, 0.0, ALU.is_equal))
            vop(lambda e: e.tensor_tensor(t2, sc, t1, ALU.max))
            vop(lambda e: e.reciprocal(t2, t2))
            vop(lambda e: e.tensor_scalar(t2, t2, gs2_ap, None, ALU.mult))
            vop(lambda e: e.tensor_scalar(fl, t1, -1.0, 1.0, ALU.mult, ALU.add))
            vop(lambda e: e.tensor_tensor(t2, t2, fl, ALU.mult))
            c = vop(lambda e: e.scalar_tensor_tensor(r2, t1, 2.0, t2, ALU.mult, ALU.add))
            return r2, s8h, c

        def quant_elem_chain(cur, w_ap, sgn_ap, r2, s8h, out_ap, ncols, nb):
            u = cur.f32(ncols); A = cur.f32(ncols); Bv = cur.f32(ncols)
            t = cur.f32(ncols); h = cur.f32(ncols); c = cur.f32(ncols)
            r2b = r2.unsqueeze(2).broadcast_to([128, nb, 16])
            s8hb = s8h.unsqueeze(2).broadcast_to([128, nb, 16])
            u3 = u.rearrange("p (g b) -> p g b", b=16)
            vop(lambda e: e.scalar_tensor_tensor(u, w_ap, -1.0, w_ap, ALU.mult, ALU.max))
            vop(lambda e: e.tensor_tensor(u3, u3, r2b, ALU.mult))
            vop(lambda e: e.tensor_scalar(u, u, 12.0, None, ALU.min))
            vop(lambda e: e.tensor_scalar(A, u, 4.0, 0.5, ALU.is_ge, ALU.mult))
            vop(lambda e: e.tensor_scalar(Bv, u, 8.0, 0.25, ALU.is_ge, ALU.mult))
            vop(lambda e: e.tensor_tensor(t, A, Bv, ALU.add))
            vop(lambda e: e.scalar_tensor_tensor(h, Bv, 3.0, t, ALU.mult, ALU.add))
            vop(lambda e: e.tensor_scalar(t, t, -1.0, 1.0, ALU.mult, ALU.add))
            vop(lambda e: e.tensor_scalar(A, h, 2.0, 1.0, ALU.mult, ALU.add))
            vop(lambda e: e.scalar_tensor_tensor(h, u, 0.5, h, ALU.add, ALU.add))
            vop(lambda e: e.tensor_tensor(h, h, t, ALU.mult))
            vop(lambda e: e.tensor_scalar(c, h, 1.0, None, ALU.is_ge))
            vop(lambda e: e.scalar_tensor_tensor(c, h, 2.0, c, ALU.is_ge, ALU.add))
            vop(lambda e: e.scalar_tensor_tensor(c, h, 3.0, c, ALU.is_ge, ALU.add))
            vop(lambda e: e.scalar_tensor_tensor(c, h, 4.0, c, ALU.is_ge, ALU.add))
            vop(lambda e: e.tensor_tensor(c, c, A, ALU.mult))
            vop(lambda e: e.tensor_tensor(c, c, sgn_ap, ALU.mult))
            o3 = out_ap.rearrange("p (g b) -> p g b", b=16)
            c3 = c.rearrange("p (g b) -> p g b", b=16)
            return vop(lambda e: e.tensor_tensor(o3, c3, s8hb, ALU.mult))

        def quant_tile(chain_base, blk_ap, src_ap, sgn_ap, out_ap, C, gs6_ap, gs2_ap):
            nb = C // 16
            vop(lambda e: e.reduce_max(
                blk_ap[:, 0:nb], src_ap[:, 0:C].rearrange("p (g k) -> p g k", k=16),
                axis=AX.X, apply_absolute_value=True))
            curq = Cursor(chain_base)
            r2, s8h, _ = quant_block_chain(curq, blk_ap[:, 0:nb], gs6_ap, gs2_ap, nb)
            elem_base = chain_base + 2 * nb
            cq = None
            for sub in range(C // 1024):
                sl = slice(sub * 1024, (sub + 1) * 1024)
                bsl = slice(sub * 64, (sub + 1) * 64)
                cq = quant_elem_chain(Cursor(elem_base),
                                      src_ap[:, sl], sgn_ap[:, sl],
                                      r2[:, bsl], s8h[:, bsl],
                                      out_ap[:, sl], 1024, 64)
            return cq

        def cross_partition_max(src_col, dst_col, tmp_row):
            ct = top(lambda e: e.transpose(psT[0:1, 0:128], src_col, idf[:]),
                     wait=(("VE", S.c("VE")), ("SE", S.c("SE"))))
            cv = vop(lambda e: e.tensor_copy(tmp_row[0:1, 0:128], psT[0:1, 0:128]),
                     wait=(("TE", ct),))
            cv = vop(lambda e: e.reduce_max(tmp_row[0:1, 0:1], tmp_row[0:1, 0:128], axis=AX.X))
            ct = top(lambda e: e.matmul(psT[:, 0:1], onesc[0:1, :], tmp_row[0:1, 0:1],
                                        start=True, stop=True), wait=(("VE", cv),))
            return vop(lambda e: e.tensor_copy(dst_col, psT[:, 0:1]), wait=(("TE", ct),))

        P0_NEED = 4096 + 4096 + 2048 + 2048 + 512 + 2 * 256 + 6 * 1024
        p0_cur = Cursor(acols - P0_NEED)
        wt_a = p0_cur.f32(4096)
        sgn0 = p0_cur.f32(4096)
        wq_a = p0_cur.bf16(4096)
        stage0 = p0_cur.bf16(4096)
        blk0 = p0_cur.f32(512)
        chain_base0 = p0_cur.off

        wcfg = [
            (gate_w, wgq_d, I, D, 6, 9, 15, 12, 0),
            (up_w, wuq_d, I, D, 7, 10, 16, 13, 0),
            (down_w, wdq_d, D, I, 8, 11, 17, 14, 1),
        ]
        p0_state = {"last_sp": None, "last_q": None, "last_ct": None,
                    "pending": None, "widx": 0}

        def emit_p0_passA():
            for (src_, dst, rows, C, sl6, sl2, slm, slr, sli) in wcfg:
                nrt = rows // 128
                gmax = scal[:, slm:slm + 1]
                last_v = S.c("VE")
                for ot in range(nrt):
                    cl = ld(wt_a[:, 0:C], src_[ot * 128:(ot + 1) * 128, :],
                            wait=(("VE", last_v),))
                    if ot == 0:
                        last_v = vop(lambda e, cc=C, g_=gmax: e.reduce_max(
                            g_, wt_a[:, 0:cc], axis=AX.X, apply_absolute_value=True),
                            wait=(cl,))
                    else:
                        vop(lambda e, cc=C: e.reduce_max(
                            blk0[:, 0:1], wt_a[:, 0:cc], axis=AX.X,
                            apply_absolute_value=True), wait=(cl,))
                        last_v = vop(lambda e, g_=gmax: e.tensor_tensor(
                            g_, g_, blk0[:, 0:1], ALU.max))
                cross_partition_max(gmax, gmax, rcpr)
                vop(lambda e, g_=gmax: e.reciprocal(blk0[:, 0:1], g_))
                vop(lambda e, a=sl6: e.tensor_scalar(scal[:, a:a + 1], blk0[:, 0:1],
                                                     GSC / 6.0, None, ALU.mult))
                vop(lambda e, a=sl2: e.tensor_scalar(scal[:, a:a + 1], blk0[:, 0:1],
                                                     GSC * 2.0, None, ALU.mult))
                vop(lambda e, s=sli: e.reciprocal(blk0[:, 1:2], scal[:, s:s + 1]))
                vop(lambda e, r=slr, g_=gmax: e.scalar_tensor_tensor(
                    scal[:, r:r + 1], g_, 1.0 / GSC, blk0[:, 1:2], ALU.mult, ALU.mult))

        p0_work = []
        for wi, (src_, dst, rows, C, sl6, sl2, slm, slr, sli) in enumerate(wcfg):
            for ot in range(rows // 128):
                p0_work.append((src_, dst, C, sl6, sl2, ot))

        def emit_p0_quant(src_, dst, C, sl6, sl2, ot):
            last_sp = p0_state["last_sp"]
            wsp0 = (last_sp,) if last_sp else ()
            wq = (("VE", p0_state["last_q"]),) if p0_state["last_q"] else (("VE", S.c("VE")),)
            cl = ld(wt_a[:, 0:C], src_[ot * 128:(ot + 1) * 128, :], wait=wq + wsp0)
            csg = sop(lambda e, cc=C: e.sign(sgn0[:, 0:cc], wt_a[:, 0:cc]), wait=(cl,))
            wtr = (("TE", p0_state["last_ct"]),) if p0_state["last_ct"] else ()
            S.emit("vector", None, wait=(("SE", csg), cl) + wtr)
            cq = quant_tile(chain_base0, blk0, wt_a, sgn0, wq_a, C,
                            scal[:, sl6:sl6 + 1], scal[:, sl2:sl2 + 1])
            p0_state["last_q"] = cq
            S.emit("scalar", None, wait=(("VE", cq),))
            p0_state["pending"] = (dst, C, ot, cq)

        def emit_p0_flush():
            if p0_state["pending"] is None:
                return
            dst, C, ot, cq = p0_state["pending"]
            p0_state["pending"] = None
            last_sp = p0_state["last_sp"]
            wsp0 = (last_sp,) if last_sp else ()
            ncc = C // 128
            ct = None
            for q in range(ncc):
                hh = q % 2
                ct = top(lambda e, qq=q, h_=hh: e.transpose(
                    psB[:, h_ * 128:(h_ + 1) * 128],
                    wq_a[:, qq * 128:(qq + 1) * 128], idb[:]),
                    wait=(("VE", cq), ("SE", S.c("SE"))) + wsp0)
                sop(lambda e, qq=q, h_=hh: e.copy(
                    stage0[:, qq * 128:(qq + 1) * 128],
                    psB[:, h_ * 128:(h_ + 1) * 128]), wait=(("TE", ct),))
            p0_state["last_ct"] = ct
            if C == 1024:
                dv4 = dst.rearrange("p (il q r) -> p il q r", il=NISL, q=ncc, r=512)
                dview = dv4[:, ot // 4, :, (ot % 4) * 128:(ot % 4 + 1) * 128]
            else:
                dview = dst.rearrange("p (db ic r) -> p db ic r",
                                      db=8, ic=NI, r=128)[:, ot, :, :]
            sview = stage0[:, 0:ncc * 128].rearrange("p (q r) -> p q r", q=ncc)
            p0_state["last_sp"] = stq(dview, sview, wait=(("SE", S.c("SE")),))

        def weave_slot():
            emit_p0_flush()
            if p0_state["widx"] < len(p0_work):
                emit_p0_quant(*p0_work[p0_state["widx"]])
                p0_state["widx"] += 1

        S.barrier()

        cur1 = Cursor()
        p3_base = 0
        xnstage = cur1.f32(1024)
        xt_a = cur1.f32(1024); xt_b = cur1.f32(1024)
        xn_t = cur1.f32(1024); sq_t = cur1.f32(1024)
        mv = cur1.f32(4)
        xq_t = cur1.bf16(1024)
        xqstage = cur1.bf16(1024)
        sgn1 = cur1.f32(1024)
        blk1 = cur1.f32(64)
        chain1 = cur1.off

        xnT3d = xnT_d.rearrange("p (q s) -> p q s", q=CC)
        xnst3 = xnstage.rearrange("p (q s) -> p q s", q=CC)
        prev_done = S.c("VE")
        prev_sp = None
        prev_sp_xn = None
        xn_guard = ()
        for tt in range(NT):
            buf = xt_a if tt % 2 == 0 else xt_b
            cl = ld(buf, x_full[tt * 128:(tt + 1) * 128, :],
                    wait=(("VE", prev_done),) + ((prev_sp,) if prev_sp else ()))
            vop(lambda e, b=buf: e.reduce_sum(mv[:, 0:1], b, axis=AX.X), wait=(cl,))
            vop(lambda e: e.tensor_scalar(mv[:, 0:1], mv[:, 0:1], 1.0 / D, None, ALU.mult))
            cve = vop(lambda e, b=buf: e.tensor_scalar(xn_t, b, mv[:, 0:1], None, ALU.subtract),
                       wait=xn_guard)
            sop(lambda e: e.activation(sq_t, xn_t, ACT.Square, bias=0.0, scale=1.0,
                                       accum_out=mv[:, 2:3]), wait=(("VE", cve),))
            cs2 = sop(lambda e: e.activation(mv[:, 3:4], mv[:, 2:3], ACT.Sqrt,
                                             bias=scal[:, 18:19], scale=1.0 / D))
            vop(lambda e: e.reciprocal(mv[:, 1:2], mv[:, 3:4]), wait=(("SE", cs2),))
            cxn = vop(lambda e: e.tensor_scalar(xn_t, xn_t, mv[:, 1:2], None, ALU.mult))
            for q in range(CC):
                wtr = [("VE", cxn), ("SE", S.c("SE"))]
                if q == 0 and prev_sp_xn:
                    wtr.append(prev_sp_xn)
                ct = top(lambda e, qq=q: e.transpose(
                    psT[:, 0:128], xn_t[:, qq * 128:(qq + 1) * 128], idf[:]),
                    wait=tuple(wtr))
                sop(lambda e, qq=q: e.copy(xnst3[:, qq, :], psT[:, 0:128]),
                    wait=(("TE", ct),))
            prev_sp_xn = stq(xnT3d[:, :, tt * 128:(tt + 1) * 128], xnst3,
                             wait=(("SE", S.c("SE")),))
            prev_done = cxn
            if tt < NTO:
                csg = sop(lambda e: e.sign(sgn1, xn_t), wait=(("VE", cxn),))
                S.emit("vector", None, wait=(("SE", csg),))
                cq = quant_tile(chain1, blk1, xn_t, sgn1, xq_t, 1024,
                                scal[:, 2:3], scal[:, 3:4])
                S.emit("scalar", None, wait=(("VE", cq),))
                for q in range(CC):
                    hh = q % 2
                    ct = top(lambda e, qq=q, h_=hh: e.transpose(
                        psB[:, h_ * 128:(h_ + 1) * 128],
                        xq_t[:, qq * 128:(qq + 1) * 128], idb[:]),
                        wait=(("VE", cq), ("SE", S.c("SE"))) + ((prev_sp,) if prev_sp else ()))
                    sop(lambda e, qq=q, h_=hh: e.copy(
                        xqstage[:, qq * 128:(qq + 1) * 128],
                        psB[:, h_ * 128:(h_ + 1) * 128]), wait=(("TE", ct),))
                dv = xqT_d.rearrange("p (q s) -> p q s", q=CC)[:, :, tt * 128:(tt + 1) * 128]
                sv = xqstage.rearrange("p (q s) -> p q s", q=CC)
                prev_sp = stq(dv, sv, wait=(("SE", S.c("SE")),))
                prev_done = cq
            xn_guard = (("TE", S.c("TE")), ("SE", S.c("SE")))

        S.barrier()
        if dbg:
            stq(dbg_xnt[:], xnT)
            S.barrier()

        cur3 = Cursor(p3_base)
        xnb_a = cur3.f32(CC * GRP * 128); xnb_b = cur3.f32(CC * GRP * 128)
        KS1 = cur3.bf16(SEQ); KS2 = cur3.bf16(SEQ)
        QS1 = cur3.bf16(OWN); QS2 = cur3.bf16(OWN)
        QW1 = cur3.bf16(OWN); QW2 = cur3.bf16(OWN)
        VH1 = cur3.bf16(NT * 65); VL1 = cur3.bf16(NT * 65)
        VH2 = cur3.bf16(NT * 65); VL2 = cur3.bf16(NT * 65)
        phr = cur3.f32(1024)
        PhiB = phr[:, 0:512].bitcast(BF16)
        PloB = phr[:, 512:1024].bitcast(BF16)
        vtf = cur3.f32(OWN + 2)
        kvg_off = cur3.off
        kvg = cur3.f32(GRP * 256)
        ket = cur3.f32(GRP * 128)
        rtw = cur3.f32(GRP * 128)
        rt1 = rtw[:, 0:GRP * 64]; rt2 = rtw[:, GRP * 64:GRP * 128]
        sstg = rcpr[:, 0:128]
        assert cur3.off <= acols - P0_NEED, f"P3 region {cur3.off} overlaps P0 at {acols - P0_NEED}"
        assert GRP * 256 + GRP * 128 + 2 * GRP * 64 >= OWN, "sfm alias too small"
        sfm = arena[:, kvg_off:kvg_off + OWN]

        def kv_slot(q):
            if q < 4:
                return expbuf[:, q * 256:(q + 1) * 256]
            if q < 6:
                return rcpr[:, (q - 4) * 256:(q - 3) * 256]
            return phr[:, (q - 6) * 256:(q - 5) * 256]

        def wq_slot(q):
            return expbuf[:, q * 128:(q + 1) * 128]

        def proc_group(tiles, is_q):
            n = len(tiles)
            if is_q:
                src = kvg.rearrange("p (t w) -> p t w", t=GRP)[:, 0:n, 0:128]
                bslc = biasg[:, 0:128]
            else:
                src = kvg.rearrange("p (t w) -> p t w", t=GRP)[:, 0:n, 0:128]
                bslc = biasg[:, 128:256]
            ketv = ket.rearrange("p (t w) -> p t w", t=GRP)[:, 0:n, :]
            bb = bslc.unsqueeze(1).broadcast_to([128, n, 128])
            vop(lambda e: e.tensor_tensor(ketv, src, bb, ALU.add))
            kkv = kvg.rearrange("p (t w) -> p t w", t=GRP)[:, 0:n, 0:128]
            rtv = rtw[:, 0:n * 128]
            ketf = ket[:, 0:n * 128]
            ket3 = ket.rearrange("p (t w) -> p t w", t=GRP)[:, 0:n, :]
            cv = vop(lambda e: e.tensor_scalar(kkv, ket3, 0.0, None, ALU.min))
            cs = sop(lambda e: e.activation(rtv, kkv, ACT.Exp, bias=0.0, scale=1.0),
                     wait=(("VE", cv),))
            vop(lambda e: e.tensor_scalar(kkv, ket3, 0.0, None, ALU.max),
                wait=(("SE", cs),))
            rtv3 = rtv.rearrange("p (t w) -> p t w", t=n)
            vop(lambda e: e.tensor_tensor(ket3, kkv, rtv3, ALU.add))
            t0 = tiles[0]
            ket5 = ket.rearrange("p (t h j u) -> p t h j u", h=2, j=32, u=2)
            x1 = ket5[:, 0:n, :, :, 0]
            x2 = ket5[:, 0:n, :, :, 1]
            sv = sinc[:, t0 * 32:(t0 + n) * 32].rearrange("p (t j) -> p t j", t=n)
            svb = sv.unsqueeze(2).broadcast_to([128, n, 2, 32])
            cvv = cosc[:, t0 * 32:(t0 + n) * 32].rearrange("p (t j) -> p t j", t=n)
            cvb = cvv.unsqueeze(2).broadcast_to([128, n, 2, 32])
            r1v3 = rt1.rearrange("p (t h j) -> p t h j", t=GRP, h=2)[:, 0:n]
            r2v3 = rt2.rearrange("p (t h j) -> p t h j", t=GRP, h=2)[:, 0:n]
            dd = kvg.rearrange("p (t q) -> p t q", t=GRP)[:, 0:n, 0:128]
            de = dd.rearrange("p t (h u j) -> p t h u j", h=2, u=2, j=32)[:, :, :, 0, :]
            do = dd.rearrange("p t (h u j) -> p t h u j", h=2, u=2, j=32)[:, :, :, 1, :]
            vop(lambda e: e.tensor_tensor(r1v3, x1, cvb, ALU.mult))
            vop(lambda e: e.tensor_tensor(r2v3, x2, svb, ALU.mult))
            vop(lambda e: e.tensor_tensor(de, r1v3, r2v3, ALU.subtract))
            vop(lambda e: e.tensor_tensor(r1v3, x1, svb, ALU.mult))
            vop(lambda e: e.tensor_tensor(r2v3, x2, cvb, ALU.mult))
            cfin = vop(lambda e: e.tensor_tensor(do, r1v3, r2v3, ALU.add))
            stg = (rt1.bitcast(BF16).rearrange("p (t w) -> p t w", t=GRP)[:, 0:n, 0:128],
                   rt2.bitcast(BF16).rearrange("p (t w) -> p t w", t=GRP)[:, 0:n, 0:128])
            kr4 = kvg.rearrange("p (t w) -> p t w", t=GRP)[:, 0:n, 0:128]

            def tr_into(sg, dstT, tiles, dep):
                for i, tt in enumerate(tiles):
                    hh = i % 2
                    ct = top(lambda e, ii=i, s_=sg, h_=hh: e.transpose(
                        psB[:, h_ * 128:(h_ + 1) * 128], s_[:, ii, :], idb[:]),
                        wait=(("VE", dep), ("SE", S.c("SE"))))
                    sop(lambda e, t=tt, d=dstT, h_=hh: e.copy(
                        d[:, t * 128:(t + 1) * 128],
                        psB[:, h_ * 128:(h_ + 1) * 128]), wait=(("TE", ct),))

            for h in range(2):
                sg = stg[0]
                hs = slice(h * 64, (h + 1) * 64)
                vop(lambda e, s_=sg, k_=hs: e.tensor_copy(s_[:, :, 0:64], kr4[:, :, k_]),
                    wait=(("TE", S.c("TE")),))
                cfin2 = vop(lambda e, s_=sg, k_=hs: e.tensor_tensor(
                    s_[:, :, 64:128], kr4[:, :, k_], s_[:, :, 0:64], ALU.subtract))
                dstT = (QS1 if h == 0 else QS2) if is_q else (KS1 if h == 0 else KS2)
                tr_into(sg, dstT, tiles, cfin2)
                if is_q:
                    sw = stg[1]
                    vop(lambda e, a=sw, b=sg: e.tensor_copy(a[:, :, 0:64], b[:, :, 64:128]),
                        wait=(("TE", S.c("TE")),))
                    cfin3 = vop(lambda e, a=sw, b=sg: e.tensor_copy(a[:, :, 64:128],
                                                                    b[:, :, 0:64]))
                    tr_into(sw, QW1 if h == 0 else QW2, tiles, cfin3)
            return S.c("SE")

        emit_p0_passA()
        p0_per_pair = (len(p0_work) + NP - 1) // NP
        for g in range(NP):
            pb_te = S.c("TE"); pb_ve = S.c("VE"); pb_se = S.c("SE")
            for wi, wsrc in enumerate((wkT, wvT)):
                for q in range(CC):
                    ld(kv_slot(q)[:, wi * 128:(wi + 1) * 128],
                       wsrc[q * 128:(q + 1) * 128, g * 128:(g + 1) * 128],
                       wait=(("TE", pb_te), ("VE", pb_ve), ("SE", pb_se)) + st_all())
            ld(biasg[:, 0:128], bias_b[:, g * 128:(g + 1) * 128])
            ld(biasg[:, 128:256], bias_b[:, D + g * 128:D + (g + 1) * 128])
            cbl = ld(biasg[:, 256:384], bias_b[:, 2 * D + g * 128:2 * D + (g + 1) * 128])
            wld = ld_all()
            cv1 = vop(lambda e: e.memset(VH1, 1.0), wait=(("TE", pb_te),))
            vop(lambda e: e.memset(VH2, 1.0))
            vop(lambda e: e.memset(VL1, 0.0))
            cv2 = vop(lambda e: e.memset(VL2, 0.0))

            pbanks = (psP0, psP1, psS0, psS1)
            pb_evac = {0: ("VE", pb_ve), 1: ("VE", pb_ve),
                       2: ("VE", pb_ve), 3: ("VE", pb_ve)}
            xnb_use = {0: 0, 1: 0}
            for gi, g0 in enumerate(range(0, NT, GRP)):
                gts = list(range(g0, min(g0 + GRP, NT)))
                xnb = xnb_a if gi % 2 == 0 else xnb_b
                xnb3 = xnb.rearrange("p (q s) -> p q s", q=CC)
                cxl = ld(xnb3[:, :, 0:len(gts) * 128],
                         xnT3d[:, :, g0 * 128:(g0 + len(gts)) * 128],
                         wait=(("TE", xnb_use[gi % 2]),) + ((prev_sp_xn,) if gi == 0 else ()))
                for i, tt in enumerate(gts):
                    bi = i % 4
                    pp = pbanks[bi]
                    wt = list(wld) + [pb_evac[bi], cxl]
                    ctev = None
                    for q in range(CC):
                        ctev = top(lambda e, qq=q, ii=i, p_=pp, x_=xnb3: e.matmul(
                            p_[:, 0:256], x_[:, qq, ii * 128:(ii + 1) * 128],
                            kv_slot(qq), start=(qq == 0), stop=(qq == CC - 1)),
                            wait=tuple(wt) if q == 0 else (),
                            inc="TE" if q == CC - 1 else None)
                    pb_evac[bi] = ("VE", vop(lambda e, ii=i, p_=pp: e.tensor_copy(
                        kvg[:, ii * 256:(ii + 1) * 256], p_[:, 0:256]),
                        wait=(("TE", S.c("TE")),)))
                xnb_use[gi % 2] = S.c("TE")
                ng = len(gts)
                kv4 = kvg.rearrange("p (t w) -> p t w", t=GRP)[:, 0:ng, :]
                proc_group(gts, is_q=False)
                ketv = ket.rearrange("p (t w) -> p t w", t=GRP)[:, 0:ng, :]
                bvp = biasg[:, 256:384].unsqueeze(1).broadcast_to([128, ng, 128])
                vop(lambda e: e.tensor_tensor(ketv, kv4[:, :, 128:256], bvp, ALU.add),
                    wait=(("TE", S.c("TE")), ("SE", S.c("SE"))))
                for h, (VH, VL) in enumerate(((VH1, VL1), (VH2, VL2))):
                    hd = slice(h * 64, (h + 1) * 64)
                    vhd = VH.rearrange("p (t w) -> p t w", t=NT)[:, g0:g0 + ng, 0:64]
                    vld = VL.rearrange("p (t w) -> p t w", t=NT)[:, g0:g0 + ng, 0:64]
                    vop(lambda e, d_=vhd, s_=hd: e.tensor_copy(d_, ketv[:, :, s_]))
                    vop(lambda e, d_=vld, s_=hd, h_=vhd: e.tensor_tensor(
                        d_, ketv[:, :, s_], h_, ALU.subtract))
                for i, tt in enumerate(gts):
                    if not (tt <= NTO or tt == NT - 1):
                        continue
                    ctv = top(lambda e, ii=i: e.transpose(
                        psT[:, 0:128], ketv[:, ii, :], idf[:]),
                        wait=(("VE", S.c("VE")), ("SE", S.c("SE"))))
                    if tt < NTO:
                        sop(lambda e, t=tt: e.copy(
                            vtf[:, 1 + t * 128:1 + (t + 1) * 128], psT[:, 0:128]),
                            wait=(("TE", ctv),))
                    elif tt == NTO:
                        sop(lambda e: e.copy(vtf[:, OWN + 1:OWN + 2], psT[:, 0:1]),
                            wait=(("TE", ctv),))
                    if tt == NT - 1:
                        sop(lambda e: e.copy(vtf[:, 0:1], psT[:, 127:128]),
                            wait=(("TE", ctv),))
            kse = S.c("SE")

            for q in range(CC):
                ld(wq_slot(q), wqT[q * 128:(q + 1) * 128, g * 128:(g + 1) * 128],
                   wait=(("TE", S.c("TE")), ("SE", S.c("SE"))))
            wldq = ld_all()
            for gi, g0 in enumerate(range(0, NTO, GRP)):
                gts = list(range(g0, min(g0 + GRP, NTO)))
                xnb = xnb_a if gi % 2 == 0 else xnb_b
                xnb3 = xnb.rearrange("p (q s) -> p q s", q=CC)
                cxl = ld(xnb3[:, :, 0:len(gts) * 128],
                         xnT3d[:, :, g0 * 128:(g0 + len(gts)) * 128],
                         wait=(("TE", xnb_use[gi % 2]),))
                for i, tt in enumerate(gts):
                    bi = i % 4
                    pp = pbanks[bi]
                    wt = list(wldq) + [pb_evac[bi], cxl]
                    ctev = None
                    for q in range(CC):
                        ctev = top(lambda e, qq=q, ii=i, p_=pp, x_=xnb3: e.matmul(
                            p_[:, 0:128], x_[:, qq, ii * 128:(ii + 1) * 128],
                            wq_slot(qq), start=(qq == 0), stop=(qq == CC - 1)),
                            wait=tuple(wt) if q == 0 else (),
                            inc="TE" if q == CC - 1 else None)
                    pb_evac[bi] = ("VE", vop(lambda e, ii=i, p_=pp: e.tensor_copy(
                        kvg[:, ii * 256:ii * 256 + 128], p_[:, 0:128]),
                        wait=(("TE", S.c("TE")),)))
                xnb_use[gi % 2] = S.c("TE")
                proc_group(gts, is_q=True)

            chalo = S.c("SE")
            vop(lambda e: e.tensor_scalar(vtf[:, 0:1], vtf[:, 0:1], maskc[:, 0:1],
                                          None, ALU.mult), wait=(("SE", chalo),))
            vop(lambda e: e.tensor_scalar(vtf[:, OWN + 1:OWN + 2], vtf[:, OWN + 1:OWN + 2],
                                          maskc[:, 1:2], None, ALU.mult))

            qkT_done = S.c("SE")
            prev_exp = [qkT_done, qkT_done]
            prev_av = [S.c("TE"), S.c("TE")]
            prev_plo = [S.c("GE"), S.c("GE")]
            psA_free = {0: S.c("VE"), 1: S.c("VE")}
            for h in range(2):
                psA = psA0 if h == 0 else psA1
                VH = VH1 if h == 0 else VH2
                VL = VL1 if h == 0 else VL2
                for qt in range(NQT):
                    exp_c = {}
                    av_c = {}
                    plo_c = {}
                    hr = slice(h * 64, (h + 1) * 64)

                    KS = KS1 if h == 0 else KS2
                    QS = QS1 if h == 0 else QS2
                    QW = QW1 if h == 0 else QW2

                    def emit_s(kb):
                        pss = psS0 if kb % 2 == 0 else psS1
                        w = [("SE", exp_c[kb - 2] if kb >= 2 else prev_exp[kb % 2])]
                        top(lambda e, k=kb, p_=pss, q_=qt, K_=KS, Q_=QS: e.matmul(
                            p_[:, 0:QTW], K_[:, k * 128:(k + 1) * 128],
                            Q_[:, q_ * QTW:(q_ + 1) * QTW],
                            start=True, stop=False), wait=tuple(w), inc=None)
                        cs_mm = top(lambda e, k=kb, p_=pss, q_=qt, K_=KS, Q_=QW: e.matmul(
                            p_[:, 0:QTW], K_[:, k * 128:(k + 1) * 128],
                            Q_[:, q_ * QTW:(q_ + 1) * QTW],
                            start=False, stop=True))
                        sl = slice((kb % 2) * 512, (kb % 2) * 512 + QTW)
                        w2 = [("TE", cs_mm),
                              ("GE", plo_c[kb - 2] if kb >= 2 else prev_plo[kb % 2])]
                        sop(lambda e, p_=pss, s_=sl: e.activation(
                            expbuf[:, s_], p_[:, 0:QTW],
                            ACT.Exp, bias=scal[:, 19:20], scale=0.125), wait=tuple(w2),
                            drain=False)
                        w2b = [("TE", av_c[kb - 2] if kb >= 2 else prev_av[kb % 2])]
                        exp_c[kb] = sop(lambda e, p_=pss, s_=sl: e.activation(
                            PhiB[:, s_], p_[:, 0:QTW],
                            ACT.Exp, bias=scal[:, 19:20], scale=0.125), wait=tuple(w2b),
                            drain=False)
                        plo_c[kb] = gop(lambda e, s_=sl: e.tensor_tensor(
                            PloB[:, s_], expbuf[:, s_], PhiB[:, s_], ALU.subtract),
                            wait=(("SE", exp_c[kb]),), drain=False)

                    def emit_av(kb):
                        sl = slice((kb % 2) * 512, (kb % 2) * 512 + QTW)
                        wav = [("SE", exp_c[kb])]
                        if kb == 0:
                            wav.append(("VE", psA_free[h]))
                        top(lambda e, k=kb, pA=psA, V_=VH, s_=sl: e.matmul(
                            pA[0:65, 0:QTW], V_[:, k * 65:(k + 1) * 65],
                            PhiB[:, s_], start=(k == 0), stop=False),
                            wait=tuple(wav), inc=None)
                        top(lambda e, k=kb, pA=psA, V_=VL, s_=sl: e.matmul(
                            pA[0:65, 0:QTW], V_[:, k * 65:(k + 1) * 65],
                            PhiB[:, s_], start=False, stop=False), inc=None)
                        av_c[kb] = top(lambda e, k=kb, pA=psA, V_=VH, s_=sl: e.matmul(
                            pA[0:65, 0:QTW], V_[:, k * 65:(k + 1) * 65],
                            PloB[:, s_], start=False, stop=(k == NT - 1)),
                            wait=(("GE", plo_c[kb]),))

                    emit_s(0)
                    for kb in range(1, NT):
                        emit_s(kb)
                        emit_av(kb - 1)
                    emit_av(NT - 1)
                    prev_exp = [exp_c[NT - 2], exp_c[NT - 1]]
                    prev_av = [av_c[NT - 2], av_c[NT - 1]]
                    prev_plo = [plo_c[NT - 2], plo_c[NT - 1]]
                    crc = vop(lambda e, pA=psA: e.reciprocal(rcpr[0:1, 0:QTW], pA[64:65, 0:QTW]),
                              wait=(("TE", av_c[NT - 1]), ("VE", S.c("VE"))))
                    crx = top(lambda e: e.matmul(psT[0:64, 0:QTW], onesc[0:1, 0:64],
                                                 rcpr[0:1, 0:QTW], start=True, stop=True),
                              wait=(("VE", crc),))
                    sf = sfm[h * 64:(h + 1) * 64, qt * QTW:(qt + 1) * QTW]
                    cev = vop(lambda e, s=sf, pA=psA: e.tensor_copy(s, pA[0:64, 0:QTW]))
                    psA_free[h] = vop(
                        lambda e, s=sf: e.tensor_tensor(s, s, psT[0:64, 0:QTW], ALU.mult),
                        wait=(("TE", crx),))
                    s_idx = h * NQT + qt
                    wtarget = ((g * 2 * NQT + s_idx + 1) * len(p0_work)) // (NP * 2 * NQT)
                    while p0_state["widx"] < wtarget:
                        weave_slot()

            wl0 = lepw[:, g * 3:g * 3 + 1]; wl1 = lepw[:, g * 3 + 1:g * 3 + 2]
            wl2 = lepw[:, g * 3 + 2:g * 3 + 3]; bl = lepb[:, g:g + 1]
            sf_all = sfm[:, 0:OWN]
            vop(lambda e, w_=wl0: e.scalar_tensor_tensor(sf_all, vtf[:, 0:OWN], w_, sf_all,
                                                         ALU.mult, ALU.add),
                wait=(("TE", S.c("TE")),))
            vop(lambda e, w_=wl1: e.scalar_tensor_tensor(sf_all, vtf[:, 1:OWN + 1], w_, sf_all,
                                                         ALU.mult, ALU.add))
            vop(lambda e, w_=wl2: e.scalar_tensor_tensor(sf_all, vtf[:, 2:OWN + 2], w_, sf_all,
                                                         ALU.mult, ALU.add))
            cfin = vop(lambda e, b_=bl: e.tensor_scalar(sf_all, sf_all, b_, None, ALU.add))
            for tt in range(NTO):
                ct = top(lambda e, t=tt: e.transpose(
                    psT[:, 0:128], sfm[:, t * 128:(t + 1) * 128], idf[:]),
                    wait=(("VE", cfin), ("SE", S.c("SE"))))
                cse = sop(lambda e: e.copy(sstg[:, 0:128], psT[:, 0:128]), wait=(("TE", ct),))
                stq(s_d[tt * 128:(tt + 1) * 128, g * 128:(g + 1) * 128], sstg[:, 0:128],
                    wait=(("SE", cse),))
                S.emit("scalar", None, wait=st_all())


        while p0_state["widx"] < len(p0_work):
            weave_slot()
        emit_p0_flush()

        S.barrier()

        cur2 = Cursor()
        xqTs = cur2.bf16(CC * OWN)
        hid = cur2.f32(2 * I)
        hq_t = cur2.bf16(I)
        hqT = cur2.bf16(2 * NI * 128)
        gstg = cur2.f32(2 * 1024)
        wg_a = cur2.bf16(CC * 512); wg_b = cur2.bf16(CC * 512)
        wu_a = cur2.bf16(CC * 512); wu_b = cur2.bf16(CC * 512)
        wd_a = cur2.bf16(NI * 128); wd_b = cur2.bf16(NI * 128)
        sgn2 = cur2.f32(I)
        blk2 = cur2.f32(256)
        chain2 = cur2.off

        xq3 = xqTs.rearrange("p (q s) -> p q s", q=CC)
        cxq = ld(xqTs, xqT_d[:])
        hid3 = hid.rearrange("p (t w) -> p t w", t=2)
        hqT3 = hqT.rearrange("p (t ic w) -> p t ic w", t=2, ic=NI)
        g3 = gstg.rearrange("p (t w) -> p t w", t=2)
        NDSL = 8
        gu_pref = {}
        p2s = {"wbuf_mm": {0: None, 1: None}, "se_pP0": 0, "se_pP1": 0,
               "ge_hd": 0}
        wdbuf_mm = {0: None, 1: None}
        wd_pref = {}
        se_pS = {0: 0, 1: 0}
        q_done = {0: 0, 1: 0}
        sp_guard = {0: None, 1: None}

        def issue_gu_load(k, isl):
            par = (k * NISL + isl) % 2
            wg = wg_a if par == 0 else wg_b
            wu = wu_a if par == 0 else wu_b
            ww = (p2s["wbuf_mm"][par],) if p2s["wbuf_mm"][par] else ()
            clg = ld(wg, wgq_d[:, isl * CC * 512:(isl + 1) * CC * 512], wait=ww)
            clu = ld(wu, wuq_d[:, isl * CC * 512:(isl + 1) * CC * 512], wait=ww)
            gu_pref[(k, isl)] = (clg, clu, wg, wu)

        def issue_wd_load(k, csl):
            par = csl % 2
            wd = wd_a if par == 0 else wd_b
            ww = (wdbuf_mm[par],) if wdbuf_mm[par] else ()
            cld = ld(wd, wdq_d[:, csl * NI * 128:(csl + 1) * NI * 128], wait=ww)
            wd_pref[(k, csl)] = (cld, wd)

        issue_gu_load(0, 0)
        issue_wd_load(0, 0)
        issue_wd_load(0, 1)
        hd_last = {0: 0, 1: 0}

        def emit_gu(k):
            par = k % 2
            for isl in range(NISL):
                clg, clu, wg, wu = gu_pref.pop((k, isl))
                wg3 = wg.rearrange("p (q s) -> p q s", q=CC)
                wu3 = wu.rearrange("p (q s) -> p q s", q=CC)
                wtm = [clg, clu, cxq, ("SE", p2s["se_pP1"])]
                cg = None
                for q in range(CC):
                    cg = top(lambda e, qq=q, k_=k, w3=wg3: e.matmul(
                        psP0[:, 0:512], xq3[:, qq, k_ * 128:(k_ + 1) * 128],
                        w3[:, qq, :], start=(qq == 0), stop=(qq == CC - 1)),
                        wait=tuple(wtm) if q == 0 else (),
                        inc="TE" if q == CC - 1 else None)
                cu = None
                for q in range(CC):
                    cu = top(lambda e, qq=q, k_=k, w3=wu3: e.matmul(
                        psP1[:, 0:512], xq3[:, qq, k_ * 128:(k_ + 1) * 128],
                        w3[:, qq, :], start=(qq == 0), stop=(qq == CC - 1)),
                        inc="TE" if q == CC - 1 else None)
                wsl = [("TE", cg)]
                if isl == 0 and q_done[par]:
                    wsl.append(("VE", q_done[par]))
                csl = sop(lambda e, p_=par, i_=isl: e.activation(
                    hid3[:, p_, i_ * 512:(i_ + 1) * 512], psP0[:, 0:512],
                    ACT.Silu, bias=0.0, scale=scal[:, 12:13]), wait=tuple(wsl))
                p2s["se_pP0"] = csl
                cup = sop(lambda e: e.activation(rcpr[:, 0:512], psP1[:, 0:512],
                                                 ACT.Copy, bias=0.0,
                                                 scale=scal[:, 13:14]),
                          wait=(("TE", cu), ("GE", p2s["ge_hd"])))
                p2s["se_pP1"] = cup
                hd = gop(lambda e, p_=par, i_=isl: e.tensor_tensor(
                    hid3[:, p_, i_ * 512:(i_ + 1) * 512],
                    hid3[:, p_, i_ * 512:(i_ + 1) * 512], rcpr[:, 0:512], ALU.mult),
                    wait=(("SE", cup),))
                p2s["ge_hd"] = hd
                p2s["wbuf_mm"][(k * NISL + isl) % 2] = ("TE", cu)
                if isl + 1 < NISL:
                    issue_gu_load(k, isl + 1)
                elif k + 1 < NTO:
                    issue_gu_load(k + 1, 0)
            hd_last[par] = hd

        emit_gu(0)
        for k in range(NTO):
            par = k % 2
            csg = sop(lambda e, p_=par: e.sign(sgn2, hid3[:, p_, :]),
                      wait=(("GE", hd_last[par]), ("VE", q_done[1 - par])))
            S.emit("vector", None, wait=(("SE", csg), ("TE", S.c("TE"))))
            cq = quant_tile(chain2, blk2, hid3[:, par, :], sgn2, hq_t, I,
                            scal[:, 4:5], scal[:, 5:6])
            q_done[par] = cq
            if k + 1 < NTO:
                emit_gu(k + 1)
            for ic in range(NI):
                hh = ic % 2
                ct = top(lambda e, i_=ic, h_=hh: e.transpose(
                    psB[:, h_ * 128:(h_ + 1) * 128],
                    hq_t[:, i_ * 128:(i_ + 1) * 128], idb[:]),
                    wait=(("VE", cq), ("SE", S.c("SE"))))
                sop(lambda e, p_=par, i_=ic, h_=hh: e.copy(
                    hqT3[:, p_, i_, :], psB[:, h_ * 128:(h_ + 1) * 128]),
                    wait=(("TE", ct),))
            se_cp = S.c("SE")
            for csl_i in range(NDSL):
                cld, wd = wd_pref.pop((k, csl_i))
                wd3 = wd.rearrange("p (ic s) -> p ic s", ic=NI)
                psS = psS0 if csl_i % 2 == 0 else psS1
                wtd = [cld, ("SE", se_cp), ("SE", se_pS[csl_i % 2])]
                cd = None
                for ic in range(NI):
                    cd = top(lambda e, p_=par, i_=ic, w3=wd3, ps=psS: e.matmul(
                        ps[:, 0:128], hqT3[:, p_, i_, :], w3[:, i_, :],
                        start=(i_ == 0), stop=(i_ == NI - 1)),
                        wait=tuple(wtd) if ic == 0 else (),
                        inc="TE" if ic == NI - 1 else None)
                wdbuf_mm[csl_i % 2] = ("TE", cd)
                if csl_i + 2 < NDSL:
                    issue_wd_load(k, csl_i + 2)
                elif k + 1 < NTO:
                    issue_wd_load(k + 1, csl_i + 2 - NDSL)
                wds = [("TE", cd)]
                if csl_i == 0 and sp_guard[par]:
                    wds.append(sp_guard[par])
                cds = sop(lambda e, p_=par, c_=csl_i, ps=psS: e.activation(
                    g3[:, p_, c_ * 128:(c_ + 1) * 128], ps[:, 0:128],
                    ACT.Silu, bias=0.0, scale=scal[:, 14:15]), wait=tuple(wds))
                se_pS[csl_i % 2] = cds
            sp_guard[par] = stq(g_d[k * 128:(k + 1) * 128, :], g3[:, par, :],
                                wait=(("SE", S.c("SE")),))

        S.barrier()

        cur4 = Cursor(0)
        st_a = cur4.f32(1024); st_b = cur4.f32(1024)
        xt4_a = cur4.f32(1024); xt4_b = cur4.f32(1024)
        ot4_a = cur4.f32(1024); ot4_b = cur4.f32(1024)
        gt4_a = cur4.f32(1024); gt4_b = cur4.f32(1024)
        for tt in range(NTO):
            sbuf_s = st_a if tt % 2 == 0 else st_b
            sbuf_x = xt4_a if tt % 2 == 0 else xt4_b
            obuf = ot4_a if tt % 2 == 0 else ot4_b
            gbuf4 = gt4_a if tt % 2 == 0 else gt4_b
            wv = (("VE", S.c("VE")),) if tt >= 2 else ()
            cl1 = ld(sbuf_s, s_d[tt * 128:(tt + 1) * 128, :], wait=wv)
            cl3 = ld(gbuf4, g_d[tt * 128:(tt + 1) * 128, :], wait=wv)
            cl2 = ld(sbuf_x, x_full[tt * 128:(tt + 1) * 128, :], wait=wv)
            wsp = st_all() if tt >= 2 else ()
            vop(lambda e, s=sbuf_s, g_=gbuf4: e.tensor_tensor(s, s, g_, ALU.mult),
                wait=(cl1, cl3) + tuple(wsp))
            cfin = vop(lambda e, s=sbuf_s, x=sbuf_x, o=obuf: e.tensor_tensor(
                o, x, s, ALU.add), wait=(cl2,))
            stq(out_t[tt * 128:(tt + 1) * 128, :], obuf, wait=(("VE", cfin),))

        S.emit("gpsimd", None, wait=st_all())

        nc._script = S
        with nc.Block() as block:
            def make_runner(eng_name):
                def runner(eng):
                    for fn, waits, inc, inc_by, drain in S.ops[eng_name]:
                        for sname, val in waits:
                            if val > 0:
                                eng.wait_ge(sems[sname], val)
                        if drain and fn is not None:
                            eng.drain()
                        if fn is None:
                            continue
                        ins = fn(eng)
                        if inc is not None and ins is not None:
                            ins.then_inc(sems[inc], inc_by)
                return runner
            block.sync(make_runner("sync"))
            block.tensor(make_runner("tensor"))
            block.vector(make_runner("vector"))
            block.scalar(make_runner("scalar"))
            block.gpsimd(make_runner("gpsimd"))

    return nc


def _rope_tables(n, d=64, base=10000.0):
    try:
        import jax
        import jax.numpy as jnp
        cpu = jax.devices("cpu")[0]
        with jax.default_device(cpu):
            position = jnp.arange(n, dtype=jnp.float32)[:, None]
            div_term = jnp.exp(jnp.arange(0, d, 2, dtype=jnp.float32)
                               * (-np.log(base) / d))
            ang = position * div_term
            return np.asarray(jnp.sin(ang)), np.asarray(jnp.cos(ang))
    except Exception:
        position = np.arange(n, dtype=np.float32)[:, None]
        div_term = np.exp(np.arange(0, d, 2, dtype=np.float32)
                          * np.float32(-np.log(base) / d)).astype(np.float32)
        ang = (position * div_term).astype(np.float32)
        return np.sin(ang).astype(np.float32), np.cos(ang).astype(np.float32)


def _tile_tables(tab, NT):
    return np.ascontiguousarray(
        tab.reshape(NT, 128, 32).transpose(1, 0, 2).reshape(128, NT * 32))


def make_core_inputs(x, w_qkv, b_qkv, w_lepe, b_lepe, gate_w, up_w, down_w,
                     s_in, s_in_down, b, j, seq):
    OWN = seq // 2
    NT = seq // 128
    roll = j * OWN
    xr = np.ascontiguousarray(np.roll(np.asarray(x[b]), -roll, axis=0)).astype(np.float32)
    sin_full, cos_full = _rope_tables(seq)
    sin_r = np.roll(sin_full, -roll, axis=0)
    cos_r = np.roll(cos_full, -roll, axis=0)
    D = 1024
    f32 = np.float32
    wq = np.ascontiguousarray(np.asarray(w_qkv[0:D]).T).astype(f32)
    wk = np.ascontiguousarray(np.asarray(w_qkv[D:2 * D]).T).astype(f32)
    wv = np.ascontiguousarray(np.asarray(w_qkv[2 * D:3 * D]).T).astype(f32)
    lw = np.ascontiguousarray(
        np.asarray(w_lepe).reshape(8, 128, 3).transpose(1, 0, 2).reshape(128, 24)).astype(f32)
    lb = np.ascontiguousarray(
        np.asarray(b_lepe).reshape(8, 128).T).astype(f32)
    maskL = 1.0 if j == 1 else 0.0
    maskR = 1.0 if j == 0 else 0.0
    return {
        "x_full": xr,
        "wqT": wq, "wkT": wk, "wvT": wv,
        "bias_b": np.ascontiguousarray(np.broadcast_to(np.asarray(b_qkv, f32), (128, 3 * D))),
        "gate_w": np.asarray(gate_w, f32), "up_w": np.asarray(up_w, f32),
        "down_w": np.asarray(down_w, f32),
        "s_in_b": np.full((128, 1), np.asarray(s_in).reshape(-1)[0], f32),
        "s_ind_b": np.full((128, 1), np.asarray(s_in_down).reshape(-1)[0], f32),
        "lepe_w": lw, "lepe_b": lb,
        "sin_t": _tile_tables(sin_r, NT), "cos_t": _tile_tables(cos_r, NT),
        "masks": np.ascontiguousarray(np.broadcast_to(
            np.array([maskL, maskR], f32), (128, 2))),
        "ident_f": np.eye(128, dtype=f32),
        "ident_b": np.eye(128).astype(ml_dtypes.bfloat16),
        "ones_in": np.ones((128, 128), f32),
    }


def kernel(L0_attn_x, L0_attn_w_qkv, L0_attn_b_qkv, L0_attn_w_lepe, L0_attn_b_lepe,
           L0_ffn_gate_w, L0_ffn_up_w, L0_ffn_down_w, L0_ffn_s_in, L0_ffn_s_in_down):
    SEQ = 4096
    OWN = SEQ // 2
    x = np.asarray(L0_attn_x, np.float32)
    in_maps = []
    for core in range(8):
        b, j = core // 2, core % 2
        in_maps.append(make_core_inputs(
            x, L0_attn_w_qkv, L0_attn_b_qkv, L0_attn_w_lepe, L0_attn_b_lepe,
            L0_ffn_gate_w, L0_ffn_up_w, L0_ffn_down_w, L0_ffn_s_in, L0_ffn_s_in_down,
            b, j, SEQ))
    nc = build_nc(seq=SEQ, drains=True)
    res = run_bass_kernel_spmd(nc, in_maps, core_ids=list(range(8)))
    out = np.empty((4, SEQ, 1024), np.float32)
    for core in range(8):
        b, j = core // 2, core % 2
        out[b, j * OWN:(j + 1) * OWN, :] = np.asarray(res.results[core]["out_s"])
    return out


# version 26
# speedup vs baseline: 1.0267x; 1.0267x over previous
import numpy as np
import ml_dtypes
import concourse.bass as bass
import concourse.mybir as mybir
from concourse.bass_utils import run_bass_kernel_spmd
from contextlib import ExitStack

F32 = mybir.dt.float32
BF16 = mybir.dt.bfloat16
ALU = mybir.AluOpType
ACT = mybir.ActivationFunctionType
AX = mybir.AxisListType

FP8_MAX = 448.0
FP4_MAX = 6.0
GSC = FP8_MAX * FP4_MAX
EXP_BIAS = -8.0

ENGINES = ("sync", "tensor", "vector", "scalar", "gpsimd")


class Script:

    def __init__(self):
        self.ops = {e: [] for e in ENGINES}
        self.counts = {}

    def emit(self, eng, fn, wait=(), inc=None, inc_by=1, drain=False):
        self.ops[eng].append((fn, tuple(wait), inc, inc_by, drain))
        if inc is not None:
            self.counts[inc] = self.counts.get(inc, 0) + inc_by
            return self.counts[inc]
        return None

    def c(self, sem):
        return self.counts.get(sem, 0)

    def barrier(self):
        snap = tuple((s, v) for s, v in self.counts.items() if v > 0)
        for e in ENGINES:
            self.emit(e, None, wait=snap)


def build_nc(seq=4096, drains=False, dbg=False):
    SEQ = seq
    OWN = SEQ // 2
    NT = SEQ // 128
    NTO = OWN // 128
    D = 1024
    CC = 8
    NP = 8
    I = 4096
    NI = I // 128
    NISL = I // 512
    CHT = min(2, NTO)
    NCH = NTO // CHT
    NQT = max(OWN // 512, 1)
    QTW = min(512, OWN)
    GRP = 4 if NT % 4 == 0 else 1

    nc = bass.Bass("TRN2", target_bir_lowering=False)

    def din(name, shape, dt=F32):
        return nc.dram_tensor(name, shape, dt, kind="ExternalInput").ap()

    x_full = din("x_full", [SEQ, D])
    wqT = din("wqT", [D, D]); wkT = din("wkT", [D, D]); wvT = din("wvT", [D, D])
    bias_b = din("bias_b", [128, 3 * D])
    gate_w = din("gate_w", [I, D]); up_w = din("up_w", [I, D]); down_w = din("down_w", [D, I])
    s_in_b = din("s_in_b", [128, 1]); s_ind_b = din("s_ind_b", [128, 1])
    lepe_w = din("lepe_w", [128, NP * 3]); lepe_b = din("lepe_b", [128, NP])
    sin_t = din("sin_t", [128, NT * 32]); cos_t = din("cos_t", [128, NT * 32])
    masks = din("masks", [128, 2])
    ident_f = din("ident_f", [128, 128]); ident_b = din("ident_b", [128, 128], BF16)
    ones_in = din("ones_in", [128, 128])

    out_t = nc.dram_tensor("out_s", [OWN, D], F32, kind="ExternalOutput").ap()

    KIND = "ExternalOutput" if dbg else "Internal"
    wgq_d = nc.dram_tensor("wgq_d", [128, CC * I], BF16, kind=KIND).ap()
    wuq_d = nc.dram_tensor("wuq_d", [128, CC * I], BF16, kind=KIND).ap()
    wdq_d = nc.dram_tensor("wdq_d", [128, NI * D], BF16, kind=KIND).ap()
    xqT_d = nc.dram_tensor("xqT_d", [128, CC * OWN], BF16, kind=KIND).ap()
    xnT_d = nc.dram_tensor("xnT_d", [128, CC * SEQ], F32, kind="Internal").ap()
    s_d = nc.dram_tensor("s_d", [OWN, D], F32, kind=KIND).ap()
    g_d = nc.dram_tensor("g_d", [OWN, D], F32, kind=KIND).ap()
    if dbg:
        dbg_xnt = nc.dram_tensor("dbg_xnt", [128, CC * SEQ], F32, kind="ExternalOutput").ap()

    ctx = ExitStack()
    with ctx:
        def sbuf(name, shape, dt=F32):
            return ctx.enter_context(nc.sbuf_tensor(name, shape, dt))

        sinc = sbuf("sinc", [128, NT * 32]); cosc = sbuf("cosc", [128, NT * 32])
        lepw = sbuf("lepw", [128, NP * 3]); lepb = sbuf("lepb", [128, NP])
        maskc = sbuf("maskc", [128, 2])
        idf = sbuf("idf", [128, 128]); idb = sbuf("idb", [128, 128], BF16)
        onesc = sbuf("onesc", [128, 128])
        scal = sbuf("scal", [128, 32])
        expbuf = sbuf("expbuf", [128, 1024])
        biasg = sbuf("biasg", [128, 384])
        rcpr = sbuf("rcpr", [128, 512])

        rem = nc.sbuf_bytes_remaining
        acols = (rem // 4 // 128) * 128 - 128
        arena = sbuf("arena", [128, acols])
        arena_b = arena[:].bitcast(BF16)

        def psum(name, dt=F32, cols=512):
            return ctx.enter_context(nc.psum_tensor(name, [128, cols], dt))
        psS0 = psum("psS0"); psS1 = psum("psS1")
        psA0 = psum("psA0"); psA1 = psum("psA1")
        psP0 = psum("psP0"); psP1 = psum("psP1")
        psT = psum("psT")
        psB = psum("psB", BF16, 1024)

        sem_names = (["TE", "VE", "SE", "GE"] + [f"L{i}" for i in range(8)]
                     + [f"S{i}" for i in range(4)])
        sems = {s: ctx.enter_context(nc.semaphore(s)) for s in sem_names}

        S = Script()

        rr = {"ld": 0, "st": 0}

        def ld(dst_ap, src_ap, wait=()):
            sem = f"L{rr['ld'] % 8}"
            rr["ld"] += 1
            w = tuple(wait)
            prev = S.c(sem)
            if prev:
                w = w + ((sem, prev),)
            v = S.emit("sync", lambda e, d=dst_ap, s=src_ap: e.dma_start(out=d, in_=s),
                       wait=w, inc=sem, inc_by=16)
            return (sem, v)

        def ld_all():
            return tuple((f"L{i}", S.c(f"L{i}")) for i in range(8) if S.c(f"L{i}"))

        def stq(dst_ap, src_ap, wait=()):
            sem = f"S{rr['st'] % 4}"
            rr["st"] += 1
            w = tuple(wait)
            prev = S.c(sem)
            if prev:
                w = w + ((sem, prev),)
            v = S.emit("gpsimd", lambda e, d=dst_ap, s=src_ap: e.dma_start(out=d, in_=s),
                       wait=w, inc=sem, inc_by=16)
            return (sem, v)

        def st_all():
            return tuple((f"S{i}", S.c(f"S{i}")) for i in range(4) if S.c(f"S{i}"))

        dr_v = drains in (True, "all")
        dr_s = drains in (True, "all", "scalar")

        def vop(fn, wait=(), inc="VE", drain=None):
            return S.emit("vector", fn, wait=wait, inc=inc,
                          drain=dr_v if drain is None else drain)

        def sop(fn, wait=(), inc="SE", drain=None):
            return S.emit("scalar", fn, wait=wait, inc=inc,
                          drain=dr_s if drain is None else drain)

        def top(fn, wait=(), inc="TE"):
            return S.emit("tensor", fn, wait=wait, inc=inc)

        def gop(fn, wait=(), inc="GE", drain=None):
            return S.emit("gpsimd", fn, wait=wait, inc=inc,
                          drain=dr_v if drain is None else drain)

        class Cursor:
            def __init__(self, off=0):
                self.off = off
            def f32(self, n):
                assert self.off + n <= acols, f"arena overflow: {self.off}+{n} > {acols}"
                ap = arena[:, self.off:self.off + n]
                self.off += n
                return ap
            def bf16(self, n):
                assert n % 2 == 0
                assert self.off + n // 2 <= acols, "arena overflow (bf16)"
                ap = arena_b[:, 2 * self.off:2 * self.off + n]
                self.off += n // 2
                return ap

        ld(sinc[:], sin_t[:]); ld(cosc[:], cos_t[:])
        ld(lepw[:], lepe_w[:]); ld(lepb[:], lepe_b[:])
        ld(maskc[:], masks[:]); ld(idf[:], ident_f[:]); ld(idb[:], ident_b[:])
        ld(onesc[:], ones_in[:])
        ld(scal[:, 0:1], s_in_b[:]); ld(scal[:, 1:2], s_ind_b[:])
        c0all = ld_all()
        vop(lambda e: e.tensor_scalar(scal[:, 2:3], scal[:, 0:1], 1.0 / 6.0, None, ALU.mult),
            wait=c0all)
        vop(lambda e: e.tensor_scalar(scal[:, 3:4], scal[:, 0:1], 2.0, None, ALU.mult))
        vop(lambda e: e.tensor_scalar(scal[:, 4:5], scal[:, 1:2], 1.0 / 6.0, None, ALU.mult))
        vop(lambda e: e.tensor_scalar(scal[:, 5:6], scal[:, 1:2], 2.0, None, ALU.mult))
        vop(lambda e: e.memset(scal[:, 18:19], 1e-5))
        vop(lambda e: e.memset(scal[:, 19:20], EXP_BIAS))
        c_consts = ld_all()
        for _e in ("tensor", "scalar", "gpsimd", "vector"):
            S.emit(_e, None, wait=c_consts)

        def quant_block_chain(cur, am, gs6_ap, gs2_ap, nb):
            r2 = cur.f32(nb); s8h = cur.f32(nb)
            sc = cur.f32(nb); p = cur.f32(nb); pinv = cur.f32(nb)
            t1 = cur.f32(nb); t2 = cur.f32(nb); fl = cur.f32(nb)
            vop(lambda e: e.tensor_scalar(sc, am, gs6_ap, 448.0, ALU.mult, ALU.min))
            vop(lambda e: e.tensor_scalar(t1, sc, 4.0, None, ALU.is_ge))
            vop(lambda e: e.tensor_scalar(p, t1, 255.0, 1.0, ALU.mult, ALU.add))
            vop(lambda e: e.tensor_scalar(pinv, t1, (2.0 ** -8) - 1.0, 1.0, ALU.mult, ALU.add))
            for mul, thr in ((15.0, 0.25), (3.0, 2.0 ** -4), (1.0, 2.0 ** -5)):
                vop(lambda e, th=thr: e.tensor_scalar(t2, p, th, None, ALU.mult))
                vop(lambda e: e.tensor_tensor(t1, sc, t2, ALU.is_ge))
                vop(lambda e, m=mul: e.tensor_scalar(t2, t1, m, 1.0, ALU.mult, ALU.add))
                vop(lambda e: e.tensor_tensor(p, p, t2, ALU.mult))
                vop(lambda e, m=mul: e.tensor_scalar(t2, t1, 1.0 / (m + 1.0) - 1.0, 1.0,
                                                     ALU.mult, ALU.add))
                vop(lambda e: e.tensor_tensor(pinv, pinv, t2, ALU.mult))
            vop(lambda e: e.scalar_tensor_tensor(t1, sc, 512.0, pinv, ALU.mult, ALU.mult))
            vop(lambda e: e.tensor_scalar(t1, t1, 0.5, None, ALU.add))
            vop(lambda e: e.tensor_scalar(t2, t1, 16.0, 16.0, ALU.is_ge, ALU.mult))
            vop(lambda e: e.tensor_tensor(t1, t1, t2, ALU.subtract))
            vop(lambda e: e.tensor_copy(fl, t2))
            vop(lambda e: e.tensor_scalar(t2, t1, 8.0, 8.0, ALU.is_ge, ALU.mult))
            vop(lambda e: e.tensor_tensor(t1, t1, t2, ALU.subtract))
            vop(lambda e: e.tensor_tensor(fl, fl, t2, ALU.add))
            vop(lambda e: e.tensor_scalar(t2, t1, 4.0, 4.0, ALU.is_ge, ALU.mult))
            vop(lambda e: e.tensor_tensor(t1, t1, t2, ALU.subtract))
            vop(lambda e: e.tensor_tensor(fl, fl, t2, ALU.add))
            vop(lambda e: e.tensor_scalar(t2, t1, 2.0, 2.0, ALU.is_ge, ALU.mult))
            vop(lambda e: e.tensor_tensor(t1, t1, t2, ALU.subtract))
            vop(lambda e: e.tensor_tensor(fl, fl, t2, ALU.add))
            vop(lambda e: e.scalar_tensor_tensor(fl, t1, 1.0, fl, ALU.is_ge, ALU.add))
            vop(lambda e: e.scalar_tensor_tensor(sc, fl, 2.0 ** -9, p, ALU.mult, ALU.mult))
            vop(lambda e: e.tensor_scalar(s8h, sc, 0.5, None, ALU.mult))
            vop(lambda e: e.tensor_single_scalar(t1, sc, 0.0, ALU.is_equal))
            vop(lambda e: e.tensor_tensor(t2, sc, t1, ALU.max))
            vop(lambda e: e.reciprocal(t2, t2))
            vop(lambda e: e.tensor_scalar(t2, t2, gs2_ap, None, ALU.mult))
            vop(lambda e: e.tensor_scalar(fl, t1, -1.0, 1.0, ALU.mult, ALU.add))
            vop(lambda e: e.tensor_tensor(t2, t2, fl, ALU.mult))
            c = vop(lambda e: e.scalar_tensor_tensor(r2, t1, 2.0, t2, ALU.mult, ALU.add))
            return r2, s8h, c

        def quant_elem_chain(cur, w_ap, sgn_ap, r2, s8h, out_ap, ncols, nb):
            u = cur.f32(ncols); A = cur.f32(ncols); Bv = cur.f32(ncols)
            t = cur.f32(ncols); h = cur.f32(ncols); c = cur.f32(ncols)
            r2b = r2.unsqueeze(2).broadcast_to([128, nb, 16])
            s8hb = s8h.unsqueeze(2).broadcast_to([128, nb, 16])
            u3 = u.rearrange("p (g b) -> p g b", b=16)
            vop(lambda e: e.scalar_tensor_tensor(u, w_ap, -1.0, w_ap, ALU.mult, ALU.max))
            vop(lambda e: e.tensor_tensor(u3, u3, r2b, ALU.mult))
            vop(lambda e: e.tensor_scalar(u, u, 12.0, None, ALU.min))
            vop(lambda e: e.tensor_scalar(A, u, 4.0, 0.5, ALU.is_ge, ALU.mult))
            vop(lambda e: e.tensor_scalar(Bv, u, 8.0, 0.25, ALU.is_ge, ALU.mult))
            vop(lambda e: e.tensor_tensor(t, A, Bv, ALU.add))
            vop(lambda e: e.scalar_tensor_tensor(h, Bv, 3.0, t, ALU.mult, ALU.add))
            vop(lambda e: e.tensor_scalar(t, t, -1.0, 1.0, ALU.mult, ALU.add))
            vop(lambda e: e.tensor_scalar(A, h, 2.0, 1.0, ALU.mult, ALU.add))
            vop(lambda e: e.scalar_tensor_tensor(h, u, 0.5, h, ALU.add, ALU.add))
            vop(lambda e: e.tensor_tensor(h, h, t, ALU.mult))
            vop(lambda e: e.tensor_scalar(c, h, 1.0, None, ALU.is_ge))
            vop(lambda e: e.scalar_tensor_tensor(c, h, 2.0, c, ALU.is_ge, ALU.add))
            vop(lambda e: e.scalar_tensor_tensor(c, h, 3.0, c, ALU.is_ge, ALU.add))
            vop(lambda e: e.scalar_tensor_tensor(c, h, 4.0, c, ALU.is_ge, ALU.add))
            vop(lambda e: e.tensor_tensor(c, c, A, ALU.mult))
            vop(lambda e: e.tensor_tensor(c, c, sgn_ap, ALU.mult))
            o3 = out_ap.rearrange("p (g b) -> p g b", b=16)
            c3 = c.rearrange("p (g b) -> p g b", b=16)
            return vop(lambda e: e.tensor_tensor(o3, c3, s8hb, ALU.mult))

        def quant_tile(chain_base, blk_ap, src_ap, sgn_ap, out_ap, C, gs6_ap, gs2_ap):
            nb = C // 16
            vop(lambda e: e.reduce_max(
                blk_ap[:, 0:nb], src_ap[:, 0:C].rearrange("p (g k) -> p g k", k=16),
                axis=AX.X, apply_absolute_value=True))
            curq = Cursor(chain_base)
            r2, s8h, _ = quant_block_chain(curq, blk_ap[:, 0:nb], gs6_ap, gs2_ap, nb)
            elem_base = chain_base + 2 * nb
            cq = None
            for sub in range(C // 1024):
                sl = slice(sub * 1024, (sub + 1) * 1024)
                bsl = slice(sub * 64, (sub + 1) * 64)
                cq = quant_elem_chain(Cursor(elem_base),
                                      src_ap[:, sl], sgn_ap[:, sl],
                                      r2[:, bsl], s8h[:, bsl],
                                      out_ap[:, sl], 1024, 64)
            return cq

        def cross_partition_max(src_col, dst_col, tmp_row):
            ct = top(lambda e: e.transpose(psT[0:1, 0:128], src_col, idf[:]),
                     wait=(("VE", S.c("VE")), ("SE", S.c("SE"))))
            cv = vop(lambda e: e.tensor_copy(tmp_row[0:1, 0:128], psT[0:1, 0:128]),
                     wait=(("TE", ct),))
            cv = vop(lambda e: e.reduce_max(tmp_row[0:1, 0:1], tmp_row[0:1, 0:128], axis=AX.X))
            ct = top(lambda e: e.matmul(psT[:, 0:1], onesc[0:1, :], tmp_row[0:1, 0:1],
                                        start=True, stop=True), wait=(("VE", cv),))
            return vop(lambda e: e.tensor_copy(dst_col, psT[:, 0:1]), wait=(("TE", ct),))

        P0_NEED = 4096 + 4096 + 2048 + 2048 + 512 + 2 * 256 + 6 * 1024
        p0_cur = Cursor(acols - P0_NEED)
        wt_a = p0_cur.f32(4096)
        sgn0 = p0_cur.f32(4096)
        wq_a = p0_cur.bf16(4096)
        stage0 = p0_cur.bf16(4096)
        blk0 = p0_cur.f32(512)
        chain_base0 = p0_cur.off

        wcfg = [
            (gate_w, wgq_d, I, D, 6, 9, 15, 12, 0),
            (up_w, wuq_d, I, D, 7, 10, 16, 13, 0),
            (down_w, wdq_d, D, I, 8, 11, 17, 14, 1),
        ]
        p0_state = {"last_sp": None, "last_q": None, "last_ct": None,
                    "pending": None, "widx": 0}

        def emit_p0_passA():
            for (src_, dst, rows, C, sl6, sl2, slm, slr, sli) in wcfg:
                nrt = rows // 128
                gmax = scal[:, slm:slm + 1]
                last_v = S.c("VE")
                for ot in range(nrt):
                    cl = ld(wt_a[:, 0:C], src_[ot * 128:(ot + 1) * 128, :],
                            wait=(("VE", last_v),))
                    if ot == 0:
                        last_v = vop(lambda e, cc=C, g_=gmax: e.reduce_max(
                            g_, wt_a[:, 0:cc], axis=AX.X, apply_absolute_value=True),
                            wait=(cl,))
                    else:
                        vop(lambda e, cc=C: e.reduce_max(
                            blk0[:, 0:1], wt_a[:, 0:cc], axis=AX.X,
                            apply_absolute_value=True), wait=(cl,))
                        last_v = vop(lambda e, g_=gmax: e.tensor_tensor(
                            g_, g_, blk0[:, 0:1], ALU.max))
                cross_partition_max(gmax, gmax, rcpr)
                vop(lambda e, g_=gmax: e.reciprocal(blk0[:, 0:1], g_))
                vop(lambda e, a=sl6: e.tensor_scalar(scal[:, a:a + 1], blk0[:, 0:1],
                                                     GSC / 6.0, None, ALU.mult))
                vop(lambda e, a=sl2: e.tensor_scalar(scal[:, a:a + 1], blk0[:, 0:1],
                                                     GSC * 2.0, None, ALU.mult))
                vop(lambda e, s=sli: e.reciprocal(blk0[:, 1:2], scal[:, s:s + 1]))
                vop(lambda e, r=slr, g_=gmax: e.scalar_tensor_tensor(
                    scal[:, r:r + 1], g_, 1.0 / GSC, blk0[:, 1:2], ALU.mult, ALU.mult))

        p0_work = []
        for wi, (src_, dst, rows, C, sl6, sl2, slm, slr, sli) in enumerate(wcfg):
            for ot in range(rows // 128):
                p0_work.append((src_, dst, C, sl6, sl2, ot))

        def emit_p0_quant(src_, dst, C, sl6, sl2, ot):
            last_sp = p0_state["last_sp"]
            wsp0 = (last_sp,) if last_sp else ()
            wq = (("VE", p0_state["last_q"]),) if p0_state["last_q"] else (("VE", S.c("VE")),)
            cl = ld(wt_a[:, 0:C], src_[ot * 128:(ot + 1) * 128, :], wait=wq + wsp0)
            csg = sop(lambda e, cc=C: e.sign(sgn0[:, 0:cc], wt_a[:, 0:cc]), wait=(cl,))
            wtr = (("TE", p0_state["last_ct"]),) if p0_state["last_ct"] else ()
            S.emit("vector", None, wait=(("SE", csg), cl) + wtr)
            cq = quant_tile(chain_base0, blk0, wt_a, sgn0, wq_a, C,
                            scal[:, sl6:sl6 + 1], scal[:, sl2:sl2 + 1])
            p0_state["last_q"] = cq
            S.emit("scalar", None, wait=(("VE", cq),))
            p0_state["pending"] = (dst, C, ot, cq)

        def emit_p0_flush():
            if p0_state["pending"] is None:
                return
            dst, C, ot, cq = p0_state["pending"]
            p0_state["pending"] = None
            last_sp = p0_state["last_sp"]
            wsp0 = (last_sp,) if last_sp else ()
            ncc = C // 128
            ct = None
            for q in range(ncc):
                hh = q % 2
                ct = top(lambda e, qq=q, h_=hh: e.transpose(
                    psB[:, h_ * 128:(h_ + 1) * 128],
                    wq_a[:, qq * 128:(qq + 1) * 128], idb[:]),
                    wait=(("VE", cq), ("SE", S.c("SE"))) + wsp0)
                sop(lambda e, qq=q, h_=hh: e.copy(
                    stage0[:, qq * 128:(qq + 1) * 128],
                    psB[:, h_ * 128:(h_ + 1) * 128]), wait=(("TE", ct),))
            p0_state["last_ct"] = ct
            if C == 1024:
                dv4 = dst.rearrange("p (il q r) -> p il q r", il=NISL, q=ncc, r=512)
                dview = dv4[:, ot // 4, :, (ot % 4) * 128:(ot % 4 + 1) * 128]
            else:
                dview = dst.rearrange("p (db ic r) -> p db ic r",
                                      db=8, ic=NI, r=128)[:, ot, :, :]
            sview = stage0[:, 0:ncc * 128].rearrange("p (q r) -> p q r", q=ncc)
            p0_state["last_sp"] = stq(dview, sview, wait=(("SE", S.c("SE")),))

        def weave_slot():
            emit_p0_flush()
            if p0_state["widx"] < len(p0_work):
                emit_p0_quant(*p0_work[p0_state["widx"]])
                p0_state["widx"] += 1

        S.barrier()

        cur1 = Cursor()
        p3_base = 0
        xnstage = cur1.f32(1024)
        xt_a = cur1.f32(1024); xt_b = cur1.f32(1024)
        xn_t = cur1.f32(1024); sq_t = cur1.f32(1024)
        mv = cur1.f32(4)
        xq_t = cur1.bf16(1024)
        xqstage = cur1.bf16(1024)
        sgn1 = cur1.f32(1024)
        blk1 = cur1.f32(64)
        chain1 = cur1.off

        xnT3d = xnT_d.rearrange("p (q s) -> p q s", q=CC)
        xnst3 = xnstage.rearrange("p (q s) -> p q s", q=CC)
        prev_done = S.c("VE")
        prev_sp = None
        prev_sp_xn = None
        xn_guard = ()
        for tt in range(NT):
            buf = xt_a if tt % 2 == 0 else xt_b
            cl = ld(buf, x_full[tt * 128:(tt + 1) * 128, :],
                    wait=(("VE", prev_done),) + ((prev_sp,) if prev_sp else ()))
            vop(lambda e, b=buf: e.reduce_sum(mv[:, 0:1], b, axis=AX.X), wait=(cl,))
            vop(lambda e: e.tensor_scalar(mv[:, 0:1], mv[:, 0:1], 1.0 / D, None, ALU.mult))
            cve = vop(lambda e, b=buf: e.tensor_scalar(xn_t, b, mv[:, 0:1], None, ALU.subtract),
                       wait=xn_guard)
            sop(lambda e: e.activation(sq_t, xn_t, ACT.Square, bias=0.0, scale=1.0,
                                       accum_out=mv[:, 2:3]), wait=(("VE", cve),))
            cs2 = sop(lambda e: e.activation(mv[:, 3:4], mv[:, 2:3], ACT.Sqrt,
                                             bias=scal[:, 18:19], scale=1.0 / D))
            vop(lambda e: e.reciprocal(mv[:, 1:2], mv[:, 3:4]), wait=(("SE", cs2),))
            cxn = vop(lambda e: e.tensor_scalar(xn_t, xn_t, mv[:, 1:2], None, ALU.mult))
            for q in range(CC):
                wtr = [("VE", cxn), ("SE", S.c("SE"))]
                if q == 0 and prev_sp_xn:
                    wtr.append(prev_sp_xn)
                ct = top(lambda e, qq=q: e.transpose(
                    psT[:, 0:128], xn_t[:, qq * 128:(qq + 1) * 128], idf[:]),
                    wait=tuple(wtr))
                sop(lambda e, qq=q: e.copy(xnst3[:, qq, :], psT[:, 0:128]),
                    wait=(("TE", ct),))
            prev_sp_xn = stq(xnT3d[:, :, tt * 128:(tt + 1) * 128], xnst3,
                             wait=(("SE", S.c("SE")),))
            prev_done = cxn
            if tt < NTO:
                csg = sop(lambda e: e.sign(sgn1, xn_t), wait=(("VE", cxn),))
                S.emit("vector", None, wait=(("SE", csg),))
                cq = quant_tile(chain1, blk1, xn_t, sgn1, xq_t, 1024,
                                scal[:, 2:3], scal[:, 3:4])
                S.emit("scalar", None, wait=(("VE", cq),))
                for q in range(CC):
                    hh = q % 2
                    ct = top(lambda e, qq=q, h_=hh: e.transpose(
                        psB[:, h_ * 128:(h_ + 1) * 128],
                        xq_t[:, qq * 128:(qq + 1) * 128], idb[:]),
                        wait=(("VE", cq), ("SE", S.c("SE"))) + ((prev_sp,) if prev_sp else ()))
                    sop(lambda e, qq=q, h_=hh: e.copy(
                        xqstage[:, qq * 128:(qq + 1) * 128],
                        psB[:, h_ * 128:(h_ + 1) * 128]), wait=(("TE", ct),))
                dv = xqT_d.rearrange("p (q s) -> p q s", q=CC)[:, :, tt * 128:(tt + 1) * 128]
                sv = xqstage.rearrange("p (q s) -> p q s", q=CC)
                prev_sp = stq(dv, sv, wait=(("SE", S.c("SE")),))
                prev_done = cq
            xn_guard = (("TE", S.c("TE")), ("SE", S.c("SE")))

        S.barrier()
        if dbg:
            stq(dbg_xnt[:], xnT)
            S.barrier()

        cur3 = Cursor(p3_base)
        xnb_a = cur3.f32(CC * GRP * 128); xnb_b = cur3.f32(CC * GRP * 128)
        KS1 = cur3.bf16(SEQ); KS2 = cur3.bf16(SEQ)
        QS1 = cur3.bf16(OWN); QS2 = cur3.bf16(OWN)
        QW1 = cur3.bf16(OWN); QW2 = cur3.bf16(OWN)
        VH1 = cur3.bf16(NT * 65); VL1 = cur3.bf16(NT * 65)
        VH2 = cur3.bf16(NT * 65); VL2 = cur3.bf16(NT * 65)
        phr = cur3.f32(1024)
        PhiB = phr[:, 0:512].bitcast(BF16)
        PloB = phr[:, 512:1024].bitcast(BF16)
        vtf = cur3.f32(OWN + 2)
        kvg_off = cur3.off
        kvg = cur3.f32(GRP * 256)
        ket = cur3.f32(GRP * 128)
        rtw = cur3.f32(GRP * 128)
        rt1 = rtw[:, 0:GRP * 64]; rt2 = rtw[:, GRP * 64:GRP * 128]
        sstg = rcpr[:, 0:128]
        assert cur3.off <= acols - P0_NEED, f"P3 region {cur3.off} overlaps P0 at {acols - P0_NEED}"
        assert GRP * 256 + GRP * 128 + 2 * GRP * 64 >= OWN, "sfm alias too small"
        sfm = arena[:, kvg_off:kvg_off + OWN]

        def kv_slot(q):
            if q < 4:
                return expbuf[:, q * 256:(q + 1) * 256]
            if q < 6:
                return rcpr[:, (q - 4) * 256:(q - 3) * 256]
            return phr[:, (q - 6) * 256:(q - 5) * 256]

        def wq_slot(q):
            return expbuf[:, q * 128:(q + 1) * 128]

        def proc_group(tiles, is_q):
            n = len(tiles)
            if is_q:
                src = kvg.rearrange("p (t w) -> p t w", t=GRP)[:, 0:n, 0:128]
                bslc = biasg[:, 0:128]
            else:
                src = kvg.rearrange("p (t w) -> p t w", t=GRP)[:, 0:n, 0:128]
                bslc = biasg[:, 128:256]
            ketv = ket.rearrange("p (t w) -> p t w", t=GRP)[:, 0:n, :]
            bb = bslc.unsqueeze(1).broadcast_to([128, n, 128])
            vop(lambda e: e.tensor_tensor(ketv, src, bb, ALU.add))
            kkv = kvg.rearrange("p (t w) -> p t w", t=GRP)[:, 0:n, 0:128]
            rtv = rtw[:, 0:n * 128]
            ketf = ket[:, 0:n * 128]
            ket3 = ket.rearrange("p (t w) -> p t w", t=GRP)[:, 0:n, :]
            cv = vop(lambda e: e.tensor_scalar(kkv, ket3, 0.0, None, ALU.min))
            cs = sop(lambda e: e.activation(rtv, kkv, ACT.Exp, bias=0.0, scale=1.0),
                     wait=(("VE", cv),))
            vop(lambda e: e.tensor_scalar(kkv, ket3, 0.0, None, ALU.max),
                wait=(("SE", cs),))
            rtv3 = rtv.rearrange("p (t w) -> p t w", t=n)
            vop(lambda e: e.tensor_tensor(ket3, kkv, rtv3, ALU.add))
            t0 = tiles[0]
            ket5 = ket.rearrange("p (t h j u) -> p t h j u", h=2, j=32, u=2)
            x1 = ket5[:, 0:n, :, :, 0]
            x2 = ket5[:, 0:n, :, :, 1]
            sv = sinc[:, t0 * 32:(t0 + n) * 32].rearrange("p (t j) -> p t j", t=n)
            svb = sv.unsqueeze(2).broadcast_to([128, n, 2, 32])
            cvv = cosc[:, t0 * 32:(t0 + n) * 32].rearrange("p (t j) -> p t j", t=n)
            cvb = cvv.unsqueeze(2).broadcast_to([128, n, 2, 32])
            r1v3 = rt1.rearrange("p (t h j) -> p t h j", t=GRP, h=2)[:, 0:n]
            r2v3 = rt2.rearrange("p (t h j) -> p t h j", t=GRP, h=2)[:, 0:n]
            dd = kvg.rearrange("p (t q) -> p t q", t=GRP)[:, 0:n, 0:128]
            de = dd.rearrange("p t (h u j) -> p t h u j", h=2, u=2, j=32)[:, :, :, 0, :]
            do = dd.rearrange("p t (h u j) -> p t h u j", h=2, u=2, j=32)[:, :, :, 1, :]
            vop(lambda e: e.tensor_tensor(r1v3, x1, cvb, ALU.mult))
            vop(lambda e: e.tensor_tensor(r2v3, x2, svb, ALU.mult))
            vop(lambda e: e.tensor_tensor(de, r1v3, r2v3, ALU.subtract))
            vop(lambda e: e.tensor_tensor(r1v3, x1, svb, ALU.mult))
            vop(lambda e: e.tensor_tensor(r2v3, x2, cvb, ALU.mult))
            cfin = vop(lambda e: e.tensor_tensor(do, r1v3, r2v3, ALU.add))
            stg = (rt1.bitcast(BF16).rearrange("p (t w) -> p t w", t=GRP)[:, 0:n, 0:128],
                   rt2.bitcast(BF16).rearrange("p (t w) -> p t w", t=GRP)[:, 0:n, 0:128])
            kr4 = kvg.rearrange("p (t w) -> p t w", t=GRP)[:, 0:n, 0:128]

            def tr_into(sg, dstT, tiles, dep):
                for i, tt in enumerate(tiles):
                    hh = i % 2
                    ct = top(lambda e, ii=i, s_=sg, h_=hh: e.transpose(
                        psB[:, h_ * 128:(h_ + 1) * 128], s_[:, ii, :], idb[:]),
                        wait=(("VE", dep), ("SE", S.c("SE"))))
                    sop(lambda e, t=tt, d=dstT, h_=hh: e.copy(
                        d[:, t * 128:(t + 1) * 128],
                        psB[:, h_ * 128:(h_ + 1) * 128]), wait=(("TE", ct),))

            for h in range(2):
                sg = stg[0]
                hs = slice(h * 64, (h + 1) * 64)
                vop(lambda e, s_=sg, k_=hs: e.tensor_copy(s_[:, :, 0:64], kr4[:, :, k_]),
                    wait=(("TE", S.c("TE")),))
                cfin2 = vop(lambda e, s_=sg, k_=hs: e.tensor_tensor(
                    s_[:, :, 64:128], kr4[:, :, k_], s_[:, :, 0:64], ALU.subtract))
                dstT = (QS1 if h == 0 else QS2) if is_q else (KS1 if h == 0 else KS2)
                tr_into(sg, dstT, tiles, cfin2)
                if is_q:
                    sw = stg[1]
                    vop(lambda e, a=sw, b=sg: e.tensor_copy(a[:, :, 0:64], b[:, :, 64:128]),
                        wait=(("TE", S.c("TE")),))
                    cfin3 = vop(lambda e, a=sw, b=sg: e.tensor_copy(a[:, :, 64:128],
                                                                    b[:, :, 0:64]))
                    tr_into(sw, QW1 if h == 0 else QW2, tiles, cfin3)
            return S.c("SE")

        emit_p0_passA()
        p0_per_pair = (len(p0_work) + NP - 1) // NP
        for g in range(NP):
            pb_te = S.c("TE"); pb_ve = S.c("VE"); pb_se = S.c("SE")
            for wi, wsrc in enumerate((wkT, wvT)):
                for q in range(CC):
                    ld(kv_slot(q)[:, wi * 128:(wi + 1) * 128],
                       wsrc[q * 128:(q + 1) * 128, g * 128:(g + 1) * 128],
                       wait=(("TE", pb_te), ("VE", pb_ve), ("SE", pb_se)) + st_all())
            ld(biasg[:, 0:128], bias_b[:, g * 128:(g + 1) * 128])
            ld(biasg[:, 128:256], bias_b[:, D + g * 128:D + (g + 1) * 128])
            cbl = ld(biasg[:, 256:384], bias_b[:, 2 * D + g * 128:2 * D + (g + 1) * 128])
            wld = ld_all()
            cv1 = vop(lambda e: e.memset(VH1, 1.0), wait=(("TE", pb_te),))
            vop(lambda e: e.memset(VH2, 1.0))
            vop(lambda e: e.memset(VL1, 0.0))
            cv2 = vop(lambda e: e.memset(VL2, 0.0))

            pbanks = (psP0, psP1, psS0, psS1)
            pb_evac = {0: ("VE", pb_ve), 1: ("VE", pb_ve),
                       2: ("VE", pb_ve), 3: ("VE", pb_ve)}
            xnb_use = {0: 0, 1: 0}
            for gi, g0 in enumerate(range(0, NT, GRP)):
                gts = list(range(g0, min(g0 + GRP, NT)))
                xnb = xnb_a if gi % 2 == 0 else xnb_b
                xnb3 = xnb.rearrange("p (q s) -> p q s", q=CC)
                cxl = ld(xnb3[:, :, 0:len(gts) * 128],
                         xnT3d[:, :, g0 * 128:(g0 + len(gts)) * 128],
                         wait=(("TE", xnb_use[gi % 2]),) + ((prev_sp_xn,) if gi == 0 else ()))
                for i, tt in enumerate(gts):
                    bi = i % 4
                    pp = pbanks[bi]
                    wt = list(wld) + [pb_evac[bi], cxl]
                    ctev = None
                    for q in range(CC):
                        ctev = top(lambda e, qq=q, ii=i, p_=pp, x_=xnb3: e.matmul(
                            p_[:, 0:256], x_[:, qq, ii * 128:(ii + 1) * 128],
                            kv_slot(qq), start=(qq == 0), stop=(qq == CC - 1)),
                            wait=tuple(wt) if q == 0 else (),
                            inc="TE" if q == CC - 1 else None)
                    pb_evac[bi] = ("VE", vop(lambda e, ii=i, p_=pp: e.tensor_copy(
                        kvg[:, ii * 256:(ii + 1) * 256], p_[:, 0:256]),
                        wait=(("TE", S.c("TE")),)))
                xnb_use[gi % 2] = S.c("TE")
                ng = len(gts)
                kv4 = kvg.rearrange("p (t w) -> p t w", t=GRP)[:, 0:ng, :]
                proc_group(gts, is_q=False)
                ketv = ket.rearrange("p (t w) -> p t w", t=GRP)[:, 0:ng, :]
                bvp = biasg[:, 256:384].unsqueeze(1).broadcast_to([128, ng, 128])
                vop(lambda e: e.tensor_tensor(ketv, kv4[:, :, 128:256], bvp, ALU.add),
                    wait=(("TE", S.c("TE")), ("SE", S.c("SE"))))
                for h, (VH, VL) in enumerate(((VH1, VL1), (VH2, VL2))):
                    hd = slice(h * 64, (h + 1) * 64)
                    vhd = VH.rearrange("p (t w) -> p t w", t=NT)[:, g0:g0 + ng, 0:64]
                    vld = VL.rearrange("p (t w) -> p t w", t=NT)[:, g0:g0 + ng, 0:64]
                    vop(lambda e, d_=vhd, s_=hd: e.tensor_copy(d_, ketv[:, :, s_]))
                    vop(lambda e, d_=vld, s_=hd, h_=vhd: e.tensor_tensor(
                        d_, ketv[:, :, s_], h_, ALU.subtract))
                for i, tt in enumerate(gts):
                    if not (tt <= NTO or tt == NT - 1):
                        continue
                    ctv = top(lambda e, ii=i: e.transpose(
                        psT[:, 0:128], ketv[:, ii, :], idf[:]),
                        wait=(("VE", S.c("VE")), ("SE", S.c("SE"))))
                    if tt < NTO:
                        sop(lambda e, t=tt: e.copy(
                            vtf[:, 1 + t * 128:1 + (t + 1) * 128], psT[:, 0:128]),
                            wait=(("TE", ctv),))
                    elif tt == NTO:
                        sop(lambda e: e.copy(vtf[:, OWN + 1:OWN + 2], psT[:, 0:1]),
                            wait=(("TE", ctv),))
                    if tt == NT - 1:
                        sop(lambda e: e.copy(vtf[:, 0:1], psT[:, 127:128]),
                            wait=(("TE", ctv),))
            kse = S.c("SE")

            for q in range(CC):
                ld(wq_slot(q), wqT[q * 128:(q + 1) * 128, g * 128:(g + 1) * 128],
                   wait=(("TE", S.c("TE")), ("SE", S.c("SE"))))
            wldq = ld_all()
            for gi, g0 in enumerate(range(0, NTO, GRP)):
                gts = list(range(g0, min(g0 + GRP, NTO)))
                xnb = xnb_a if gi % 2 == 0 else xnb_b
                xnb3 = xnb.rearrange("p (q s) -> p q s", q=CC)
                cxl = ld(xnb3[:, :, 0:len(gts) * 128],
                         xnT3d[:, :, g0 * 128:(g0 + len(gts)) * 128],
                         wait=(("TE", xnb_use[gi % 2]),))
                for i, tt in enumerate(gts):
                    bi = i % 4
                    pp = pbanks[bi]
                    wt = list(wldq) + [pb_evac[bi], cxl]
                    ctev = None
                    for q in range(CC):
                        ctev = top(lambda e, qq=q, ii=i, p_=pp, x_=xnb3: e.matmul(
                            p_[:, 0:128], x_[:, qq, ii * 128:(ii + 1) * 128],
                            wq_slot(qq), start=(qq == 0), stop=(qq == CC - 1)),
                            wait=tuple(wt) if q == 0 else (),
                            inc="TE" if q == CC - 1 else None)
                    pb_evac[bi] = ("VE", vop(lambda e, ii=i, p_=pp: e.tensor_copy(
                        kvg[:, ii * 256:ii * 256 + 128], p_[:, 0:128]),
                        wait=(("TE", S.c("TE")),)))
                xnb_use[gi % 2] = S.c("TE")
                proc_group(gts, is_q=True)

            chalo = S.c("SE")
            vop(lambda e: e.tensor_scalar(vtf[:, 0:1], vtf[:, 0:1], maskc[:, 0:1],
                                          None, ALU.mult), wait=(("SE", chalo),))
            vop(lambda e: e.tensor_scalar(vtf[:, OWN + 1:OWN + 2], vtf[:, OWN + 1:OWN + 2],
                                          maskc[:, 1:2], None, ALU.mult))

            qkT_done = S.c("SE")
            prev_exp = [qkT_done, qkT_done]
            prev_av = [S.c("TE"), S.c("TE")]
            prev_plo = [S.c("VE"), S.c("VE")]
            psA_free = {0: S.c("VE"), 1: S.c("VE")}
            for h in range(2):
                psA = psA0 if h == 0 else psA1
                VH = VH1 if h == 0 else VH2
                VL = VL1 if h == 0 else VL2
                for qt in range(NQT):
                    exp_c = {}
                    av_c = {}
                    plo_c = {}
                    hr = slice(h * 64, (h + 1) * 64)

                    KS = KS1 if h == 0 else KS2
                    QS = QS1 if h == 0 else QS2
                    QW = QW1 if h == 0 else QW2

                    def emit_s(kb):
                        pss = psS0 if kb % 2 == 0 else psS1
                        w = [("SE", exp_c[kb - 2] if kb >= 2 else prev_exp[kb % 2])]
                        top(lambda e, k=kb, p_=pss, q_=qt, K_=KS, Q_=QS: e.matmul(
                            p_[:, 0:QTW], K_[:, k * 128:(k + 1) * 128],
                            Q_[:, q_ * QTW:(q_ + 1) * QTW],
                            start=True, stop=False), wait=tuple(w), inc=None)
                        cs_mm = top(lambda e, k=kb, p_=pss, q_=qt, K_=KS, Q_=QW: e.matmul(
                            p_[:, 0:QTW], K_[:, k * 128:(k + 1) * 128],
                            Q_[:, q_ * QTW:(q_ + 1) * QTW],
                            start=False, stop=True))
                        sl = slice((kb % 2) * 512, (kb % 2) * 512 + QTW)
                        w2 = [("TE", cs_mm),
                              ("VE", plo_c[kb - 2] if kb >= 2 else prev_plo[kb % 2])]
                        sop(lambda e, p_=pss, s_=sl: e.activation(
                            expbuf[:, s_], p_[:, 0:QTW],
                            ACT.Exp, bias=scal[:, 19:20], scale=0.125), wait=tuple(w2),
                            drain=False)
                        w2b = [("TE", av_c[kb - 2] if kb >= 2 else prev_av[kb % 2])]
                        exp_c[kb] = sop(lambda e, p_=pss, s_=sl: e.activation(
                            PhiB[:, s_], p_[:, 0:QTW],
                            ACT.Exp, bias=scal[:, 19:20], scale=0.125), wait=tuple(w2b),
                            drain=False)
                        plo_c[kb] = vop(lambda e, s_=sl: e.tensor_tensor(
                            PloB[:, s_], expbuf[:, s_], PhiB[:, s_], ALU.subtract),
                            wait=(("SE", exp_c[kb]),), drain=False)

                    def emit_av(kb):
                        sl = slice((kb % 2) * 512, (kb % 2) * 512 + QTW)
                        wav = [("SE", exp_c[kb])]
                        if kb == 0:
                            wav.append(("VE", psA_free[h]))
                        top(lambda e, k=kb, pA=psA, V_=VH, s_=sl: e.matmul(
                            pA[0:65, 0:QTW], V_[:, k * 65:(k + 1) * 65],
                            PhiB[:, s_], start=(k == 0), stop=False),
                            wait=tuple(wav), inc=None)
                        top(lambda e, k=kb, pA=psA, V_=VL, s_=sl: e.matmul(
                            pA[0:65, 0:QTW], V_[:, k * 65:(k + 1) * 65],
                            PhiB[:, s_], start=False, stop=False), inc=None)
                        av_c[kb] = top(lambda e, k=kb, pA=psA, V_=VH, s_=sl: e.matmul(
                            pA[0:65, 0:QTW], V_[:, k * 65:(k + 1) * 65],
                            PloB[:, s_], start=False, stop=(k == NT - 1)),
                            wait=(("VE", plo_c[kb]),))

                    emit_s(0)
                    for kb in range(1, NT):
                        emit_s(kb)
                        emit_av(kb - 1)
                    emit_av(NT - 1)
                    prev_exp = [exp_c[NT - 2], exp_c[NT - 1]]
                    prev_av = [av_c[NT - 2], av_c[NT - 1]]
                    prev_plo = [plo_c[NT - 2], plo_c[NT - 1]]
                    crc = vop(lambda e, pA=psA: e.reciprocal(rcpr[0:1, 0:QTW], pA[64:65, 0:QTW]),
                              wait=(("TE", av_c[NT - 1]), ("VE", S.c("VE"))))
                    crx = top(lambda e: e.matmul(psT[0:64, 0:QTW], onesc[0:1, 0:64],
                                                 rcpr[0:1, 0:QTW], start=True, stop=True),
                              wait=(("VE", crc),))
                    sf = sfm[h * 64:(h + 1) * 64, qt * QTW:(qt + 1) * QTW]
                    cev = vop(lambda e, s=sf, pA=psA: e.tensor_copy(s, pA[0:64, 0:QTW]))
                    psA_free[h] = vop(
                        lambda e, s=sf: e.tensor_tensor(s, s, psT[0:64, 0:QTW], ALU.mult),
                        wait=(("TE", crx),))
                    s_idx = h * NQT + qt
                    wtarget = ((g * 2 * NQT + s_idx + 1) * len(p0_work)) // (NP * 2 * NQT)
                    while p0_state["widx"] < wtarget:
                        weave_slot()

            wl0 = lepw[:, g * 3:g * 3 + 1]; wl1 = lepw[:, g * 3 + 1:g * 3 + 2]
            wl2 = lepw[:, g * 3 + 2:g * 3 + 3]; bl = lepb[:, g:g + 1]
            sf_all = sfm[:, 0:OWN]
            vop(lambda e, w_=wl0: e.scalar_tensor_tensor(sf_all, vtf[:, 0:OWN], w_, sf_all,
                                                         ALU.mult, ALU.add),
                wait=(("TE", S.c("TE")),))
            vop(lambda e, w_=wl1: e.scalar_tensor_tensor(sf_all, vtf[:, 1:OWN + 1], w_, sf_all,
                                                         ALU.mult, ALU.add))
            vop(lambda e, w_=wl2: e.scalar_tensor_tensor(sf_all, vtf[:, 2:OWN + 2], w_, sf_all,
                                                         ALU.mult, ALU.add))
            cfin = vop(lambda e, b_=bl: e.tensor_scalar(sf_all, sf_all, b_, None, ALU.add))
            for tt in range(NTO):
                ct = top(lambda e, t=tt: e.transpose(
                    psT[:, 0:128], sfm[:, t * 128:(t + 1) * 128], idf[:]),
                    wait=(("VE", cfin), ("SE", S.c("SE"))))
                cse = sop(lambda e: e.copy(sstg[:, 0:128], psT[:, 0:128]), wait=(("TE", ct),))
                stq(s_d[tt * 128:(tt + 1) * 128, g * 128:(g + 1) * 128], sstg[:, 0:128],
                    wait=(("SE", cse),))
                S.emit("scalar", None, wait=st_all())


        while p0_state["widx"] < len(p0_work):
            weave_slot()
        emit_p0_flush()

        S.barrier()

        cur2 = Cursor()
        xqTs = cur2.bf16(CC * OWN)
        hid = cur2.f32(2 * I)
        hq_t = cur2.bf16(I)
        hqT = cur2.bf16(2 * NI * 128)
        gstg = cur2.f32(2 * 1024)
        wg_a = cur2.bf16(CC * 512); wg_b = cur2.bf16(CC * 512)
        wu_a = cur2.bf16(CC * 512); wu_b = cur2.bf16(CC * 512)
        wd_a = cur2.bf16(NI * 128); wd_b = cur2.bf16(NI * 128)
        sgn2 = cur2.f32(I)
        blk2 = cur2.f32(256)
        chain2 = cur2.off

        xq3 = xqTs.rearrange("p (q s) -> p q s", q=CC)
        cxq = ld(xqTs, xqT_d[:])
        hid3 = hid.rearrange("p (t w) -> p t w", t=2)
        hqT3 = hqT.rearrange("p (t ic w) -> p t ic w", t=2, ic=NI)
        g3 = gstg.rearrange("p (t w) -> p t w", t=2)
        NDSL = 8
        gu_pref = {}
        p2s = {"wbuf_mm": {0: None, 1: None}, "se_pP0": 0, "se_pP1": 0,
               "ge_hd": 0}
        wdbuf_mm = {0: None, 1: None}
        wd_pref = {}
        se_pS = {0: 0, 1: 0}
        q_done = {0: 0, 1: 0}
        sp_guard = {0: None, 1: None}

        def issue_gu_load(k, isl):
            par = (k * NISL + isl) % 2
            wg = wg_a if par == 0 else wg_b
            wu = wu_a if par == 0 else wu_b
            ww = (p2s["wbuf_mm"][par],) if p2s["wbuf_mm"][par] else ()
            clg = ld(wg, wgq_d[:, isl * CC * 512:(isl + 1) * CC * 512], wait=ww)
            clu = ld(wu, wuq_d[:, isl * CC * 512:(isl + 1) * CC * 512], wait=ww)
            gu_pref[(k, isl)] = (clg, clu, wg, wu)

        def issue_wd_load(k, csl):
            par = csl % 2
            wd = wd_a if par == 0 else wd_b
            ww = (wdbuf_mm[par],) if wdbuf_mm[par] else ()
            cld = ld(wd, wdq_d[:, csl * NI * 128:(csl + 1) * NI * 128], wait=ww)
            wd_pref[(k, csl)] = (cld, wd)

        issue_gu_load(0, 0)
        issue_wd_load(0, 0)
        issue_wd_load(0, 1)
        hd_last = {0: 0, 1: 0}

        def emit_gu(k):
            par = k % 2
            for isl in range(NISL):
                clg, clu, wg, wu = gu_pref.pop((k, isl))
                wg3 = wg.rearrange("p (q s) -> p q s", q=CC)
                wu3 = wu.rearrange("p (q s) -> p q s", q=CC)
                wtm = [clg, clu, cxq, ("SE", p2s["se_pP1"])]
                cg = None
                for q in range(CC):
                    cg = top(lambda e, qq=q, k_=k, w3=wg3: e.matmul(
                        psP0[:, 0:512], xq3[:, qq, k_ * 128:(k_ + 1) * 128],
                        w3[:, qq, :], start=(qq == 0), stop=(qq == CC - 1)),
                        wait=tuple(wtm) if q == 0 else (),
                        inc="TE" if q == CC - 1 else None)
                cu = None
                for q in range(CC):
                    cu = top(lambda e, qq=q, k_=k, w3=wu3: e.matmul(
                        psP1[:, 0:512], xq3[:, qq, k_ * 128:(k_ + 1) * 128],
                        w3[:, qq, :], start=(qq == 0), stop=(qq == CC - 1)),
                        inc="TE" if q == CC - 1 else None)
                wsl = [("TE", cg)]
                if isl == 0 and q_done[par]:
                    wsl.append(("VE", q_done[par]))
                csl = sop(lambda e, p_=par, i_=isl: e.activation(
                    hid3[:, p_, i_ * 512:(i_ + 1) * 512], psP0[:, 0:512],
                    ACT.Silu, bias=0.0, scale=scal[:, 12:13]), wait=tuple(wsl))
                p2s["se_pP0"] = csl
                cup = sop(lambda e: e.activation(rcpr[:, 0:512], psP1[:, 0:512],
                                                 ACT.Copy, bias=0.0,
                                                 scale=scal[:, 13:14]),
                          wait=(("TE", cu), ("GE", p2s["ge_hd"])))
                p2s["se_pP1"] = cup
                hd = gop(lambda e, p_=par, i_=isl: e.tensor_tensor(
                    hid3[:, p_, i_ * 512:(i_ + 1) * 512],
                    hid3[:, p_, i_ * 512:(i_ + 1) * 512], rcpr[:, 0:512], ALU.mult),
                    wait=(("SE", cup),))
                p2s["ge_hd"] = hd
                p2s["wbuf_mm"][(k * NISL + isl) % 2] = ("TE", cu)
                if isl + 1 < NISL:
                    issue_gu_load(k, isl + 1)
                elif k + 1 < NTO:
                    issue_gu_load(k + 1, 0)
            hd_last[par] = hd

        emit_gu(0)
        for k in range(NTO):
            par = k % 2
            csg = sop(lambda e, p_=par: e.sign(sgn2, hid3[:, p_, :]),
                      wait=(("GE", hd_last[par]), ("VE", q_done[1 - par])))
            S.emit("vector", None, wait=(("SE", csg), ("TE", S.c("TE"))))
            cq = quant_tile(chain2, blk2, hid3[:, par, :], sgn2, hq_t, I,
                            scal[:, 4:5], scal[:, 5:6])
            q_done[par] = cq
            if k + 1 < NTO:
                emit_gu(k + 1)
            for ic in range(NI):
                hh = ic % 2
                ct = top(lambda e, i_=ic, h_=hh: e.transpose(
                    psB[:, h_ * 128:(h_ + 1) * 128],
                    hq_t[:, i_ * 128:(i_ + 1) * 128], idb[:]),
                    wait=(("VE", cq), ("SE", S.c("SE"))))
                sop(lambda e, p_=par, i_=ic, h_=hh: e.copy(
                    hqT3[:, p_, i_, :], psB[:, h_ * 128:(h_ + 1) * 128]),
                    wait=(("TE", ct),))
            se_cp = S.c("SE")
            for csl_i in range(NDSL):
                cld, wd = wd_pref.pop((k, csl_i))
                wd3 = wd.rearrange("p (ic s) -> p ic s", ic=NI)
                psS = psS0 if csl_i % 2 == 0 else psS1
                wtd = [cld, ("SE", se_cp), ("SE", se_pS[csl_i % 2])]
                cd = None
                for ic in range(NI):
                    cd = top(lambda e, p_=par, i_=ic, w3=wd3, ps=psS: e.matmul(
                        ps[:, 0:128], hqT3[:, p_, i_, :], w3[:, i_, :],
                        start=(i_ == 0), stop=(i_ == NI - 1)),
                        wait=tuple(wtd) if ic == 0 else (),
                        inc="TE" if ic == NI - 1 else None)
                wdbuf_mm[csl_i % 2] = ("TE", cd)
                if csl_i + 2 < NDSL:
                    issue_wd_load(k, csl_i + 2)
                elif k + 1 < NTO:
                    issue_wd_load(k + 1, csl_i + 2 - NDSL)
                wds = [("TE", cd)]
                if csl_i == 0 and sp_guard[par]:
                    wds.append(sp_guard[par])
                cds = sop(lambda e, p_=par, c_=csl_i, ps=psS: e.activation(
                    g3[:, p_, c_ * 128:(c_ + 1) * 128], ps[:, 0:128],
                    ACT.Silu, bias=0.0, scale=scal[:, 14:15]), wait=tuple(wds))
                se_pS[csl_i % 2] = cds
            sp_guard[par] = stq(g_d[k * 128:(k + 1) * 128, :], g3[:, par, :],
                                wait=(("SE", S.c("SE")),))

        S.barrier()

        cur4 = Cursor(0)
        st_a = cur4.f32(1024); st_b = cur4.f32(1024)
        xt4_a = cur4.f32(1024); xt4_b = cur4.f32(1024)
        ot4_a = cur4.f32(1024); ot4_b = cur4.f32(1024)
        gt4_a = cur4.f32(1024); gt4_b = cur4.f32(1024)
        for tt in range(NTO):
            sbuf_s = st_a if tt % 2 == 0 else st_b
            sbuf_x = xt4_a if tt % 2 == 0 else xt4_b
            obuf = ot4_a if tt % 2 == 0 else ot4_b
            gbuf4 = gt4_a if tt % 2 == 0 else gt4_b
            wv = (("VE", S.c("VE")),) if tt >= 2 else ()
            cl1 = ld(sbuf_s, s_d[tt * 128:(tt + 1) * 128, :], wait=wv)
            cl3 = ld(gbuf4, g_d[tt * 128:(tt + 1) * 128, :], wait=wv)
            cl2 = ld(sbuf_x, x_full[tt * 128:(tt + 1) * 128, :], wait=wv)
            wsp = st_all() if tt >= 2 else ()
            vop(lambda e, s=sbuf_s, g_=gbuf4: e.tensor_tensor(s, s, g_, ALU.mult),
                wait=(cl1, cl3) + tuple(wsp))
            cfin = vop(lambda e, s=sbuf_s, x=sbuf_x, o=obuf: e.tensor_tensor(
                o, x, s, ALU.add), wait=(cl2,))
            stq(out_t[tt * 128:(tt + 1) * 128, :], obuf, wait=(("VE", cfin),))

        S.emit("gpsimd", None, wait=st_all())

        nc._script = S
        with nc.Block() as block:
            def make_runner(eng_name):
                def runner(eng):
                    for fn, waits, inc, inc_by, drain in S.ops[eng_name]:
                        for sname, val in waits:
                            if val > 0:
                                eng.wait_ge(sems[sname], val)
                        if drain and fn is not None:
                            eng.drain()
                        if fn is None:
                            continue
                        ins = fn(eng)
                        if inc is not None and ins is not None:
                            ins.then_inc(sems[inc], inc_by)
                return runner
            block.sync(make_runner("sync"))
            block.tensor(make_runner("tensor"))
            block.vector(make_runner("vector"))
            block.scalar(make_runner("scalar"))
            block.gpsimd(make_runner("gpsimd"))

    return nc


def _rope_tables(n, d=64, base=10000.0):
    try:
        import jax
        import jax.numpy as jnp
        cpu = jax.devices("cpu")[0]
        with jax.default_device(cpu):
            position = jnp.arange(n, dtype=jnp.float32)[:, None]
            div_term = jnp.exp(jnp.arange(0, d, 2, dtype=jnp.float32)
                               * (-np.log(base) / d))
            ang = position * div_term
            return np.asarray(jnp.sin(ang)), np.asarray(jnp.cos(ang))
    except Exception:
        position = np.arange(n, dtype=np.float32)[:, None]
        div_term = np.exp(np.arange(0, d, 2, dtype=np.float32)
                          * np.float32(-np.log(base) / d)).astype(np.float32)
        ang = (position * div_term).astype(np.float32)
        return np.sin(ang).astype(np.float32), np.cos(ang).astype(np.float32)


def _tile_tables(tab, NT):
    return np.ascontiguousarray(
        tab.reshape(NT, 128, 32).transpose(1, 0, 2).reshape(128, NT * 32))


def make_core_inputs(x, w_qkv, b_qkv, w_lepe, b_lepe, gate_w, up_w, down_w,
                     s_in, s_in_down, b, j, seq):
    OWN = seq // 2
    NT = seq // 128
    roll = j * OWN
    xr = np.ascontiguousarray(np.roll(np.asarray(x[b]), -roll, axis=0)).astype(np.float32)
    sin_full, cos_full = _rope_tables(seq)
    sin_r = np.roll(sin_full, -roll, axis=0)
    cos_r = np.roll(cos_full, -roll, axis=0)
    D = 1024
    f32 = np.float32
    wq = np.ascontiguousarray(np.asarray(w_qkv[0:D]).T).astype(f32)
    wk = np.ascontiguousarray(np.asarray(w_qkv[D:2 * D]).T).astype(f32)
    wv = np.ascontiguousarray(np.asarray(w_qkv[2 * D:3 * D]).T).astype(f32)
    lw = np.ascontiguousarray(
        np.asarray(w_lepe).reshape(8, 128, 3).transpose(1, 0, 2).reshape(128, 24)).astype(f32)
    lb = np.ascontiguousarray(
        np.asarray(b_lepe).reshape(8, 128).T).astype(f32)
    maskL = 1.0 if j == 1 else 0.0
    maskR = 1.0 if j == 0 else 0.0
    return {
        "x_full": xr,
        "wqT": wq, "wkT": wk, "wvT": wv,
        "bias_b": np.ascontiguousarray(np.broadcast_to(np.asarray(b_qkv, f32), (128, 3 * D))),
        "gate_w": np.asarray(gate_w, f32), "up_w": np.asarray(up_w, f32),
        "down_w": np.asarray(down_w, f32),
        "s_in_b": np.full((128, 1), np.asarray(s_in).reshape(-1)[0], f32),
        "s_ind_b": np.full((128, 1), np.asarray(s_in_down).reshape(-1)[0], f32),
        "lepe_w": lw, "lepe_b": lb,
        "sin_t": _tile_tables(sin_r, NT), "cos_t": _tile_tables(cos_r, NT),
        "masks": np.ascontiguousarray(np.broadcast_to(
            np.array([maskL, maskR], f32), (128, 2))),
        "ident_f": np.eye(128, dtype=f32),
        "ident_b": np.eye(128).astype(ml_dtypes.bfloat16),
        "ones_in": np.ones((128, 128), f32),
    }


def kernel(L0_attn_x, L0_attn_w_qkv, L0_attn_b_qkv, L0_attn_w_lepe, L0_attn_b_lepe,
           L0_ffn_gate_w, L0_ffn_up_w, L0_ffn_down_w, L0_ffn_s_in, L0_ffn_s_in_down):
    SEQ = 4096
    OWN = SEQ // 2
    x = np.asarray(L0_attn_x, np.float32)
    in_maps = []
    for core in range(8):
        b, j = core // 2, core % 2
        in_maps.append(make_core_inputs(
            x, L0_attn_w_qkv, L0_attn_b_qkv, L0_attn_w_lepe, L0_attn_b_lepe,
            L0_ffn_gate_w, L0_ffn_up_w, L0_ffn_down_w, L0_ffn_s_in, L0_ffn_s_in_down,
            b, j, SEQ))
    nc = build_nc(seq=SEQ, drains=True)
    res = run_bass_kernel_spmd(nc, in_maps, core_ids=list(range(8)))
    out = np.empty((4, SEQ, 1024), np.float32)
    for core in range(8):
        b, j = core // 2, core % 2
        out[b, j * OWN:(j + 1) * OWN, :] = np.asarray(res.results[core]["out_s"])
    return out


# version 27
# speedup vs baseline: 1.0278x; 1.0011x over previous
import numpy as np
import ml_dtypes
import concourse.bass as bass
import concourse.mybir as mybir
from concourse.bass_utils import run_bass_kernel_spmd
from contextlib import ExitStack

F32 = mybir.dt.float32
BF16 = mybir.dt.bfloat16
ALU = mybir.AluOpType
ACT = mybir.ActivationFunctionType
AX = mybir.AxisListType

FP8_MAX = 448.0
FP4_MAX = 6.0
GSC = FP8_MAX * FP4_MAX
EXP_BIAS = -8.0

ENGINES = ("sync", "tensor", "vector", "scalar", "gpsimd")


class Script:

    def __init__(self):
        self.ops = {e: [] for e in ENGINES}
        self.counts = {}

    def emit(self, eng, fn, wait=(), inc=None, inc_by=1, drain=False):
        self.ops[eng].append((fn, tuple(wait), inc, inc_by, drain))
        if inc is not None:
            self.counts[inc] = self.counts.get(inc, 0) + inc_by
            return self.counts[inc]
        return None

    def c(self, sem):
        return self.counts.get(sem, 0)

    def barrier(self):
        snap = tuple((s, v) for s, v in self.counts.items() if v > 0)
        for e in ENGINES:
            self.emit(e, None, wait=snap)


def build_nc(seq=4096, drains=False, dbg=False):
    SEQ = seq
    OWN = SEQ // 2
    NT = SEQ // 128
    NTO = OWN // 128
    D = 1024
    CC = 8
    NP = 8
    I = 4096
    NI = I // 128
    NISL = I // 512
    CHT = min(2, NTO)
    NCH = NTO // CHT
    NQT = max(OWN // 512, 1)
    QTW = min(512, OWN)
    GRP = 4 if NT % 4 == 0 else 1

    nc = bass.Bass("TRN2", target_bir_lowering=False)

    def din(name, shape, dt=F32):
        return nc.dram_tensor(name, shape, dt, kind="ExternalInput").ap()

    x_full = din("x_full", [SEQ, D])
    wqT = din("wqT", [D, D]); wkT = din("wkT", [D, D]); wvT = din("wvT", [D, D])
    bias_b = din("bias_b", [128, 3 * D])
    gate_w = din("gate_w", [I, D]); up_w = din("up_w", [I, D]); down_w = din("down_w", [D, I])
    s_in_b = din("s_in_b", [128, 1]); s_ind_b = din("s_ind_b", [128, 1])
    lepe_w = din("lepe_w", [128, NP * 3]); lepe_b = din("lepe_b", [128, NP])
    sin_t = din("sin_t", [128, NT * 32]); cos_t = din("cos_t", [128, NT * 32])
    masks = din("masks", [128, 2])
    ident_f = din("ident_f", [128, 128]); ident_b = din("ident_b", [128, 128], BF16)
    ones_in = din("ones_in", [128, 128])

    out_t = nc.dram_tensor("out_s", [OWN, D], F32, kind="ExternalOutput").ap()

    KIND = "ExternalOutput" if dbg else "Internal"
    wgq_d = nc.dram_tensor("wgq_d", [128, CC * I], BF16, kind=KIND).ap()
    wuq_d = nc.dram_tensor("wuq_d", [128, CC * I], BF16, kind=KIND).ap()
    wdq_d = nc.dram_tensor("wdq_d", [128, NI * D], BF16, kind=KIND).ap()
    xqT_d = nc.dram_tensor("xqT_d", [128, CC * OWN], BF16, kind=KIND).ap()
    xnT_d = nc.dram_tensor("xnT_d", [128, CC * SEQ], F32, kind="Internal").ap()
    s_d = nc.dram_tensor("s_d", [OWN, D], F32, kind=KIND).ap()
    g_d = nc.dram_tensor("g_d", [OWN, D], F32, kind=KIND).ap()
    if dbg:
        dbg_xnt = nc.dram_tensor("dbg_xnt", [128, CC * SEQ], F32, kind="ExternalOutput").ap()

    ctx = ExitStack()
    with ctx:
        def sbuf(name, shape, dt=F32):
            return ctx.enter_context(nc.sbuf_tensor(name, shape, dt))

        sinc = sbuf("sinc", [128, NT * 32]); cosc = sbuf("cosc", [128, NT * 32])
        lepw = sbuf("lepw", [128, NP * 3]); lepb = sbuf("lepb", [128, NP])
        maskc = sbuf("maskc", [128, 2])
        idf = sbuf("idf", [128, 128]); idb = sbuf("idb", [128, 128], BF16)
        onesc = sbuf("onesc", [128, 128])
        scal = sbuf("scal", [128, 32])
        expbuf = sbuf("expbuf", [128, 1024])
        biasg = sbuf("biasg", [128, 384])
        rcpr = sbuf("rcpr", [128, 512])

        rem = nc.sbuf_bytes_remaining
        acols = (rem // 4 // 128) * 128 - 128
        arena = sbuf("arena", [128, acols])
        arena_b = arena[:].bitcast(BF16)

        def psum(name, dt=F32, cols=512):
            return ctx.enter_context(nc.psum_tensor(name, [128, cols], dt))
        psS0 = psum("psS0"); psS1 = psum("psS1")
        psA0 = psum("psA0"); psA1 = psum("psA1")
        psP0 = psum("psP0"); psP1 = psum("psP1")
        psT = psum("psT")
        psB = psum("psB", BF16, 1024)

        sem_names = (["TE", "VE", "SE", "GE"] + [f"L{i}" for i in range(8)]
                     + [f"S{i}" for i in range(4)])
        sems = {s: ctx.enter_context(nc.semaphore(s)) for s in sem_names}

        S = Script()

        rr = {"ld": 0, "st": 0}

        def ld(dst_ap, src_ap, wait=()):
            sem = f"L{rr['ld'] % 8}"
            rr["ld"] += 1
            w = tuple(wait)
            prev = S.c(sem)
            if prev:
                w = w + ((sem, prev),)
            v = S.emit("sync", lambda e, d=dst_ap, s=src_ap: e.dma_start(out=d, in_=s),
                       wait=w, inc=sem, inc_by=16)
            return (sem, v)

        def ld_all():
            return tuple((f"L{i}", S.c(f"L{i}")) for i in range(8) if S.c(f"L{i}"))

        def stq(dst_ap, src_ap, wait=()):
            sem = f"S{rr['st'] % 4}"
            rr["st"] += 1
            w = tuple(wait)
            prev = S.c(sem)
            if prev:
                w = w + ((sem, prev),)
            v = S.emit("gpsimd", lambda e, d=dst_ap, s=src_ap: e.dma_start(out=d, in_=s),
                       wait=w, inc=sem, inc_by=16)
            return (sem, v)

        def st_all():
            return tuple((f"S{i}", S.c(f"S{i}")) for i in range(4) if S.c(f"S{i}"))

        dr_v = drains in (True, "all")
        dr_s = drains in (True, "all", "scalar")

        def vop(fn, wait=(), inc="VE", drain=None):
            return S.emit("vector", fn, wait=wait, inc=inc,
                          drain=dr_v if drain is None else drain)

        def sop(fn, wait=(), inc="SE", drain=None):
            return S.emit("scalar", fn, wait=wait, inc=inc,
                          drain=dr_s if drain is None else drain)

        def top(fn, wait=(), inc="TE"):
            return S.emit("tensor", fn, wait=wait, inc=inc)

        def gop(fn, wait=(), inc="GE", drain=None):
            return S.emit("gpsimd", fn, wait=wait, inc=inc,
                          drain=dr_v if drain is None else drain)

        class Cursor:
            def __init__(self, off=0):
                self.off = off
            def f32(self, n):
                assert self.off + n <= acols, f"arena overflow: {self.off}+{n} > {acols}"
                ap = arena[:, self.off:self.off + n]
                self.off += n
                return ap
            def bf16(self, n):
                assert n % 2 == 0
                assert self.off + n // 2 <= acols, "arena overflow (bf16)"
                ap = arena_b[:, 2 * self.off:2 * self.off + n]
                self.off += n // 2
                return ap

        ld(sinc[:], sin_t[:]); ld(cosc[:], cos_t[:])
        ld(lepw[:], lepe_w[:]); ld(lepb[:], lepe_b[:])
        ld(maskc[:], masks[:]); ld(idf[:], ident_f[:]); ld(idb[:], ident_b[:])
        ld(onesc[:], ones_in[:])
        ld(scal[:, 0:1], s_in_b[:]); ld(scal[:, 1:2], s_ind_b[:])
        c0all = ld_all()
        vop(lambda e: e.tensor_scalar(scal[:, 2:3], scal[:, 0:1], 1.0 / 6.0, None, ALU.mult),
            wait=c0all)
        vop(lambda e: e.tensor_scalar(scal[:, 3:4], scal[:, 0:1], 2.0, None, ALU.mult))
        vop(lambda e: e.tensor_scalar(scal[:, 4:5], scal[:, 1:2], 1.0 / 6.0, None, ALU.mult))
        vop(lambda e: e.tensor_scalar(scal[:, 5:6], scal[:, 1:2], 2.0, None, ALU.mult))
        vop(lambda e: e.memset(scal[:, 18:19], 1e-5))
        vop(lambda e: e.memset(scal[:, 19:20], EXP_BIAS))
        c_consts = ld_all()
        for _e in ("tensor", "scalar", "gpsimd", "vector"):
            S.emit(_e, None, wait=c_consts)

        def quant_block_chain(cur, am, gs6_ap, gs2_ap, nb):
            r2 = cur.f32(nb); s8h = cur.f32(nb)
            sc = cur.f32(nb); p = cur.f32(nb); pinv = cur.f32(nb)
            t1 = cur.f32(nb); t2 = cur.f32(nb); fl = cur.f32(nb)
            vop(lambda e: e.tensor_scalar(sc, am, gs6_ap, 448.0, ALU.mult, ALU.min))
            vop(lambda e: e.tensor_scalar(t1, sc, 4.0, None, ALU.is_ge))
            vop(lambda e: e.tensor_scalar(p, t1, 255.0, 1.0, ALU.mult, ALU.add))
            vop(lambda e: e.tensor_scalar(pinv, t1, (2.0 ** -8) - 1.0, 1.0, ALU.mult, ALU.add))
            for mul, thr in ((15.0, 0.25), (3.0, 2.0 ** -4), (1.0, 2.0 ** -5)):
                vop(lambda e, th=thr: e.tensor_scalar(t2, p, th, None, ALU.mult))
                vop(lambda e: e.tensor_tensor(t1, sc, t2, ALU.is_ge))
                vop(lambda e, m=mul: e.tensor_scalar(t2, t1, m, 1.0, ALU.mult, ALU.add))
                vop(lambda e: e.tensor_tensor(p, p, t2, ALU.mult))
                vop(lambda e, m=mul: e.tensor_scalar(t2, t1, 1.0 / (m + 1.0) - 1.0, 1.0,
                                                     ALU.mult, ALU.add))
                vop(lambda e: e.tensor_tensor(pinv, pinv, t2, ALU.mult))
            vop(lambda e: e.scalar_tensor_tensor(t1, sc, 512.0, pinv, ALU.mult, ALU.mult))
            vop(lambda e: e.tensor_scalar(t1, t1, 0.5, None, ALU.add))
            vop(lambda e: e.tensor_scalar(t2, t1, 16.0, 16.0, ALU.is_ge, ALU.mult))
            vop(lambda e: e.tensor_tensor(t1, t1, t2, ALU.subtract))
            vop(lambda e: e.tensor_copy(fl, t2))
            vop(lambda e: e.tensor_scalar(t2, t1, 8.0, 8.0, ALU.is_ge, ALU.mult))
            vop(lambda e: e.tensor_tensor(t1, t1, t2, ALU.subtract))
            vop(lambda e: e.tensor_tensor(fl, fl, t2, ALU.add))
            vop(lambda e: e.tensor_scalar(t2, t1, 4.0, 4.0, ALU.is_ge, ALU.mult))
            vop(lambda e: e.tensor_tensor(t1, t1, t2, ALU.subtract))
            vop(lambda e: e.tensor_tensor(fl, fl, t2, ALU.add))
            vop(lambda e: e.tensor_scalar(t2, t1, 2.0, 2.0, ALU.is_ge, ALU.mult))
            vop(lambda e: e.tensor_tensor(t1, t1, t2, ALU.subtract))
            vop(lambda e: e.tensor_tensor(fl, fl, t2, ALU.add))
            vop(lambda e: e.scalar_tensor_tensor(fl, t1, 1.0, fl, ALU.is_ge, ALU.add))
            vop(lambda e: e.scalar_tensor_tensor(sc, fl, 2.0 ** -9, p, ALU.mult, ALU.mult))
            vop(lambda e: e.tensor_scalar(s8h, sc, 0.5, None, ALU.mult))
            vop(lambda e: e.tensor_single_scalar(t1, sc, 0.0, ALU.is_equal))
            vop(lambda e: e.tensor_tensor(t2, sc, t1, ALU.max))
            vop(lambda e: e.reciprocal(t2, t2))
            vop(lambda e: e.tensor_scalar(t2, t2, gs2_ap, None, ALU.mult))
            vop(lambda e: e.tensor_scalar(fl, t1, -1.0, 1.0, ALU.mult, ALU.add))
            vop(lambda e: e.tensor_tensor(t2, t2, fl, ALU.mult))
            c = vop(lambda e: e.scalar_tensor_tensor(r2, t1, 2.0, t2, ALU.mult, ALU.add))
            return r2, s8h, c

        def quant_elem_chain(cur, w_ap, sgn_ap, r2, s8h, out_ap, ncols, nb):
            u = cur.f32(ncols); A = cur.f32(ncols); Bv = cur.f32(ncols)
            t = cur.f32(ncols); h = cur.f32(ncols); c = cur.f32(ncols)
            r2b = r2.unsqueeze(2).broadcast_to([128, nb, 16])
            s8hb = s8h.unsqueeze(2).broadcast_to([128, nb, 16])
            u3 = u.rearrange("p (g b) -> p g b", b=16)
            vop(lambda e: e.scalar_tensor_tensor(u, w_ap, -1.0, w_ap, ALU.mult, ALU.max))
            vop(lambda e: e.tensor_tensor(u3, u3, r2b, ALU.mult))
            vop(lambda e: e.tensor_scalar(u, u, 12.0, None, ALU.min))
            vop(lambda e: e.tensor_scalar(A, u, 4.0, 0.5, ALU.is_ge, ALU.mult))
            vop(lambda e: e.tensor_scalar(Bv, u, 8.0, 0.25, ALU.is_ge, ALU.mult))
            vop(lambda e: e.tensor_tensor(t, A, Bv, ALU.add))
            vop(lambda e: e.scalar_tensor_tensor(h, Bv, 3.0, t, ALU.mult, ALU.add))
            vop(lambda e: e.tensor_scalar(t, t, -1.0, 1.0, ALU.mult, ALU.add))
            vop(lambda e: e.tensor_scalar(A, h, 2.0, 1.0, ALU.mult, ALU.add))
            vop(lambda e: e.scalar_tensor_tensor(h, u, 0.5, h, ALU.add, ALU.add))
            vop(lambda e: e.tensor_tensor(h, h, t, ALU.mult))
            vop(lambda e: e.tensor_scalar(c, h, 1.0, None, ALU.is_ge))
            vop(lambda e: e.scalar_tensor_tensor(c, h, 2.0, c, ALU.is_ge, ALU.add))
            vop(lambda e: e.scalar_tensor_tensor(c, h, 3.0, c, ALU.is_ge, ALU.add))
            vop(lambda e: e.scalar_tensor_tensor(c, h, 4.0, c, ALU.is_ge, ALU.add))
            vop(lambda e: e.tensor_tensor(c, c, A, ALU.mult))
            vop(lambda e: e.tensor_tensor(c, c, sgn_ap, ALU.mult))
            o3 = out_ap.rearrange("p (g b) -> p g b", b=16)
            c3 = c.rearrange("p (g b) -> p g b", b=16)
            return vop(lambda e: e.tensor_tensor(o3, c3, s8hb, ALU.mult))

        def quant_tile(chain_base, blk_ap, src_ap, sgn_ap, out_ap, C, gs6_ap, gs2_ap):
            nb = C // 16
            vop(lambda e: e.reduce_max(
                blk_ap[:, 0:nb], src_ap[:, 0:C].rearrange("p (g k) -> p g k", k=16),
                axis=AX.X, apply_absolute_value=True))
            curq = Cursor(chain_base)
            r2, s8h, _ = quant_block_chain(curq, blk_ap[:, 0:nb], gs6_ap, gs2_ap, nb)
            elem_base = chain_base + 2 * nb
            cq = None
            for sub in range(C // 1024):
                sl = slice(sub * 1024, (sub + 1) * 1024)
                bsl = slice(sub * 64, (sub + 1) * 64)
                cq = quant_elem_chain(Cursor(elem_base),
                                      src_ap[:, sl], sgn_ap[:, sl],
                                      r2[:, bsl], s8h[:, bsl],
                                      out_ap[:, sl], 1024, 64)
            return cq

        def cross_partition_max(src_col, dst_col, tmp_row):
            ct = top(lambda e: e.transpose(psT[0:1, 0:128], src_col, idf[:]),
                     wait=(("VE", S.c("VE")), ("SE", S.c("SE"))))
            cv = vop(lambda e: e.tensor_copy(tmp_row[0:1, 0:128], psT[0:1, 0:128]),
                     wait=(("TE", ct),))
            cv = vop(lambda e: e.reduce_max(tmp_row[0:1, 0:1], tmp_row[0:1, 0:128], axis=AX.X))
            ct = top(lambda e: e.matmul(psT[:, 0:1], onesc[0:1, :], tmp_row[0:1, 0:1],
                                        start=True, stop=True), wait=(("VE", cv),))
            return vop(lambda e: e.tensor_copy(dst_col, psT[:, 0:1]), wait=(("TE", ct),))

        P0_NEED = 4096 + 4096 + 2048 + 2048 + 512 + 2 * 256 + 6 * 1024
        p0_cur = Cursor(acols - P0_NEED)
        wt_a = p0_cur.f32(4096)
        sgn0 = p0_cur.f32(4096)
        wq_a = p0_cur.bf16(4096)
        stage0 = p0_cur.bf16(4096)
        blk0 = p0_cur.f32(512)
        chain_base0 = p0_cur.off

        wcfg = [
            (gate_w, wgq_d, I, D, 6, 9, 15, 12, 0),
            (up_w, wuq_d, I, D, 7, 10, 16, 13, 0),
            (down_w, wdq_d, D, I, 8, 11, 17, 14, 1),
        ]
        p0_state = {"last_sp": None, "last_q": None, "last_ct": None,
                    "pending": None, "widx": 0}

        def emit_p0_passA():
            for (src_, dst, rows, C, sl6, sl2, slm, slr, sli) in wcfg:
                nrt = rows // 128
                gmax = scal[:, slm:slm + 1]
                last_v = S.c("VE")
                for ot in range(nrt):
                    cl = ld(wt_a[:, 0:C], src_[ot * 128:(ot + 1) * 128, :],
                            wait=(("VE", last_v),))
                    if ot == 0:
                        last_v = vop(lambda e, cc=C, g_=gmax: e.reduce_max(
                            g_, wt_a[:, 0:cc], axis=AX.X, apply_absolute_value=True),
                            wait=(cl,))
                    else:
                        vop(lambda e, cc=C: e.reduce_max(
                            blk0[:, 0:1], wt_a[:, 0:cc], axis=AX.X,
                            apply_absolute_value=True), wait=(cl,))
                        last_v = vop(lambda e, g_=gmax: e.tensor_tensor(
                            g_, g_, blk0[:, 0:1], ALU.max))
                cross_partition_max(gmax, gmax, rcpr)
                vop(lambda e, g_=gmax: e.reciprocal(blk0[:, 0:1], g_))
                vop(lambda e, a=sl6: e.tensor_scalar(scal[:, a:a + 1], blk0[:, 0:1],
                                                     GSC / 6.0, None, ALU.mult))
                vop(lambda e, a=sl2: e.tensor_scalar(scal[:, a:a + 1], blk0[:, 0:1],
                                                     GSC * 2.0, None, ALU.mult))
                vop(lambda e, s=sli: e.reciprocal(blk0[:, 1:2], scal[:, s:s + 1]))
                vop(lambda e, r=slr, g_=gmax: e.scalar_tensor_tensor(
                    scal[:, r:r + 1], g_, 1.0 / GSC, blk0[:, 1:2], ALU.mult, ALU.mult))

        p0_work = []
        for wi, (src_, dst, rows, C, sl6, sl2, slm, slr, sli) in enumerate(wcfg):
            for ot in range(rows // 128):
                p0_work.append((src_, dst, C, sl6, sl2, ot))

        def emit_p0_quant(src_, dst, C, sl6, sl2, ot):
            last_sp = p0_state["last_sp"]
            wsp0 = (last_sp,) if last_sp else ()
            wq = (("VE", p0_state["last_q"]),) if p0_state["last_q"] else (("VE", S.c("VE")),)
            cl = ld(wt_a[:, 0:C], src_[ot * 128:(ot + 1) * 128, :], wait=wq + wsp0)
            csg = sop(lambda e, cc=C: e.sign(sgn0[:, 0:cc], wt_a[:, 0:cc]), wait=(cl,))
            wtr = (("TE", p0_state["last_ct"]),) if p0_state["last_ct"] else ()
            S.emit("vector", None, wait=(("SE", csg), cl) + wtr)
            cq = quant_tile(chain_base0, blk0, wt_a, sgn0, wq_a, C,
                            scal[:, sl6:sl6 + 1], scal[:, sl2:sl2 + 1])
            p0_state["last_q"] = cq
            S.emit("scalar", None, wait=(("VE", cq),))
            p0_state["pending"] = (dst, C, ot, cq)

        def emit_p0_flush():
            if p0_state["pending"] is None:
                return
            dst, C, ot, cq = p0_state["pending"]
            p0_state["pending"] = None
            last_sp = p0_state["last_sp"]
            wsp0 = (last_sp,) if last_sp else ()
            ncc = C // 128
            ct = None
            for q in range(ncc):
                hh = q % 2
                ct = top(lambda e, qq=q, h_=hh: e.transpose(
                    psB[:, h_ * 128:(h_ + 1) * 128],
                    wq_a[:, qq * 128:(qq + 1) * 128], idb[:]),
                    wait=(("VE", cq), ("SE", S.c("SE"))) + wsp0)
                sop(lambda e, qq=q, h_=hh: e.copy(
                    stage0[:, qq * 128:(qq + 1) * 128],
                    psB[:, h_ * 128:(h_ + 1) * 128]), wait=(("TE", ct),))
            p0_state["last_ct"] = ct
            if C == 1024:
                dv4 = dst.rearrange("p (il q r) -> p il q r", il=NISL, q=ncc, r=512)
                dview = dv4[:, ot // 4, :, (ot % 4) * 128:(ot % 4 + 1) * 128]
            else:
                dview = dst.rearrange("p (db ic r) -> p db ic r",
                                      db=8, ic=NI, r=128)[:, ot, :, :]
            sview = stage0[:, 0:ncc * 128].rearrange("p (q r) -> p q r", q=ncc)
            p0_state["last_sp"] = stq(dview, sview, wait=(("SE", S.c("SE")),))

        def weave_slot():
            emit_p0_flush()
            if p0_state["widx"] < len(p0_work):
                emit_p0_quant(*p0_work[p0_state["widx"]])
                p0_state["widx"] += 1

        S.barrier()

        cur1 = Cursor()
        p3_base = 0
        xnstage = cur1.f32(1024)
        xt_a = cur1.f32(1024); xt_b = cur1.f32(1024)
        xn_t = cur1.f32(1024); sq_t = cur1.f32(1024)
        mv = cur1.f32(4)
        xq_t = cur1.bf16(1024)
        xqstage = cur1.bf16(1024)
        sgn1 = cur1.f32(1024)
        blk1 = cur1.f32(64)
        chain1 = cur1.off

        xnT3d = xnT_d.rearrange("p (q s) -> p q s", q=CC)
        xnst3 = xnstage.rearrange("p (q s) -> p q s", q=CC)
        prev_done = S.c("VE")
        prev_sp = None
        prev_sp_xn = None
        xn_guard = ()
        for tt in range(NT):
            buf = xt_a if tt % 2 == 0 else xt_b
            cl = ld(buf, x_full[tt * 128:(tt + 1) * 128, :],
                    wait=(("VE", prev_done),) + ((prev_sp,) if prev_sp else ()))
            vop(lambda e, b=buf: e.reduce_sum(mv[:, 0:1], b, axis=AX.X), wait=(cl,))
            vop(lambda e: e.tensor_scalar(mv[:, 0:1], mv[:, 0:1], 1.0 / D, None, ALU.mult))
            cve = vop(lambda e, b=buf: e.tensor_scalar(xn_t, b, mv[:, 0:1], None, ALU.subtract),
                       wait=xn_guard)
            sop(lambda e: e.activation(sq_t, xn_t, ACT.Square, bias=0.0, scale=1.0,
                                       accum_out=mv[:, 2:3]), wait=(("VE", cve),))
            cs2 = sop(lambda e: e.activation(mv[:, 3:4], mv[:, 2:3], ACT.Sqrt,
                                             bias=scal[:, 18:19], scale=1.0 / D))
            vop(lambda e: e.reciprocal(mv[:, 1:2], mv[:, 3:4]), wait=(("SE", cs2),))
            cxn = vop(lambda e: e.tensor_scalar(xn_t, xn_t, mv[:, 1:2], None, ALU.mult))
            for q in range(CC):
                wtr = [("VE", cxn), ("SE", S.c("SE"))]
                if q == 0 and prev_sp_xn:
                    wtr.append(prev_sp_xn)
                ct = top(lambda e, qq=q: e.transpose(
                    psT[:, 0:128], xn_t[:, qq * 128:(qq + 1) * 128], idf[:]),
                    wait=tuple(wtr))
                sop(lambda e, qq=q: e.copy(xnst3[:, qq, :], psT[:, 0:128]),
                    wait=(("TE", ct),))
            prev_sp_xn = stq(xnT3d[:, :, tt * 128:(tt + 1) * 128], xnst3,
                             wait=(("SE", S.c("SE")),))
            prev_done = cxn
            if tt < NTO:
                csg = sop(lambda e: e.sign(sgn1, xn_t), wait=(("VE", cxn),))
                S.emit("vector", None, wait=(("SE", csg),))
                cq = quant_tile(chain1, blk1, xn_t, sgn1, xq_t, 1024,
                                scal[:, 2:3], scal[:, 3:4])
                S.emit("scalar", None, wait=(("VE", cq),))
                for q in range(CC):
                    hh = q % 2
                    ct = top(lambda e, qq=q, h_=hh: e.transpose(
                        psB[:, h_ * 128:(h_ + 1) * 128],
                        xq_t[:, qq * 128:(qq + 1) * 128], idb[:]),
                        wait=(("VE", cq), ("SE", S.c("SE"))) + ((prev_sp,) if prev_sp else ()))
                    sop(lambda e, qq=q, h_=hh: e.copy(
                        xqstage[:, qq * 128:(qq + 1) * 128],
                        psB[:, h_ * 128:(h_ + 1) * 128]), wait=(("TE", ct),))
                dv = xqT_d.rearrange("p (q s) -> p q s", q=CC)[:, :, tt * 128:(tt + 1) * 128]
                sv = xqstage.rearrange("p (q s) -> p q s", q=CC)
                prev_sp = stq(dv, sv, wait=(("SE", S.c("SE")),))
                prev_done = cq
            xn_guard = (("TE", S.c("TE")), ("SE", S.c("SE")))

        S.barrier()
        if dbg:
            stq(dbg_xnt[:], xnT)
            S.barrier()

        cur3 = Cursor(p3_base)
        xnb_a = cur3.f32(CC * GRP * 128); xnb_b = cur3.f32(CC * GRP * 128)
        KS1 = cur3.bf16(SEQ); KS2 = cur3.bf16(SEQ)
        QS1 = cur3.bf16(OWN); QS2 = cur3.bf16(OWN)
        QW1 = cur3.bf16(OWN); QW2 = cur3.bf16(OWN)
        VH1 = cur3.bf16(NT * 65); VL1 = cur3.bf16(NT * 65)
        VH2 = cur3.bf16(NT * 65); VL2 = cur3.bf16(NT * 65)
        phr = cur3.f32(1024)
        PhiB = phr[:, 0:512].bitcast(BF16)
        PloB = phr[:, 512:1024].bitcast(BF16)
        vtf = cur3.f32(OWN + 2)
        kvg_off = cur3.off
        kvg = cur3.f32(GRP * 256)
        ket = cur3.f32(GRP * 128)
        rtw = cur3.f32(GRP * 128)
        rt1 = rtw[:, 0:GRP * 64]; rt2 = rtw[:, GRP * 64:GRP * 128]
        sstg = rcpr[:, 0:128]
        assert cur3.off <= acols - P0_NEED, f"P3 region {cur3.off} overlaps P0 at {acols - P0_NEED}"
        assert GRP * 256 + GRP * 128 + 2 * GRP * 64 >= OWN, "sfm alias too small"
        sfm = arena[:, kvg_off:kvg_off + OWN]

        def kv_slot(q):
            if q < 4:
                return expbuf[:, q * 256:(q + 1) * 256]
            if q < 6:
                return rcpr[:, (q - 4) * 256:(q - 3) * 256]
            return phr[:, (q - 6) * 256:(q - 5) * 256]

        def wq_slot(q):
            return expbuf[:, q * 128:(q + 1) * 128]

        def proc_group(tiles, is_q):
            n = len(tiles)
            if is_q:
                src = kvg.rearrange("p (t w) -> p t w", t=GRP)[:, 0:n, 0:128]
                bslc = biasg[:, 0:128]
            else:
                src = kvg.rearrange("p (t w) -> p t w", t=GRP)[:, 0:n, 0:128]
                bslc = biasg[:, 128:256]
            ketv = ket.rearrange("p (t w) -> p t w", t=GRP)[:, 0:n, :]
            bb = bslc.unsqueeze(1).broadcast_to([128, n, 128])
            vop(lambda e: e.tensor_tensor(ketv, src, bb, ALU.add))
            kkv = kvg.rearrange("p (t w) -> p t w", t=GRP)[:, 0:n, 0:128]
            rtv = rtw[:, 0:n * 128]
            ketf = ket[:, 0:n * 128]
            ket3 = ket.rearrange("p (t w) -> p t w", t=GRP)[:, 0:n, :]
            cv = vop(lambda e: e.tensor_scalar(kkv, ket3, 0.0, None, ALU.min))
            cs = sop(lambda e: e.activation(rtv, kkv, ACT.Exp, bias=0.0, scale=1.0),
                     wait=(("VE", cv),))
            vop(lambda e: e.tensor_scalar(kkv, ket3, 0.0, None, ALU.max),
                wait=(("SE", cs),))
            rtv3 = rtv.rearrange("p (t w) -> p t w", t=n)
            vop(lambda e: e.tensor_tensor(ket3, kkv, rtv3, ALU.add))
            t0 = tiles[0]
            ket5 = ket.rearrange("p (t h j u) -> p t h j u", h=2, j=32, u=2)
            x1 = ket5[:, 0:n, :, :, 0]
            x2 = ket5[:, 0:n, :, :, 1]
            sv = sinc[:, t0 * 32:(t0 + n) * 32].rearrange("p (t j) -> p t j", t=n)
            svb = sv.unsqueeze(2).broadcast_to([128, n, 2, 32])
            cvv = cosc[:, t0 * 32:(t0 + n) * 32].rearrange("p (t j) -> p t j", t=n)
            cvb = cvv.unsqueeze(2).broadcast_to([128, n, 2, 32])
            r1v3 = rt1.rearrange("p (t h j) -> p t h j", t=GRP, h=2)[:, 0:n]
            r2v3 = rt2.rearrange("p (t h j) -> p t h j", t=GRP, h=2)[:, 0:n]
            dd = kvg.rearrange("p (t q) -> p t q", t=GRP)[:, 0:n, 0:128]
            de = dd.rearrange("p t (h u j) -> p t h u j", h=2, u=2, j=32)[:, :, :, 0, :]
            do = dd.rearrange("p t (h u j) -> p t h u j", h=2, u=2, j=32)[:, :, :, 1, :]
            vop(lambda e: e.tensor_tensor(r1v3, x1, cvb, ALU.mult))
            vop(lambda e: e.tensor_tensor(r2v3, x2, svb, ALU.mult))
            vop(lambda e: e.tensor_tensor(de, r1v3, r2v3, ALU.subtract))
            vop(lambda e: e.tensor_tensor(r1v3, x1, svb, ALU.mult))
            vop(lambda e: e.tensor_tensor(r2v3, x2, cvb, ALU.mult))
            cfin = vop(lambda e: e.tensor_tensor(do, r1v3, r2v3, ALU.add))
            stg = (rt1.bitcast(BF16).rearrange("p (t w) -> p t w", t=GRP)[:, 0:n, 0:128],
                   rt2.bitcast(BF16).rearrange("p (t w) -> p t w", t=GRP)[:, 0:n, 0:128])
            kr4 = kvg.rearrange("p (t w) -> p t w", t=GRP)[:, 0:n, 0:128]

            def tr_into(sg, dstT, tiles, dep):
                for i, tt in enumerate(tiles):
                    hh = i % 2
                    ct = top(lambda e, ii=i, s_=sg, h_=hh: e.transpose(
                        psB[:, h_ * 128:(h_ + 1) * 128], s_[:, ii, :], idb[:]),
                        wait=(("VE", dep), ("SE", S.c("SE"))))
                    sop(lambda e, t=tt, d=dstT, h_=hh: e.copy(
                        d[:, t * 128:(t + 1) * 128],
                        psB[:, h_ * 128:(h_ + 1) * 128]), wait=(("TE", ct),))

            for h in range(2):
                sg = stg[0]
                hs = slice(h * 64, (h + 1) * 64)
                vop(lambda e, s_=sg, k_=hs: e.tensor_copy(s_[:, :, 0:64], kr4[:, :, k_]),
                    wait=(("TE", S.c("TE")),))
                cfin2 = vop(lambda e, s_=sg, k_=hs: e.tensor_tensor(
                    s_[:, :, 64:128], kr4[:, :, k_], s_[:, :, 0:64], ALU.subtract))
                dstT = (QS1 if h == 0 else QS2) if is_q else (KS1 if h == 0 else KS2)
                tr_into(sg, dstT, tiles, cfin2)
                if is_q:
                    sw = stg[1]
                    vop(lambda e, a=sw, b=sg: e.tensor_copy(a[:, :, 0:64], b[:, :, 64:128]),
                        wait=(("TE", S.c("TE")),))
                    cfin3 = vop(lambda e, a=sw, b=sg: e.tensor_copy(a[:, :, 64:128],
                                                                    b[:, :, 0:64]))
                    tr_into(sw, QW1 if h == 0 else QW2, tiles, cfin3)
            return S.c("SE")

        emit_p0_passA()
        p0_per_pair = (len(p0_work) + NP - 1) // NP
        for g in range(NP):
            pb_te = S.c("TE"); pb_ve = S.c("VE"); pb_se = S.c("SE")
            for wi, wsrc in enumerate((wkT, wvT)):
                for q in range(CC):
                    ld(kv_slot(q)[:, wi * 128:(wi + 1) * 128],
                       wsrc[q * 128:(q + 1) * 128, g * 128:(g + 1) * 128],
                       wait=(("TE", pb_te), ("VE", pb_ve), ("SE", pb_se)) + st_all())
            ld(biasg[:, 0:128], bias_b[:, g * 128:(g + 1) * 128])
            ld(biasg[:, 128:256], bias_b[:, D + g * 128:D + (g + 1) * 128])
            cbl = ld(biasg[:, 256:384], bias_b[:, 2 * D + g * 128:2 * D + (g + 1) * 128])
            wld = ld_all()
            cv1 = vop(lambda e: e.memset(VH1, 1.0), wait=(("TE", pb_te),))
            vop(lambda e: e.memset(VH2, 1.0))
            vop(lambda e: e.memset(VL1, 0.0))
            cv2 = vop(lambda e: e.memset(VL2, 0.0))

            pbanks = (psP0, psP1, psS0, psS1)
            pb_evac = {0: ("VE", pb_ve), 1: ("VE", pb_ve),
                       2: ("VE", pb_ve), 3: ("VE", pb_ve)}
            xnb_use = {0: 0, 1: 0}
            for gi, g0 in enumerate(range(0, NT, GRP)):
                gts = list(range(g0, min(g0 + GRP, NT)))
                xnb = xnb_a if gi % 2 == 0 else xnb_b
                xnb3 = xnb.rearrange("p (q s) -> p q s", q=CC)
                cxl = ld(xnb3[:, :, 0:len(gts) * 128],
                         xnT3d[:, :, g0 * 128:(g0 + len(gts)) * 128],
                         wait=(("TE", xnb_use[gi % 2]),) + ((prev_sp_xn,) if gi == 0 else ()))
                for i, tt in enumerate(gts):
                    bi = i % 4
                    pp = pbanks[bi]
                    wt = list(wld) + [pb_evac[bi], cxl]
                    ctev = None
                    for q in range(CC):
                        ctev = top(lambda e, qq=q, ii=i, p_=pp, x_=xnb3: e.matmul(
                            p_[:, 0:256], x_[:, qq, ii * 128:(ii + 1) * 128],
                            kv_slot(qq), start=(qq == 0), stop=(qq == CC - 1)),
                            wait=tuple(wt) if q == 0 else (),
                            inc="TE" if q == CC - 1 else None)
                    pb_evac[bi] = ("VE", vop(lambda e, ii=i, p_=pp: e.tensor_copy(
                        kvg[:, ii * 256:(ii + 1) * 256], p_[:, 0:256]),
                        wait=(("TE", S.c("TE")),)))
                xnb_use[gi % 2] = S.c("TE")
                ng = len(gts)
                kv4 = kvg.rearrange("p (t w) -> p t w", t=GRP)[:, 0:ng, :]
                proc_group(gts, is_q=False)
                ketv = ket.rearrange("p (t w) -> p t w", t=GRP)[:, 0:ng, :]
                bvp = biasg[:, 256:384].unsqueeze(1).broadcast_to([128, ng, 128])
                vop(lambda e: e.tensor_tensor(ketv, kv4[:, :, 128:256], bvp, ALU.add),
                    wait=(("TE", S.c("TE")), ("SE", S.c("SE"))))
                for h, (VH, VL) in enumerate(((VH1, VL1), (VH2, VL2))):
                    hd = slice(h * 64, (h + 1) * 64)
                    vhd = VH.rearrange("p (t w) -> p t w", t=NT)[:, g0:g0 + ng, 0:64]
                    vld = VL.rearrange("p (t w) -> p t w", t=NT)[:, g0:g0 + ng, 0:64]
                    vop(lambda e, d_=vhd, s_=hd: e.tensor_copy(d_, ketv[:, :, s_]))
                    vop(lambda e, d_=vld, s_=hd, h_=vhd: e.tensor_tensor(
                        d_, ketv[:, :, s_], h_, ALU.subtract))
                for i, tt in enumerate(gts):
                    if not (tt <= NTO or tt == NT - 1):
                        continue
                    ctv = top(lambda e, ii=i: e.transpose(
                        psT[:, 0:128], ketv[:, ii, :], idf[:]),
                        wait=(("VE", S.c("VE")), ("SE", S.c("SE"))))
                    if tt < NTO:
                        sop(lambda e, t=tt: e.copy(
                            vtf[:, 1 + t * 128:1 + (t + 1) * 128], psT[:, 0:128]),
                            wait=(("TE", ctv),))
                    elif tt == NTO:
                        sop(lambda e: e.copy(vtf[:, OWN + 1:OWN + 2], psT[:, 0:1]),
                            wait=(("TE", ctv),))
                    if tt == NT - 1:
                        sop(lambda e: e.copy(vtf[:, 0:1], psT[:, 127:128]),
                            wait=(("TE", ctv),))
            kse = S.c("SE")

            for q in range(CC):
                ld(wq_slot(q), wqT[q * 128:(q + 1) * 128, g * 128:(g + 1) * 128],
                   wait=(("TE", S.c("TE")), ("SE", S.c("SE"))))
            wldq = ld_all()
            for gi, g0 in enumerate(range(0, NTO, GRP)):
                gts = list(range(g0, min(g0 + GRP, NTO)))
                xnb = xnb_a if gi % 2 == 0 else xnb_b
                xnb3 = xnb.rearrange("p (q s) -> p q s", q=CC)
                cxl = ld(xnb3[:, :, 0:len(gts) * 128],
                         xnT3d[:, :, g0 * 128:(g0 + len(gts)) * 128],
                         wait=(("TE", xnb_use[gi % 2]),))
                for i, tt in enumerate(gts):
                    bi = i % 4
                    pp = pbanks[bi]
                    wt = list(wldq) + [pb_evac[bi], cxl]
                    ctev = None
                    for q in range(CC):
                        ctev = top(lambda e, qq=q, ii=i, p_=pp, x_=xnb3: e.matmul(
                            p_[:, 0:128], x_[:, qq, ii * 128:(ii + 1) * 128],
                            wq_slot(qq), start=(qq == 0), stop=(qq == CC - 1)),
                            wait=tuple(wt) if q == 0 else (),
                            inc="TE" if q == CC - 1 else None)
                    pb_evac[bi] = ("VE", vop(lambda e, ii=i, p_=pp: e.tensor_copy(
                        kvg[:, ii * 256:ii * 256 + 128], p_[:, 0:128]),
                        wait=(("TE", S.c("TE")),)))
                xnb_use[gi % 2] = S.c("TE")
                proc_group(gts, is_q=True)

            chalo = S.c("SE")
            vop(lambda e: e.tensor_scalar(vtf[:, 0:1], vtf[:, 0:1], maskc[:, 0:1],
                                          None, ALU.mult), wait=(("SE", chalo),))
            vop(lambda e: e.tensor_scalar(vtf[:, OWN + 1:OWN + 2], vtf[:, OWN + 1:OWN + 2],
                                          maskc[:, 1:2], None, ALU.mult))

            qkT_done = S.c("SE")
            prev_exp = [qkT_done, qkT_done]
            prev_av = [S.c("TE"), S.c("TE")]
            prev_plo = [S.c("VE"), S.c("VE")]
            psA_free = {0: S.c("VE"), 1: S.c("VE")}
            for h in range(2):
                psA = psA0 if h == 0 else psA1
                VH = VH1 if h == 0 else VH2
                VL = VL1 if h == 0 else VL2
                for qt in range(NQT):
                    exp_c = {}
                    pf_c = {}
                    av_c = {}
                    plo_c = {}
                    hr = slice(h * 64, (h + 1) * 64)

                    KS = KS1 if h == 0 else KS2
                    QS = QS1 if h == 0 else QS2
                    QW = QW1 if h == 0 else QW2

                    def emit_s(kb):
                        pss = psS0 if kb % 2 == 0 else psS1
                        w = [("SE", pf_c[kb - 2] if kb >= 2 else prev_exp[kb % 2])]
                        top(lambda e, k=kb, p_=pss, q_=qt, K_=KS, Q_=QS: e.matmul(
                            p_[:, 0:QTW], K_[:, k * 128:(k + 1) * 128],
                            Q_[:, q_ * QTW:(q_ + 1) * QTW],
                            start=True, stop=False), wait=tuple(w), inc=None)
                        cs_mm = top(lambda e, k=kb, p_=pss, q_=qt, K_=KS, Q_=QW: e.matmul(
                            p_[:, 0:QTW], K_[:, k * 128:(k + 1) * 128],
                            Q_[:, q_ * QTW:(q_ + 1) * QTW],
                            start=False, stop=True))
                        sl = slice((kb % 2) * 512, (kb % 2) * 512 + QTW)
                        w2b = [("TE", cs_mm),
                               ("TE", av_c[kb - 2] if kb >= 2 else prev_av[kb % 2])]
                        exp_c[kb] = sop(lambda e, p_=pss, s_=sl: e.activation(
                            PhiB[:, s_], p_[:, 0:QTW],
                            ACT.Exp, bias=scal[:, 19:20], scale=0.125), wait=tuple(w2b),
                            drain=False)
                        w2 = [("VE", plo_c[kb - 2] if kb >= 2 else prev_plo[kb % 2])]
                        pf_c[kb] = sop(lambda e, p_=pss, s_=sl: e.activation(
                            expbuf[:, s_], p_[:, 0:QTW],
                            ACT.Exp, bias=scal[:, 19:20], scale=0.125), wait=tuple(w2),
                            drain=False)
                        plo_c[kb] = vop(lambda e, s_=sl: e.tensor_tensor(
                            PloB[:, s_], expbuf[:, s_], PhiB[:, s_], ALU.subtract),
                            wait=(("SE", pf_c[kb]),), drain=False)

                    def emit_av(kb):
                        sl = slice((kb % 2) * 512, (kb % 2) * 512 + QTW)
                        wav = [("SE", exp_c[kb])]
                        if kb == 0:
                            wav.append(("VE", psA_free[h]))
                        top(lambda e, k=kb, pA=psA, V_=VH, s_=sl: e.matmul(
                            pA[0:65, 0:QTW], V_[:, k * 65:(k + 1) * 65],
                            PhiB[:, s_], start=(k == 0), stop=False),
                            wait=tuple(wav), inc=None)
                        top(lambda e, k=kb, pA=psA, V_=VL, s_=sl: e.matmul(
                            pA[0:65, 0:QTW], V_[:, k * 65:(k + 1) * 65],
                            PhiB[:, s_], start=False, stop=False), inc=None)
                        av_c[kb] = top(lambda e, k=kb, pA=psA, V_=VH, s_=sl: e.matmul(
                            pA[0:65, 0:QTW], V_[:, k * 65:(k + 1) * 65],
                            PloB[:, s_], start=False, stop=(k == NT - 1)),
                            wait=(("VE", plo_c[kb]),))

                    emit_s(0)
                    for kb in range(1, NT):
                        emit_s(kb)
                        emit_av(kb - 1)
                    emit_av(NT - 1)
                    prev_exp = [pf_c[NT - 2], pf_c[NT - 1]]
                    prev_av = [av_c[NT - 2], av_c[NT - 1]]
                    prev_plo = [plo_c[NT - 2], plo_c[NT - 1]]
                    crc = vop(lambda e, pA=psA: e.reciprocal(rcpr[0:1, 0:QTW], pA[64:65, 0:QTW]),
                              wait=(("TE", av_c[NT - 1]), ("VE", S.c("VE"))))
                    crx = top(lambda e: e.matmul(psT[0:64, 0:QTW], onesc[0:1, 0:64],
                                                 rcpr[0:1, 0:QTW], start=True, stop=True),
                              wait=(("VE", crc),))
                    sf = sfm[h * 64:(h + 1) * 64, qt * QTW:(qt + 1) * QTW]
                    cev = vop(lambda e, s=sf, pA=psA: e.tensor_copy(s, pA[0:64, 0:QTW]))
                    psA_free[h] = vop(
                        lambda e, s=sf: e.tensor_tensor(s, s, psT[0:64, 0:QTW], ALU.mult),
                        wait=(("TE", crx),))
                    s_idx = h * NQT + qt
                    wtarget = ((g * 2 * NQT + s_idx + 1) * len(p0_work)) // (NP * 2 * NQT)
                    while p0_state["widx"] < wtarget:
                        weave_slot()

            wl0 = lepw[:, g * 3:g * 3 + 1]; wl1 = lepw[:, g * 3 + 1:g * 3 + 2]
            wl2 = lepw[:, g * 3 + 2:g * 3 + 3]; bl = lepb[:, g:g + 1]
            sf_all = sfm[:, 0:OWN]
            vop(lambda e, w_=wl0: e.scalar_tensor_tensor(sf_all, vtf[:, 0:OWN], w_, sf_all,
                                                         ALU.mult, ALU.add),
                wait=(("TE", S.c("TE")),))
            vop(lambda e, w_=wl1: e.scalar_tensor_tensor(sf_all, vtf[:, 1:OWN + 1], w_, sf_all,
                                                         ALU.mult, ALU.add))
            vop(lambda e, w_=wl2: e.scalar_tensor_tensor(sf_all, vtf[:, 2:OWN + 2], w_, sf_all,
                                                         ALU.mult, ALU.add))
            cfin = vop(lambda e, b_=bl: e.tensor_scalar(sf_all, sf_all, b_, None, ALU.add))
            for tt in range(NTO):
                ct = top(lambda e, t=tt: e.transpose(
                    psT[:, 0:128], sfm[:, t * 128:(t + 1) * 128], idf[:]),
                    wait=(("VE", cfin), ("SE", S.c("SE"))))
                cse = sop(lambda e: e.copy(sstg[:, 0:128], psT[:, 0:128]), wait=(("TE", ct),))
                stq(s_d[tt * 128:(tt + 1) * 128, g * 128:(g + 1) * 128], sstg[:, 0:128],
                    wait=(("SE", cse),))
                S.emit("scalar", None, wait=st_all())


        while p0_state["widx"] < len(p0_work):
            weave_slot()
        emit_p0_flush()

        S.barrier()

        cur2 = Cursor()
        xqTs = cur2.bf16(CC * OWN)
        hid = cur2.f32(2 * I)
        hq_t = cur2.bf16(I)
        hqT = cur2.bf16(2 * NI * 128)
        gstg = cur2.f32(2 * 1024)
        wg_a = cur2.bf16(CC * 512); wg_b = cur2.bf16(CC * 512)
        wu_a = cur2.bf16(CC * 512); wu_b = cur2.bf16(CC * 512)
        wd_a = cur2.bf16(NI * 128); wd_b = cur2.bf16(NI * 128)
        sgn2 = cur2.f32(I)
        blk2 = cur2.f32(256)
        chain2 = cur2.off

        xq3 = xqTs.rearrange("p (q s) -> p q s", q=CC)
        cxq = ld(xqTs, xqT_d[:])
        hid3 = hid.rearrange("p (t w) -> p t w", t=2)
        hqT3 = hqT.rearrange("p (t ic w) -> p t ic w", t=2, ic=NI)
        g3 = gstg.rearrange("p (t w) -> p t w", t=2)
        NDSL = 8
        gu_pref = {}
        p2s = {"wbuf_mm": {0: None, 1: None}, "se_pP0": 0, "se_pP1": 0,
               "ge_hd": 0}
        wdbuf_mm = {0: None, 1: None}
        wd_pref = {}
        se_pS = {0: 0, 1: 0}
        q_done = {0: 0, 1: 0}
        sp_guard = {0: None, 1: None}

        def issue_gu_load(k, isl):
            par = (k * NISL + isl) % 2
            wg = wg_a if par == 0 else wg_b
            wu = wu_a if par == 0 else wu_b
            ww = (p2s["wbuf_mm"][par],) if p2s["wbuf_mm"][par] else ()
            clg = ld(wg, wgq_d[:, isl * CC * 512:(isl + 1) * CC * 512], wait=ww)
            clu = ld(wu, wuq_d[:, isl * CC * 512:(isl + 1) * CC * 512], wait=ww)
            gu_pref[(k, isl)] = (clg, clu, wg, wu)

        def issue_wd_load(k, csl):
            par = csl % 2
            wd = wd_a if par == 0 else wd_b
            ww = (wdbuf_mm[par],) if wdbuf_mm[par] else ()
            cld = ld(wd, wdq_d[:, csl * NI * 128:(csl + 1) * NI * 128], wait=ww)
            wd_pref[(k, csl)] = (cld, wd)

        issue_gu_load(0, 0)
        issue_wd_load(0, 0)
        issue_wd_load(0, 1)
        hd_last = {0: 0, 1: 0}

        def emit_gu(k):
            par = k % 2
            for isl in range(NISL):
                clg, clu, wg, wu = gu_pref.pop((k, isl))
                wg3 = wg.rearrange("p (q s) -> p q s", q=CC)
                wu3 = wu.rearrange("p (q s) -> p q s", q=CC)
                wtm = [clg, clu, cxq, ("SE", p2s["se_pP1"])]
                cg = None
                for q in range(CC):
                    cg = top(lambda e, qq=q, k_=k, w3=wg3: e.matmul(
                        psP0[:, 0:512], xq3[:, qq, k_ * 128:(k_ + 1) * 128],
                        w3[:, qq, :], start=(qq == 0), stop=(qq == CC - 1)),
                        wait=tuple(wtm) if q == 0 else (),
                        inc="TE" if q == CC - 1 else None)
                cu = None
                for q in range(CC):
                    cu = top(lambda e, qq=q, k_=k, w3=wu3: e.matmul(
                        psP1[:, 0:512], xq3[:, qq, k_ * 128:(k_ + 1) * 128],
                        w3[:, qq, :], start=(qq == 0), stop=(qq == CC - 1)),
                        inc="TE" if q == CC - 1 else None)
                wsl = [("TE", cg)]
                if isl == 0 and q_done[par]:
                    wsl.append(("VE", q_done[par]))
                csl = sop(lambda e, p_=par, i_=isl: e.activation(
                    hid3[:, p_, i_ * 512:(i_ + 1) * 512], psP0[:, 0:512],
                    ACT.Silu, bias=0.0, scale=scal[:, 12:13]), wait=tuple(wsl))
                p2s["se_pP0"] = csl
                cup = sop(lambda e: e.activation(rcpr[:, 0:512], psP1[:, 0:512],
                                                 ACT.Copy, bias=0.0,
                                                 scale=scal[:, 13:14]),
                          wait=(("TE", cu), ("GE", p2s["ge_hd"])))
                p2s["se_pP1"] = cup
                hd = gop(lambda e, p_=par, i_=isl: e.tensor_tensor(
                    hid3[:, p_, i_ * 512:(i_ + 1) * 512],
                    hid3[:, p_, i_ * 512:(i_ + 1) * 512], rcpr[:, 0:512], ALU.mult),
                    wait=(("SE", cup),))
                p2s["ge_hd"] = hd
                p2s["wbuf_mm"][(k * NISL + isl) % 2] = ("TE", cu)
                if isl + 1 < NISL:
                    issue_gu_load(k, isl + 1)
                elif k + 1 < NTO:
                    issue_gu_load(k + 1, 0)
            hd_last[par] = hd

        emit_gu(0)
        for k in range(NTO):
            par = k % 2
            csg = sop(lambda e, p_=par: e.sign(sgn2, hid3[:, p_, :]),
                      wait=(("GE", hd_last[par]), ("VE", q_done[1 - par])))
            S.emit("vector", None, wait=(("SE", csg), ("TE", S.c("TE"))))
            cq = quant_tile(chain2, blk2, hid3[:, par, :], sgn2, hq_t, I,
                            scal[:, 4:5], scal[:, 5:6])
            q_done[par] = cq
            if k + 1 < NTO:
                emit_gu(k + 1)
            for ic in range(NI):
                hh = ic % 2
                ct = top(lambda e, i_=ic, h_=hh: e.transpose(
                    psB[:, h_ * 128:(h_ + 1) * 128],
                    hq_t[:, i_ * 128:(i_ + 1) * 128], idb[:]),
                    wait=(("VE", cq), ("SE", S.c("SE"))))
                sop(lambda e, p_=par, i_=ic, h_=hh: e.copy(
                    hqT3[:, p_, i_, :], psB[:, h_ * 128:(h_ + 1) * 128]),
                    wait=(("TE", ct),))
            se_cp = S.c("SE")
            for csl_i in range(NDSL):
                cld, wd = wd_pref.pop((k, csl_i))
                wd3 = wd.rearrange("p (ic s) -> p ic s", ic=NI)
                psS = psS0 if csl_i % 2 == 0 else psS1
                wtd = [cld, ("SE", se_cp), ("SE", se_pS[csl_i % 2])]
                cd = None
                for ic in range(NI):
                    cd = top(lambda e, p_=par, i_=ic, w3=wd3, ps=psS: e.matmul(
                        ps[:, 0:128], hqT3[:, p_, i_, :], w3[:, i_, :],
                        start=(i_ == 0), stop=(i_ == NI - 1)),
                        wait=tuple(wtd) if ic == 0 else (),
                        inc="TE" if ic == NI - 1 else None)
                wdbuf_mm[csl_i % 2] = ("TE", cd)
                if csl_i + 2 < NDSL:
                    issue_wd_load(k, csl_i + 2)
                elif k + 1 < NTO:
                    issue_wd_load(k + 1, csl_i + 2 - NDSL)
                wds = [("TE", cd)]
                if csl_i == 0 and sp_guard[par]:
                    wds.append(sp_guard[par])
                cds = sop(lambda e, p_=par, c_=csl_i, ps=psS: e.activation(
                    g3[:, p_, c_ * 128:(c_ + 1) * 128], ps[:, 0:128],
                    ACT.Silu, bias=0.0, scale=scal[:, 14:15]), wait=tuple(wds))
                se_pS[csl_i % 2] = cds
            sp_guard[par] = stq(g_d[k * 128:(k + 1) * 128, :], g3[:, par, :],
                                wait=(("SE", S.c("SE")),))

        S.barrier()

        cur4 = Cursor(0)
        st_a = cur4.f32(1024); st_b = cur4.f32(1024)
        xt4_a = cur4.f32(1024); xt4_b = cur4.f32(1024)
        ot4_a = cur4.f32(1024); ot4_b = cur4.f32(1024)
        gt4_a = cur4.f32(1024); gt4_b = cur4.f32(1024)
        for tt in range(NTO):
            sbuf_s = st_a if tt % 2 == 0 else st_b
            sbuf_x = xt4_a if tt % 2 == 0 else xt4_b
            obuf = ot4_a if tt % 2 == 0 else ot4_b
            gbuf4 = gt4_a if tt % 2 == 0 else gt4_b
            wv = (("VE", S.c("VE")),) if tt >= 2 else ()
            cl1 = ld(sbuf_s, s_d[tt * 128:(tt + 1) * 128, :], wait=wv)
            cl3 = ld(gbuf4, g_d[tt * 128:(tt + 1) * 128, :], wait=wv)
            cl2 = ld(sbuf_x, x_full[tt * 128:(tt + 1) * 128, :], wait=wv)
            wsp = st_all() if tt >= 2 else ()
            vop(lambda e, s=sbuf_s, g_=gbuf4: e.tensor_tensor(s, s, g_, ALU.mult),
                wait=(cl1, cl3) + tuple(wsp))
            cfin = vop(lambda e, s=sbuf_s, x=sbuf_x, o=obuf: e.tensor_tensor(
                o, x, s, ALU.add), wait=(cl2,))
            stq(out_t[tt * 128:(tt + 1) * 128, :], obuf, wait=(("VE", cfin),))

        S.emit("gpsimd", None, wait=st_all())

        nc._script = S
        with nc.Block() as block:
            def make_runner(eng_name):
                def runner(eng):
                    for fn, waits, inc, inc_by, drain in S.ops[eng_name]:
                        for sname, val in waits:
                            if val > 0:
                                eng.wait_ge(sems[sname], val)
                        if drain and fn is not None:
                            eng.drain()
                        if fn is None:
                            continue
                        ins = fn(eng)
                        if inc is not None and ins is not None:
                            ins.then_inc(sems[inc], inc_by)
                return runner
            block.sync(make_runner("sync"))
            block.tensor(make_runner("tensor"))
            block.vector(make_runner("vector"))
            block.scalar(make_runner("scalar"))
            block.gpsimd(make_runner("gpsimd"))

    return nc


def _rope_tables(n, d=64, base=10000.0):
    try:
        import jax
        import jax.numpy as jnp
        cpu = jax.devices("cpu")[0]
        with jax.default_device(cpu):
            position = jnp.arange(n, dtype=jnp.float32)[:, None]
            div_term = jnp.exp(jnp.arange(0, d, 2, dtype=jnp.float32)
                               * (-np.log(base) / d))
            ang = position * div_term
            return np.asarray(jnp.sin(ang)), np.asarray(jnp.cos(ang))
    except Exception:
        position = np.arange(n, dtype=np.float32)[:, None]
        div_term = np.exp(np.arange(0, d, 2, dtype=np.float32)
                          * np.float32(-np.log(base) / d)).astype(np.float32)
        ang = (position * div_term).astype(np.float32)
        return np.sin(ang).astype(np.float32), np.cos(ang).astype(np.float32)


def _tile_tables(tab, NT):
    return np.ascontiguousarray(
        tab.reshape(NT, 128, 32).transpose(1, 0, 2).reshape(128, NT * 32))


def make_core_inputs(x, w_qkv, b_qkv, w_lepe, b_lepe, gate_w, up_w, down_w,
                     s_in, s_in_down, b, j, seq):
    OWN = seq // 2
    NT = seq // 128
    roll = j * OWN
    xr = np.ascontiguousarray(np.roll(np.asarray(x[b]), -roll, axis=0)).astype(np.float32)
    sin_full, cos_full = _rope_tables(seq)
    sin_r = np.roll(sin_full, -roll, axis=0)
    cos_r = np.roll(cos_full, -roll, axis=0)
    D = 1024
    f32 = np.float32
    wq = np.ascontiguousarray(np.asarray(w_qkv[0:D]).T).astype(f32)
    wk = np.ascontiguousarray(np.asarray(w_qkv[D:2 * D]).T).astype(f32)
    wv = np.ascontiguousarray(np.asarray(w_qkv[2 * D:3 * D]).T).astype(f32)
    lw = np.ascontiguousarray(
        np.asarray(w_lepe).reshape(8, 128, 3).transpose(1, 0, 2).reshape(128, 24)).astype(f32)
    lb = np.ascontiguousarray(
        np.asarray(b_lepe).reshape(8, 128).T).astype(f32)
    maskL = 1.0 if j == 1 else 0.0
    maskR = 1.0 if j == 0 else 0.0
    return {
        "x_full": xr,
        "wqT": wq, "wkT": wk, "wvT": wv,
        "bias_b": np.ascontiguousarray(np.broadcast_to(np.asarray(b_qkv, f32), (128, 3 * D))),
        "gate_w": np.asarray(gate_w, f32), "up_w": np.asarray(up_w, f32),
        "down_w": np.asarray(down_w, f32),
        "s_in_b": np.full((128, 1), np.asarray(s_in).reshape(-1)[0], f32),
        "s_ind_b": np.full((128, 1), np.asarray(s_in_down).reshape(-1)[0], f32),
        "lepe_w": lw, "lepe_b": lb,
        "sin_t": _tile_tables(sin_r, NT), "cos_t": _tile_tables(cos_r, NT),
        "masks": np.ascontiguousarray(np.broadcast_to(
            np.array([maskL, maskR], f32), (128, 2))),
        "ident_f": np.eye(128, dtype=f32),
        "ident_b": np.eye(128).astype(ml_dtypes.bfloat16),
        "ones_in": np.ones((128, 128), f32),
    }


def kernel(L0_attn_x, L0_attn_w_qkv, L0_attn_b_qkv, L0_attn_w_lepe, L0_attn_b_lepe,
           L0_ffn_gate_w, L0_ffn_up_w, L0_ffn_down_w, L0_ffn_s_in, L0_ffn_s_in_down):
    SEQ = 4096
    OWN = SEQ // 2
    x = np.asarray(L0_attn_x, np.float32)
    in_maps = []
    for core in range(8):
        b, j = core // 2, core % 2
        in_maps.append(make_core_inputs(
            x, L0_attn_w_qkv, L0_attn_b_qkv, L0_attn_w_lepe, L0_attn_b_lepe,
            L0_ffn_gate_w, L0_ffn_up_w, L0_ffn_down_w, L0_ffn_s_in, L0_ffn_s_in_down,
            b, j, SEQ))
    nc = build_nc(seq=SEQ, drains=True)
    res = run_bass_kernel_spmd(nc, in_maps, core_ids=list(range(8)))
    out = np.empty((4, SEQ, 1024), np.float32)
    for core in range(8):
        b, j = core // 2, core % 2
        out[b, j * OWN:(j + 1) * OWN, :] = np.asarray(res.results[core]["out_s"])
    return out


# version 28
# speedup vs baseline: 1.0293x; 1.0015x over previous
import numpy as np
import ml_dtypes
import concourse.bass as bass
import concourse.mybir as mybir
from concourse.bass_utils import run_bass_kernel_spmd
from contextlib import ExitStack

F32 = mybir.dt.float32
BF16 = mybir.dt.bfloat16
ALU = mybir.AluOpType
ACT = mybir.ActivationFunctionType
AX = mybir.AxisListType

FP8_MAX = 448.0
FP4_MAX = 6.0
GSC = FP8_MAX * FP4_MAX
EXP_BIAS = -8.0

ENGINES = ("sync", "tensor", "vector", "scalar", "gpsimd")


class Script:

    def __init__(self):
        self.ops = {e: [] for e in ENGINES}
        self.counts = {}

    def emit(self, eng, fn, wait=(), inc=None, inc_by=1, drain=False):
        self.ops[eng].append((fn, tuple(wait), inc, inc_by, drain))
        if inc is not None:
            self.counts[inc] = self.counts.get(inc, 0) + inc_by
            return self.counts[inc]
        return None

    def c(self, sem):
        return self.counts.get(sem, 0)

    def barrier(self):
        snap = tuple((s, v) for s, v in self.counts.items() if v > 0)
        for e in ENGINES:
            self.emit(e, None, wait=snap)


def build_nc(seq=4096, drains=False, dbg=False):
    SEQ = seq
    OWN = SEQ // 2
    NT = SEQ // 128
    NTO = OWN // 128
    D = 1024
    CC = 8
    NP = 8
    I = 4096
    NI = I // 128
    NISL = I // 512
    CHT = min(2, NTO)
    NCH = NTO // CHT
    NQT = max(OWN // 512, 1)
    QTW = min(512, OWN)
    GRP = 4 if NT % 4 == 0 else 1

    nc = bass.Bass("TRN2", target_bir_lowering=False)

    def din(name, shape, dt=F32):
        return nc.dram_tensor(name, shape, dt, kind="ExternalInput").ap()

    x_full = din("x_full", [SEQ, D])
    wqT = din("wqT", [D, D]); wkT = din("wkT", [D, D]); wvT = din("wvT", [D, D])
    bias_b = din("bias_b", [128, 3 * D])
    gate_w = din("gate_w", [I, D]); up_w = din("up_w", [I, D]); down_w = din("down_w", [D, I])
    s_in_b = din("s_in_b", [128, 1]); s_ind_b = din("s_ind_b", [128, 1])
    lepe_w = din("lepe_w", [128, NP * 3]); lepe_b = din("lepe_b", [128, NP])
    sin_t = din("sin_t", [128, NT * 32]); cos_t = din("cos_t", [128, NT * 32])
    masks = din("masks", [128, 2])
    ident_f = din("ident_f", [128, 128]); ident_b = din("ident_b", [128, 128], BF16)
    ones_in = din("ones_in", [128, 128])

    out_t = nc.dram_tensor("out_s", [OWN, D], F32, kind="ExternalOutput").ap()

    KIND = "ExternalOutput" if dbg else "Internal"
    wgq_d = nc.dram_tensor("wgq_d", [128, CC * I], BF16, kind=KIND).ap()
    wuq_d = nc.dram_tensor("wuq_d", [128, CC * I], BF16, kind=KIND).ap()
    wdq_d = nc.dram_tensor("wdq_d", [128, NI * D], BF16, kind=KIND).ap()
    xqT_d = nc.dram_tensor("xqT_d", [128, CC * OWN], BF16, kind=KIND).ap()
    xnT_d = nc.dram_tensor("xnT_d", [128, CC * SEQ], F32, kind="Internal").ap()
    s_d = nc.dram_tensor("s_d", [OWN, D], F32, kind=KIND).ap()
    g_d = nc.dram_tensor("g_d", [OWN, D], F32, kind=KIND).ap()
    if dbg:
        dbg_xnt = nc.dram_tensor("dbg_xnt", [128, CC * SEQ], F32, kind="ExternalOutput").ap()

    ctx = ExitStack()
    with ctx:
        def sbuf(name, shape, dt=F32):
            return ctx.enter_context(nc.sbuf_tensor(name, shape, dt))

        sinc = sbuf("sinc", [128, NT * 32]); cosc = sbuf("cosc", [128, NT * 32])
        lepw = sbuf("lepw", [128, NP * 3]); lepb = sbuf("lepb", [128, NP])
        maskc = sbuf("maskc", [128, 2])
        idf = sbuf("idf", [128, 128]); idb = sbuf("idb", [128, 128], BF16)
        onesc = sbuf("onesc", [128, 128])
        scal = sbuf("scal", [128, 32])
        expbuf = sbuf("expbuf", [128, 1024])
        biasg = sbuf("biasg", [128, 384])
        rcpr = sbuf("rcpr", [128, 512])

        rem = nc.sbuf_bytes_remaining
        acols = (rem // 4 // 128) * 128 - 128
        arena = sbuf("arena", [128, acols])
        arena_b = arena[:].bitcast(BF16)

        def psum(name, dt=F32, cols=512):
            return ctx.enter_context(nc.psum_tensor(name, [128, cols], dt))
        psS0 = psum("psS0"); psS1 = psum("psS1")
        psA0 = psum("psA0"); psA1 = psum("psA1")
        psP0 = psum("psP0"); psP1 = psum("psP1")
        psT = psum("psT")
        psB = psum("psB", BF16, 1024)

        sem_names = (["TE", "VE", "SE", "GE"] + [f"L{i}" for i in range(8)]
                     + [f"S{i}" for i in range(4)])
        sems = {s: ctx.enter_context(nc.semaphore(s)) for s in sem_names}

        S = Script()

        rr = {"ld": 0, "st": 0}

        def ld(dst_ap, src_ap, wait=()):
            sem = f"L{rr['ld'] % 8}"
            rr["ld"] += 1
            w = tuple(wait)
            prev = S.c(sem)
            if prev:
                w = w + ((sem, prev),)
            v = S.emit("sync", lambda e, d=dst_ap, s=src_ap: e.dma_start(out=d, in_=s),
                       wait=w, inc=sem, inc_by=16)
            return (sem, v)

        def ld_all():
            return tuple((f"L{i}", S.c(f"L{i}")) for i in range(8) if S.c(f"L{i}"))

        def stq(dst_ap, src_ap, wait=()):
            sem = f"S{rr['st'] % 4}"
            rr["st"] += 1
            w = tuple(wait)
            prev = S.c(sem)
            if prev:
                w = w + ((sem, prev),)
            v = S.emit("gpsimd", lambda e, d=dst_ap, s=src_ap: e.dma_start(out=d, in_=s),
                       wait=w, inc=sem, inc_by=16)
            return (sem, v)

        def st_all():
            return tuple((f"S{i}", S.c(f"S{i}")) for i in range(4) if S.c(f"S{i}"))

        dr_v = drains in (True, "all")
        dr_s = drains in (True, "all", "scalar")

        def vop(fn, wait=(), inc="VE", drain=None):
            return S.emit("vector", fn, wait=wait, inc=inc,
                          drain=dr_v if drain is None else drain)

        def sop(fn, wait=(), inc="SE", drain=None):
            return S.emit("scalar", fn, wait=wait, inc=inc,
                          drain=dr_s if drain is None else drain)

        def top(fn, wait=(), inc="TE"):
            return S.emit("tensor", fn, wait=wait, inc=inc)

        def gop(fn, wait=(), inc="GE", drain=None):
            return S.emit("gpsimd", fn, wait=wait, inc=inc,
                          drain=dr_v if drain is None else drain)

        class Cursor:
            def __init__(self, off=0):
                self.off = off
            def f32(self, n):
                assert self.off + n <= acols, f"arena overflow: {self.off}+{n} > {acols}"
                ap = arena[:, self.off:self.off + n]
                self.off += n
                return ap
            def bf16(self, n):
                assert n % 2 == 0
                assert self.off + n // 2 <= acols, "arena overflow (bf16)"
                ap = arena_b[:, 2 * self.off:2 * self.off + n]
                self.off += n // 2
                return ap

        ld(sinc[:], sin_t[:]); ld(cosc[:], cos_t[:])
        ld(lepw[:], lepe_w[:]); ld(lepb[:], lepe_b[:])
        ld(maskc[:], masks[:]); ld(idf[:], ident_f[:]); ld(idb[:], ident_b[:])
        ld(onesc[:], ones_in[:])
        ld(scal[:, 0:1], s_in_b[:]); ld(scal[:, 1:2], s_ind_b[:])
        c0all = ld_all()
        vop(lambda e: e.tensor_scalar(scal[:, 2:3], scal[:, 0:1], 1.0 / 6.0, None, ALU.mult),
            wait=c0all)
        vop(lambda e: e.tensor_scalar(scal[:, 3:4], scal[:, 0:1], 2.0, None, ALU.mult))
        vop(lambda e: e.tensor_scalar(scal[:, 4:5], scal[:, 1:2], 1.0 / 6.0, None, ALU.mult))
        vop(lambda e: e.tensor_scalar(scal[:, 5:6], scal[:, 1:2], 2.0, None, ALU.mult))
        vop(lambda e: e.memset(scal[:, 18:19], 1e-5))
        vop(lambda e: e.memset(scal[:, 19:20], EXP_BIAS))
        c_consts = ld_all()
        for _e in ("tensor", "scalar", "gpsimd", "vector"):
            S.emit(_e, None, wait=c_consts)

        def quant_block_chain(cur, am, gs6_ap, gs2_ap, nb):
            r2 = cur.f32(nb); s8h = cur.f32(nb)
            sc = cur.f32(nb); p = cur.f32(nb); pinv = cur.f32(nb)
            t1 = cur.f32(nb); t2 = cur.f32(nb); fl = cur.f32(nb)
            vop(lambda e: e.tensor_scalar(sc, am, gs6_ap, 448.0, ALU.mult, ALU.min))
            vop(lambda e: e.tensor_scalar(t1, sc, 4.0, None, ALU.is_ge))
            vop(lambda e: e.tensor_scalar(p, t1, 255.0, 1.0, ALU.mult, ALU.add))
            vop(lambda e: e.tensor_scalar(pinv, t1, (2.0 ** -8) - 1.0, 1.0, ALU.mult, ALU.add))
            for mul, thr in ((15.0, 0.25), (3.0, 2.0 ** -4), (1.0, 2.0 ** -5)):
                vop(lambda e, th=thr: e.tensor_scalar(t2, p, th, None, ALU.mult))
                vop(lambda e: e.tensor_tensor(t1, sc, t2, ALU.is_ge))
                vop(lambda e, m=mul: e.tensor_scalar(t2, t1, m, 1.0, ALU.mult, ALU.add))
                vop(lambda e: e.tensor_tensor(p, p, t2, ALU.mult))
                vop(lambda e, m=mul: e.tensor_scalar(t2, t1, 1.0 / (m + 1.0) - 1.0, 1.0,
                                                     ALU.mult, ALU.add))
                vop(lambda e: e.tensor_tensor(pinv, pinv, t2, ALU.mult))
            vop(lambda e: e.scalar_tensor_tensor(t1, sc, 512.0, pinv, ALU.mult, ALU.mult))
            vop(lambda e: e.tensor_scalar(t1, t1, 0.5, None, ALU.add))
            vop(lambda e: e.tensor_scalar(t2, t1, 16.0, 16.0, ALU.is_ge, ALU.mult))
            vop(lambda e: e.tensor_tensor(t1, t1, t2, ALU.subtract))
            vop(lambda e: e.tensor_copy(fl, t2))
            vop(lambda e: e.tensor_scalar(t2, t1, 8.0, 8.0, ALU.is_ge, ALU.mult))
            vop(lambda e: e.tensor_tensor(t1, t1, t2, ALU.subtract))
            vop(lambda e: e.tensor_tensor(fl, fl, t2, ALU.add))
            vop(lambda e: e.tensor_scalar(t2, t1, 4.0, 4.0, ALU.is_ge, ALU.mult))
            vop(lambda e: e.tensor_tensor(t1, t1, t2, ALU.subtract))
            vop(lambda e: e.tensor_tensor(fl, fl, t2, ALU.add))
            vop(lambda e: e.tensor_scalar(t2, t1, 2.0, 2.0, ALU.is_ge, ALU.mult))
            vop(lambda e: e.tensor_tensor(t1, t1, t2, ALU.subtract))
            vop(lambda e: e.tensor_tensor(fl, fl, t2, ALU.add))
            vop(lambda e: e.scalar_tensor_tensor(fl, t1, 1.0, fl, ALU.is_ge, ALU.add))
            vop(lambda e: e.scalar_tensor_tensor(sc, fl, 2.0 ** -9, p, ALU.mult, ALU.mult))
            vop(lambda e: e.tensor_scalar(s8h, sc, 0.5, None, ALU.mult))
            vop(lambda e: e.tensor_single_scalar(t1, sc, 0.0, ALU.is_equal))
            vop(lambda e: e.tensor_tensor(t2, sc, t1, ALU.max))
            vop(lambda e: e.reciprocal(t2, t2))
            vop(lambda e: e.tensor_scalar(t2, t2, gs2_ap, None, ALU.mult))
            vop(lambda e: e.tensor_scalar(fl, t1, -1.0, 1.0, ALU.mult, ALU.add))
            vop(lambda e: e.tensor_tensor(t2, t2, fl, ALU.mult))
            c = vop(lambda e: e.scalar_tensor_tensor(r2, t1, 2.0, t2, ALU.mult, ALU.add))
            return r2, s8h, c

        def quant_elem_chain(cur, w_ap, sgn_ap, r2, s8h, out_ap, ncols, nb):
            u = cur.f32(ncols); A = cur.f32(ncols); Bv = cur.f32(ncols)
            t = cur.f32(ncols); h = cur.f32(ncols); c = cur.f32(ncols)
            r2b = r2.unsqueeze(2).broadcast_to([128, nb, 16])
            s8hb = s8h.unsqueeze(2).broadcast_to([128, nb, 16])
            u3 = u.rearrange("p (g b) -> p g b", b=16)
            vop(lambda e: e.scalar_tensor_tensor(u, w_ap, -1.0, w_ap, ALU.mult, ALU.max))
            vop(lambda e: e.tensor_tensor(u3, u3, r2b, ALU.mult))
            vop(lambda e: e.tensor_scalar(u, u, 12.0, None, ALU.min))
            vop(lambda e: e.tensor_scalar(A, u, 4.0, 0.5, ALU.is_ge, ALU.mult))
            vop(lambda e: e.tensor_scalar(Bv, u, 8.0, 0.25, ALU.is_ge, ALU.mult))
            vop(lambda e: e.tensor_tensor(t, A, Bv, ALU.add))
            vop(lambda e: e.scalar_tensor_tensor(h, Bv, 3.0, t, ALU.mult, ALU.add))
            vop(lambda e: e.tensor_scalar(t, t, -1.0, 1.0, ALU.mult, ALU.add))
            vop(lambda e: e.tensor_scalar(A, h, 2.0, 1.0, ALU.mult, ALU.add))
            vop(lambda e: e.scalar_tensor_tensor(h, u, 0.5, h, ALU.add, ALU.add))
            vop(lambda e: e.tensor_tensor(h, h, t, ALU.mult))
            vop(lambda e: e.tensor_scalar(c, h, 1.0, None, ALU.is_ge))
            vop(lambda e: e.scalar_tensor_tensor(c, h, 2.0, c, ALU.is_ge, ALU.add))
            vop(lambda e: e.scalar_tensor_tensor(c, h, 3.0, c, ALU.is_ge, ALU.add))
            vop(lambda e: e.scalar_tensor_tensor(c, h, 4.0, c, ALU.is_ge, ALU.add))
            vop(lambda e: e.tensor_tensor(c, c, A, ALU.mult))
            vop(lambda e: e.tensor_tensor(c, c, sgn_ap, ALU.mult))
            o3 = out_ap.rearrange("p (g b) -> p g b", b=16)
            c3 = c.rearrange("p (g b) -> p g b", b=16)
            return vop(lambda e: e.tensor_tensor(o3, c3, s8hb, ALU.mult))

        def quant_tile(chain_base, blk_ap, src_ap, sgn_ap, out_ap, C, gs6_ap, gs2_ap):
            nb = C // 16
            vop(lambda e: e.reduce_max(
                blk_ap[:, 0:nb], src_ap[:, 0:C].rearrange("p (g k) -> p g k", k=16),
                axis=AX.X, apply_absolute_value=True))
            curq = Cursor(chain_base)
            r2, s8h, _ = quant_block_chain(curq, blk_ap[:, 0:nb], gs6_ap, gs2_ap, nb)
            elem_base = chain_base + 2 * nb
            cq = None
            for sub in range(C // 1024):
                sl = slice(sub * 1024, (sub + 1) * 1024)
                bsl = slice(sub * 64, (sub + 1) * 64)
                cq = quant_elem_chain(Cursor(elem_base),
                                      src_ap[:, sl], sgn_ap[:, sl],
                                      r2[:, bsl], s8h[:, bsl],
                                      out_ap[:, sl], 1024, 64)
            return cq

        def cross_partition_max(src_col, dst_col, tmp_row):
            ct = top(lambda e: e.transpose(psT[0:1, 0:128], src_col, idf[:]),
                     wait=(("VE", S.c("VE")), ("SE", S.c("SE"))))
            cv = vop(lambda e: e.tensor_copy(tmp_row[0:1, 0:128], psT[0:1, 0:128]),
                     wait=(("TE", ct),))
            cv = vop(lambda e: e.reduce_max(tmp_row[0:1, 0:1], tmp_row[0:1, 0:128], axis=AX.X))
            ct = top(lambda e: e.matmul(psT[:, 0:1], onesc[0:1, :], tmp_row[0:1, 0:1],
                                        start=True, stop=True), wait=(("VE", cv),))
            return vop(lambda e: e.tensor_copy(dst_col, psT[:, 0:1]), wait=(("TE", ct),))

        P0_NEED = 4096 + 4096 + 2048 + 2048 + 512 + 2 * 256 + 6 * 1024
        p0_cur = Cursor(acols - P0_NEED)
        wt_a = p0_cur.f32(4096)
        sgn0 = p0_cur.f32(4096)
        wq_a = p0_cur.bf16(4096)
        stage0 = p0_cur.bf16(4096)
        blk0 = p0_cur.f32(512)
        chain_base0 = p0_cur.off

        wcfg = [
            (gate_w, wgq_d, I, D, 6, 9, 15, 12, 0),
            (up_w, wuq_d, I, D, 7, 10, 16, 13, 0),
            (down_w, wdq_d, D, I, 8, 11, 17, 14, 1),
        ]
        p0_state = {"last_sp": None, "last_q": None, "last_ct": None,
                    "pending": None, "widx": 0}

        def emit_p0_passA():
            for (src_, dst, rows, C, sl6, sl2, slm, slr, sli) in wcfg:
                nrt = rows // 128
                gmax = scal[:, slm:slm + 1]
                last_v = S.c("VE")
                for ot in range(nrt):
                    cl = ld(wt_a[:, 0:C], src_[ot * 128:(ot + 1) * 128, :],
                            wait=(("VE", last_v),))
                    if ot == 0:
                        last_v = vop(lambda e, cc=C, g_=gmax: e.reduce_max(
                            g_, wt_a[:, 0:cc], axis=AX.X, apply_absolute_value=True),
                            wait=(cl,))
                    else:
                        vop(lambda e, cc=C: e.reduce_max(
                            blk0[:, 0:1], wt_a[:, 0:cc], axis=AX.X,
                            apply_absolute_value=True), wait=(cl,))
                        last_v = vop(lambda e, g_=gmax: e.tensor_tensor(
                            g_, g_, blk0[:, 0:1], ALU.max))
                cross_partition_max(gmax, gmax, rcpr)
                vop(lambda e, g_=gmax: e.reciprocal(blk0[:, 0:1], g_))
                vop(lambda e, a=sl6: e.tensor_scalar(scal[:, a:a + 1], blk0[:, 0:1],
                                                     GSC / 6.0, None, ALU.mult))
                vop(lambda e, a=sl2: e.tensor_scalar(scal[:, a:a + 1], blk0[:, 0:1],
                                                     GSC * 2.0, None, ALU.mult))
                vop(lambda e, s=sli: e.reciprocal(blk0[:, 1:2], scal[:, s:s + 1]))
                vop(lambda e, r=slr, g_=gmax: e.scalar_tensor_tensor(
                    scal[:, r:r + 1], g_, 1.0 / GSC, blk0[:, 1:2], ALU.mult, ALU.mult))

        p0_work = []
        for wi, (src_, dst, rows, C, sl6, sl2, slm, slr, sli) in enumerate(wcfg):
            for ot in range(rows // 128):
                p0_work.append((src_, dst, C, sl6, sl2, ot))

        def emit_p0_quant(src_, dst, C, sl6, sl2, ot):
            last_sp = p0_state["last_sp"]
            wsp0 = (last_sp,) if last_sp else ()
            wq = (("VE", p0_state["last_q"]),) if p0_state["last_q"] else (("VE", S.c("VE")),)
            cl = ld(wt_a[:, 0:C], src_[ot * 128:(ot + 1) * 128, :], wait=wq + wsp0)
            csg = sop(lambda e, cc=C: e.sign(sgn0[:, 0:cc], wt_a[:, 0:cc]), wait=(cl,))
            wtr = (("TE", p0_state["last_ct"]),) if p0_state["last_ct"] else ()
            S.emit("vector", None, wait=(("SE", csg), cl) + wtr)
            cq = quant_tile(chain_base0, blk0, wt_a, sgn0, wq_a, C,
                            scal[:, sl6:sl6 + 1], scal[:, sl2:sl2 + 1])
            p0_state["last_q"] = cq
            S.emit("scalar", None, wait=(("VE", cq),))
            p0_state["pending"] = (dst, C, ot, cq)

        def emit_p0_flush():
            if p0_state["pending"] is None:
                return
            dst, C, ot, cq = p0_state["pending"]
            p0_state["pending"] = None
            last_sp = p0_state["last_sp"]
            wsp0 = (last_sp,) if last_sp else ()
            ncc = C // 128
            ct = None
            for q in range(ncc):
                hh = q % 2
                ct = top(lambda e, qq=q, h_=hh: e.transpose(
                    psB[:, h_ * 128:(h_ + 1) * 128],
                    wq_a[:, qq * 128:(qq + 1) * 128], idb[:]),
                    wait=(("VE", cq), ("SE", S.c("SE"))) + wsp0)
                sop(lambda e, qq=q, h_=hh: e.copy(
                    stage0[:, qq * 128:(qq + 1) * 128],
                    psB[:, h_ * 128:(h_ + 1) * 128]), wait=(("TE", ct),))
            p0_state["last_ct"] = ct
            if C == 1024:
                dv4 = dst.rearrange("p (il q r) -> p il q r", il=NISL, q=ncc, r=512)
                dview = dv4[:, ot // 4, :, (ot % 4) * 128:(ot % 4 + 1) * 128]
            else:
                dview = dst.rearrange("p (db ic r) -> p db ic r",
                                      db=8, ic=NI, r=128)[:, ot, :, :]
            sview = stage0[:, 0:ncc * 128].rearrange("p (q r) -> p q r", q=ncc)
            p0_state["last_sp"] = stq(dview, sview, wait=(("SE", S.c("SE")),))

        def weave_slot():
            emit_p0_flush()
            if p0_state["widx"] < len(p0_work):
                emit_p0_quant(*p0_work[p0_state["widx"]])
                p0_state["widx"] += 1

        S.barrier()

        cur1 = Cursor()
        p3_base = 0
        xnstage = cur1.f32(1024)
        xt_a = cur1.f32(1024); xt_b = cur1.f32(1024)
        xn_t = cur1.f32(1024); sq_t = cur1.f32(1024)
        mv = cur1.f32(4)
        xq_t = cur1.bf16(1024)
        xqstage = cur1.bf16(1024)
        sgn1 = cur1.f32(1024)
        blk1 = cur1.f32(64)
        chain1 = cur1.off

        xnT3d = xnT_d.rearrange("p (q s) -> p q s", q=CC)
        xnst3 = xnstage.rearrange("p (q s) -> p q s", q=CC)
        prev_done = S.c("VE")
        prev_sp = None
        prev_sp_xn = None
        xn_guard = ()
        for tt in range(NT):
            buf = xt_a if tt % 2 == 0 else xt_b
            cl = ld(buf, x_full[tt * 128:(tt + 1) * 128, :],
                    wait=(("VE", prev_done),) + ((prev_sp,) if prev_sp else ()))
            vop(lambda e, b=buf: e.reduce_sum(mv[:, 0:1], b, axis=AX.X), wait=(cl,))
            vop(lambda e: e.tensor_scalar(mv[:, 0:1], mv[:, 0:1], 1.0 / D, None, ALU.mult))
            cve = vop(lambda e, b=buf: e.tensor_scalar(xn_t, b, mv[:, 0:1], None, ALU.subtract),
                       wait=xn_guard)
            sop(lambda e: e.activation(sq_t, xn_t, ACT.Square, bias=0.0, scale=1.0,
                                       accum_out=mv[:, 2:3]), wait=(("VE", cve),))
            cs2 = sop(lambda e: e.activation(mv[:, 3:4], mv[:, 2:3], ACT.Sqrt,
                                             bias=scal[:, 18:19], scale=1.0 / D))
            vop(lambda e: e.reciprocal(mv[:, 1:2], mv[:, 3:4]), wait=(("SE", cs2),))
            cxn = vop(lambda e: e.tensor_scalar(xn_t, xn_t, mv[:, 1:2], None, ALU.mult))
            for q in range(CC):
                wtr = [("VE", cxn), ("SE", S.c("SE"))]
                if q == 0 and prev_sp_xn:
                    wtr.append(prev_sp_xn)
                ct = top(lambda e, qq=q: e.transpose(
                    psT[:, 0:128], xn_t[:, qq * 128:(qq + 1) * 128], idf[:]),
                    wait=tuple(wtr))
                sop(lambda e, qq=q: e.copy(xnst3[:, qq, :], psT[:, 0:128]),
                    wait=(("TE", ct),))
            prev_sp_xn = stq(xnT3d[:, :, tt * 128:(tt + 1) * 128], xnst3,
                             wait=(("SE", S.c("SE")),))
            prev_done = cxn
            if tt < NTO:
                csg = sop(lambda e: e.sign(sgn1, xn_t), wait=(("VE", cxn),))
                S.emit("vector", None, wait=(("SE", csg),))
                cq = quant_tile(chain1, blk1, xn_t, sgn1, xq_t, 1024,
                                scal[:, 2:3], scal[:, 3:4])
                S.emit("scalar", None, wait=(("VE", cq),))
                for q in range(CC):
                    hh = q % 2
                    ct = top(lambda e, qq=q, h_=hh: e.transpose(
                        psB[:, h_ * 128:(h_ + 1) * 128],
                        xq_t[:, qq * 128:(qq + 1) * 128], idb[:]),
                        wait=(("VE", cq), ("SE", S.c("SE"))) + ((prev_sp,) if prev_sp else ()))
                    sop(lambda e, qq=q, h_=hh: e.copy(
                        xqstage[:, qq * 128:(qq + 1) * 128],
                        psB[:, h_ * 128:(h_ + 1) * 128]), wait=(("TE", ct),))
                dv = xqT_d.rearrange("p (q s) -> p q s", q=CC)[:, :, tt * 128:(tt + 1) * 128]
                sv = xqstage.rearrange("p (q s) -> p q s", q=CC)
                prev_sp = stq(dv, sv, wait=(("SE", S.c("SE")),))
                prev_done = cq
            xn_guard = (("TE", S.c("TE")), ("SE", S.c("SE")))

        S.barrier()
        if dbg:
            stq(dbg_xnt[:], xnT)
            S.barrier()

        cur3 = Cursor(p3_base)
        xnb_a = cur3.f32(CC * GRP * 128); xnb_b = cur3.f32(CC * GRP * 128)
        KS1 = cur3.bf16(SEQ); KS2 = cur3.bf16(SEQ)
        QS1 = cur3.bf16(OWN); QS2 = cur3.bf16(OWN)
        QW1 = cur3.bf16(OWN); QW2 = cur3.bf16(OWN)
        VH1 = cur3.bf16(NT * 65); VL1 = cur3.bf16(NT * 65)
        VH2 = cur3.bf16(NT * 65); VL2 = cur3.bf16(NT * 65)
        phr = cur3.f32(1024)
        PhiB = phr[:, 0:512].bitcast(BF16)
        PloB = phr[:, 512:1024].bitcast(BF16)
        vtf = cur3.f32(OWN + 2)
        kvg_off = cur3.off
        kvg = cur3.f32(GRP * 256)
        ket = cur3.f32(GRP * 128)
        rtw = cur3.f32(GRP * 128)
        rt1 = rtw[:, 0:GRP * 64]; rt2 = rtw[:, GRP * 64:GRP * 128]
        sstg = rcpr[:, 0:128]
        assert cur3.off <= acols - P0_NEED, f"P3 region {cur3.off} overlaps P0 at {acols - P0_NEED}"
        assert GRP * 256 + GRP * 128 + 2 * GRP * 64 >= OWN, "sfm alias too small"
        sfm = arena[:, kvg_off:kvg_off + OWN]

        def kv_slot(q):
            if q < 4:
                return expbuf[:, q * 256:(q + 1) * 256]
            if q < 6:
                return rcpr[:, (q - 4) * 256:(q - 3) * 256]
            return phr[:, (q - 6) * 256:(q - 5) * 256]

        def wq_slot(q):
            return expbuf[:, q * 128:(q + 1) * 128]

        def proc_group(tiles, is_q):
            n = len(tiles)
            if is_q:
                src = kvg.rearrange("p (t w) -> p t w", t=GRP)[:, 0:n, 0:128]
                bslc = biasg[:, 0:128]
            else:
                src = kvg.rearrange("p (t w) -> p t w", t=GRP)[:, 0:n, 0:128]
                bslc = biasg[:, 128:256]
            ketv = ket.rearrange("p (t w) -> p t w", t=GRP)[:, 0:n, :]
            bb = bslc.unsqueeze(1).broadcast_to([128, n, 128])
            vop(lambda e: e.tensor_tensor(ketv, src, bb, ALU.add))
            kkv = kvg.rearrange("p (t w) -> p t w", t=GRP)[:, 0:n, 0:128]
            rtv = rtw[:, 0:n * 128]
            ketf = ket[:, 0:n * 128]
            ket3 = ket.rearrange("p (t w) -> p t w", t=GRP)[:, 0:n, :]
            cv = vop(lambda e: e.tensor_scalar(kkv, ket3, 0.0, None, ALU.min))
            cs = sop(lambda e: e.activation(rtv, kkv, ACT.Exp, bias=0.0, scale=1.0),
                     wait=(("VE", cv),))
            vop(lambda e: e.tensor_scalar(kkv, ket3, 0.0, None, ALU.max),
                wait=(("SE", cs),))
            rtv3 = rtv.rearrange("p (t w) -> p t w", t=n)
            vop(lambda e: e.tensor_tensor(ket3, kkv, rtv3, ALU.add))
            t0 = tiles[0]
            ket5 = ket.rearrange("p (t h j u) -> p t h j u", h=2, j=32, u=2)
            x1 = ket5[:, 0:n, :, :, 0]
            x2 = ket5[:, 0:n, :, :, 1]
            sv = sinc[:, t0 * 32:(t0 + n) * 32].rearrange("p (t j) -> p t j", t=n)
            svb = sv.unsqueeze(2).broadcast_to([128, n, 2, 32])
            cvv = cosc[:, t0 * 32:(t0 + n) * 32].rearrange("p (t j) -> p t j", t=n)
            cvb = cvv.unsqueeze(2).broadcast_to([128, n, 2, 32])
            r1v3 = rt1.rearrange("p (t h j) -> p t h j", t=GRP, h=2)[:, 0:n]
            r2v3 = rt2.rearrange("p (t h j) -> p t h j", t=GRP, h=2)[:, 0:n]
            dd = kvg.rearrange("p (t q) -> p t q", t=GRP)[:, 0:n, 0:128]
            de = dd.rearrange("p t (h u j) -> p t h u j", h=2, u=2, j=32)[:, :, :, 0, :]
            do = dd.rearrange("p t (h u j) -> p t h u j", h=2, u=2, j=32)[:, :, :, 1, :]
            vop(lambda e: e.tensor_tensor(r1v3, x1, cvb, ALU.mult))
            vop(lambda e: e.tensor_tensor(r2v3, x2, svb, ALU.mult))
            vop(lambda e: e.tensor_tensor(de, r1v3, r2v3, ALU.subtract))
            vop(lambda e: e.tensor_tensor(r1v3, x1, svb, ALU.mult))
            vop(lambda e: e.tensor_tensor(r2v3, x2, cvb, ALU.mult))
            cfin = vop(lambda e: e.tensor_tensor(do, r1v3, r2v3, ALU.add))
            stg = (rt1.bitcast(BF16).rearrange("p (t w) -> p t w", t=GRP)[:, 0:n, 0:128],
                   rt2.bitcast(BF16).rearrange("p (t w) -> p t w", t=GRP)[:, 0:n, 0:128])
            kr4 = kvg.rearrange("p (t w) -> p t w", t=GRP)[:, 0:n, 0:128]

            def tr_into(sg, dstT, tiles, dep):
                for i, tt in enumerate(tiles):
                    hh = i % 2
                    ct = top(lambda e, ii=i, s_=sg, h_=hh: e.transpose(
                        psB[:, h_ * 128:(h_ + 1) * 128], s_[:, ii, :], idb[:]),
                        wait=(("VE", dep), ("SE", S.c("SE"))))
                    sop(lambda e, t=tt, d=dstT, h_=hh: e.copy(
                        d[:, t * 128:(t + 1) * 128],
                        psB[:, h_ * 128:(h_ + 1) * 128]), wait=(("TE", ct),))

            for h in range(2):
                sg = stg[0]
                hs = slice(h * 64, (h + 1) * 64)
                vop(lambda e, s_=sg, k_=hs: e.tensor_copy(s_[:, :, 0:64], kr4[:, :, k_]),
                    wait=(("TE", S.c("TE")),))
                cfin2 = vop(lambda e, s_=sg, k_=hs: e.tensor_tensor(
                    s_[:, :, 64:128], kr4[:, :, k_], s_[:, :, 0:64], ALU.subtract))
                dstT = (QS1 if h == 0 else QS2) if is_q else (KS1 if h == 0 else KS2)
                tr_into(sg, dstT, tiles, cfin2)
                if is_q:
                    sw = stg[1]
                    vop(lambda e, a=sw, b=sg: e.tensor_copy(a[:, :, 0:64], b[:, :, 64:128]),
                        wait=(("TE", S.c("TE")),))
                    cfin3 = vop(lambda e, a=sw, b=sg: e.tensor_copy(a[:, :, 64:128],
                                                                    b[:, :, 0:64]))
                    tr_into(sw, QW1 if h == 0 else QW2, tiles, cfin3)
            return S.c("SE")

        emit_p0_passA()
        p0_per_pair = (len(p0_work) + NP - 1) // NP
        for g in range(NP):
            pb_te = S.c("TE"); pb_ve = S.c("VE"); pb_se = S.c("SE")
            for wi, wsrc in enumerate((wkT, wvT)):
                for q in range(CC):
                    ld(kv_slot(q)[:, wi * 128:(wi + 1) * 128],
                       wsrc[q * 128:(q + 1) * 128, g * 128:(g + 1) * 128],
                       wait=(("TE", pb_te), ("VE", pb_ve), ("SE", pb_se)) + st_all())
            ld(biasg[:, 0:128], bias_b[:, g * 128:(g + 1) * 128])
            ld(biasg[:, 128:256], bias_b[:, D + g * 128:D + (g + 1) * 128])
            cbl = ld(biasg[:, 256:384], bias_b[:, 2 * D + g * 128:2 * D + (g + 1) * 128])
            wld = ld_all()
            cv1 = vop(lambda e: e.memset(VH1, 1.0), wait=(("TE", pb_te),))
            vop(lambda e: e.memset(VH2, 1.0))
            vop(lambda e: e.memset(VL1, 0.0))
            cv2 = vop(lambda e: e.memset(VL2, 0.0))

            pbanks = (psP0, psP1, psS0, psS1)
            pb_evac = {0: ("VE", pb_ve), 1: ("VE", pb_ve),
                       2: ("VE", pb_ve), 3: ("VE", pb_ve)}
            xnb_use = {0: 0, 1: 0}
            for gi, g0 in enumerate(range(0, NT, GRP)):
                gts = list(range(g0, min(g0 + GRP, NT)))
                xnb = xnb_a if gi % 2 == 0 else xnb_b
                xnb3 = xnb.rearrange("p (q s) -> p q s", q=CC)
                cxl = ld(xnb3[:, :, 0:len(gts) * 128],
                         xnT3d[:, :, g0 * 128:(g0 + len(gts)) * 128],
                         wait=(("TE", xnb_use[gi % 2]),) + ((prev_sp_xn,) if gi == 0 else ()))
                for i, tt in enumerate(gts):
                    bi = i % 4
                    pp = pbanks[bi]
                    wt = list(wld) + [pb_evac[bi], cxl]
                    ctev = None
                    for q in range(CC):
                        ctev = top(lambda e, qq=q, ii=i, p_=pp, x_=xnb3: e.matmul(
                            p_[:, 0:256], x_[:, qq, ii * 128:(ii + 1) * 128],
                            kv_slot(qq), start=(qq == 0), stop=(qq == CC - 1)),
                            wait=tuple(wt) if q == 0 else (),
                            inc="TE" if q == CC - 1 else None)
                    pb_evac[bi] = ("VE", vop(lambda e, ii=i, p_=pp: e.tensor_copy(
                        kvg[:, ii * 256:(ii + 1) * 256], p_[:, 0:256]),
                        wait=(("TE", S.c("TE")),)))
                xnb_use[gi % 2] = S.c("TE")
                ng = len(gts)
                kv4 = kvg.rearrange("p (t w) -> p t w", t=GRP)[:, 0:ng, :]
                proc_group(gts, is_q=False)
                ketv = ket.rearrange("p (t w) -> p t w", t=GRP)[:, 0:ng, :]
                bvp = biasg[:, 256:384].unsqueeze(1).broadcast_to([128, ng, 128])
                vop(lambda e: e.tensor_tensor(ketv, kv4[:, :, 128:256], bvp, ALU.add),
                    wait=(("TE", S.c("TE")), ("SE", S.c("SE"))))
                for h, (VH, VL) in enumerate(((VH1, VL1), (VH2, VL2))):
                    hd = slice(h * 64, (h + 1) * 64)
                    vhd = VH.rearrange("p (t w) -> p t w", t=NT)[:, g0:g0 + ng, 0:64]
                    vld = VL.rearrange("p (t w) -> p t w", t=NT)[:, g0:g0 + ng, 0:64]
                    vop(lambda e, d_=vhd, s_=hd: e.tensor_copy(d_, ketv[:, :, s_]))
                    vop(lambda e, d_=vld, s_=hd, h_=vhd: e.tensor_tensor(
                        d_, ketv[:, :, s_], h_, ALU.subtract))
                for i, tt in enumerate(gts):
                    if not (tt <= NTO or tt == NT - 1):
                        continue
                    ctv = top(lambda e, ii=i: e.transpose(
                        psT[:, 0:128], ketv[:, ii, :], idf[:]),
                        wait=(("VE", S.c("VE")), ("SE", S.c("SE"))))
                    if tt < NTO:
                        sop(lambda e, t=tt: e.copy(
                            vtf[:, 1 + t * 128:1 + (t + 1) * 128], psT[:, 0:128]),
                            wait=(("TE", ctv),))
                    elif tt == NTO:
                        sop(lambda e: e.copy(vtf[:, OWN + 1:OWN + 2], psT[:, 0:1]),
                            wait=(("TE", ctv),))
                    if tt == NT - 1:
                        sop(lambda e: e.copy(vtf[:, 0:1], psT[:, 127:128]),
                            wait=(("TE", ctv),))
            kse = S.c("SE")

            for q in range(CC):
                ld(wq_slot(q), wqT[q * 128:(q + 1) * 128, g * 128:(g + 1) * 128],
                   wait=(("TE", S.c("TE")), ("SE", S.c("SE"))))
            wldq = ld_all()
            for gi, g0 in enumerate(range(0, NTO, GRP)):
                gts = list(range(g0, min(g0 + GRP, NTO)))
                xnb = xnb_a if gi % 2 == 0 else xnb_b
                xnb3 = xnb.rearrange("p (q s) -> p q s", q=CC)
                cxl = ld(xnb3[:, :, 0:len(gts) * 128],
                         xnT3d[:, :, g0 * 128:(g0 + len(gts)) * 128],
                         wait=(("TE", xnb_use[gi % 2]),))
                for i, tt in enumerate(gts):
                    bi = i % 4
                    pp = pbanks[bi]
                    wt = list(wldq) + [pb_evac[bi], cxl]
                    ctev = None
                    for q in range(CC):
                        ctev = top(lambda e, qq=q, ii=i, p_=pp, x_=xnb3: e.matmul(
                            p_[:, 0:128], x_[:, qq, ii * 128:(ii + 1) * 128],
                            wq_slot(qq), start=(qq == 0), stop=(qq == CC - 1)),
                            wait=tuple(wt) if q == 0 else (),
                            inc="TE" if q == CC - 1 else None)
                    pb_evac[bi] = ("VE", vop(lambda e, ii=i, p_=pp: e.tensor_copy(
                        kvg[:, ii * 256:ii * 256 + 128], p_[:, 0:128]),
                        wait=(("TE", S.c("TE")),)))
                xnb_use[gi % 2] = S.c("TE")
                proc_group(gts, is_q=True)

            chalo = S.c("SE")
            vop(lambda e: e.tensor_scalar(vtf[:, 0:1], vtf[:, 0:1], maskc[:, 0:1],
                                          None, ALU.mult), wait=(("SE", chalo),))
            vop(lambda e: e.tensor_scalar(vtf[:, OWN + 1:OWN + 2], vtf[:, OWN + 1:OWN + 2],
                                          maskc[:, 1:2], None, ALU.mult))

            qkT_done = S.c("SE")
            prev_exp = [qkT_done, qkT_done]
            prev_av = [S.c("TE"), S.c("TE")]
            prev_plo = [S.c("VE"), S.c("VE")]
            psA_free = {0: S.c("VE"), 1: S.c("VE")}
            for h in range(2):
                psA = psA0 if h == 0 else psA1
                VH = VH1 if h == 0 else VH2
                VL = VL1 if h == 0 else VL2
                for qt in range(NQT):
                    exp_c = {}
                    pf_c = {}
                    av_c = {}
                    plo_c = {}
                    hr = slice(h * 64, (h + 1) * 64)

                    KS = KS1 if h == 0 else KS2
                    QS = QS1 if h == 0 else QS2
                    QW = QW1 if h == 0 else QW2

                    def emit_s(kb):
                        pss = psS0 if kb % 2 == 0 else psS1
                        w = [("SE", pf_c[kb - 2] if kb >= 2 else prev_exp[kb % 2])]
                        top(lambda e, k=kb, p_=pss, q_=qt, K_=KS, Q_=QS: e.matmul(
                            p_[:, 0:QTW], K_[:, k * 128:(k + 1) * 128],
                            Q_[:, q_ * QTW:(q_ + 1) * QTW],
                            start=True, stop=False), wait=tuple(w), inc=None)
                        cs_mm = top(lambda e, k=kb, p_=pss, q_=qt, K_=KS, Q_=QW: e.matmul(
                            p_[:, 0:QTW], K_[:, k * 128:(k + 1) * 128],
                            Q_[:, q_ * QTW:(q_ + 1) * QTW],
                            start=False, stop=True))
                        sl = slice((kb % 2) * 512, (kb % 2) * 512 + QTW)
                        w2b = [("TE", cs_mm),
                               ("TE", av_c[kb - 2] if kb >= 2 else prev_av[kb % 2])]
                        exp_c[kb] = sop(lambda e, p_=pss, s_=sl: e.activation(
                            PhiB[:, s_], p_[:, 0:QTW],
                            ACT.Exp, bias=scal[:, 19:20], scale=0.125), wait=tuple(w2b),
                            drain=False)
                        w2 = [("VE", plo_c[kb - 2] if kb >= 2 else prev_plo[kb % 2])]
                        pf_c[kb] = sop(lambda e, p_=pss, s_=sl: e.activation(
                            expbuf[:, s_], p_[:, 0:QTW],
                            ACT.Exp, bias=scal[:, 19:20], scale=0.125), wait=tuple(w2),
                            drain=False)
                        plo_c[kb] = vop(lambda e, s_=sl: e.tensor_tensor(
                            PloB[:, s_], expbuf[:, s_], PhiB[:, s_], ALU.subtract),
                            wait=(("SE", pf_c[kb]),), drain=False)

                    def emit_av(kb):
                        sl = slice((kb % 2) * 512, (kb % 2) * 512 + QTW)
                        wav = [("SE", exp_c[kb])]
                        if kb == 0:
                            wav.append(("VE", psA_free[h]))
                        top(lambda e, k=kb, pA=psA, V_=VH, s_=sl: e.matmul(
                            pA[0:65, 0:QTW], V_[:, k * 65:(k + 1) * 65],
                            PhiB[:, s_], start=(k == 0), stop=False),
                            wait=tuple(wav), inc=None)
                        top(lambda e, k=kb, pA=psA, V_=VL, s_=sl: e.matmul(
                            pA[0:65, 0:QTW], V_[:, k * 65:(k + 1) * 65],
                            PhiB[:, s_], start=False, stop=False), inc=None)
                        av_c[kb] = top(lambda e, k=kb, pA=psA, V_=VH, s_=sl: e.matmul(
                            pA[0:65, 0:QTW], V_[:, k * 65:(k + 1) * 65],
                            PloB[:, s_], start=False, stop=(k == NT - 1)),
                            wait=(("VE", plo_c[kb]),))

                    emit_s(0)
                    emit_s(1)
                    for kb in range(2, NT):
                        emit_av(kb - 2)
                        emit_s(kb)
                    emit_av(NT - 2)
                    emit_av(NT - 1)
                    prev_exp = [pf_c[NT - 2], pf_c[NT - 1]]
                    prev_av = [av_c[NT - 2], av_c[NT - 1]]
                    prev_plo = [plo_c[NT - 2], plo_c[NT - 1]]
                    crc = vop(lambda e, pA=psA: e.reciprocal(rcpr[0:1, 0:QTW], pA[64:65, 0:QTW]),
                              wait=(("TE", av_c[NT - 1]), ("VE", S.c("VE"))))
                    crx = top(lambda e: e.matmul(psT[0:64, 0:QTW], onesc[0:1, 0:64],
                                                 rcpr[0:1, 0:QTW], start=True, stop=True),
                              wait=(("VE", crc),))
                    sf = sfm[h * 64:(h + 1) * 64, qt * QTW:(qt + 1) * QTW]
                    cev = vop(lambda e, s=sf, pA=psA: e.tensor_copy(s, pA[0:64, 0:QTW]))
                    psA_free[h] = vop(
                        lambda e, s=sf: e.tensor_tensor(s, s, psT[0:64, 0:QTW], ALU.mult),
                        wait=(("TE", crx),))
                    s_idx = h * NQT + qt
                    wtarget = ((g * 2 * NQT + s_idx + 1) * len(p0_work)) // (NP * 2 * NQT)
                    while p0_state["widx"] < wtarget:
                        weave_slot()

            wl0 = lepw[:, g * 3:g * 3 + 1]; wl1 = lepw[:, g * 3 + 1:g * 3 + 2]
            wl2 = lepw[:, g * 3 + 2:g * 3 + 3]; bl = lepb[:, g:g + 1]
            sf_all = sfm[:, 0:OWN]
            vop(lambda e, w_=wl0: e.scalar_tensor_tensor(sf_all, vtf[:, 0:OWN], w_, sf_all,
                                                         ALU.mult, ALU.add),
                wait=(("TE", S.c("TE")),))
            vop(lambda e, w_=wl1: e.scalar_tensor_tensor(sf_all, vtf[:, 1:OWN + 1], w_, sf_all,
                                                         ALU.mult, ALU.add))
            vop(lambda e, w_=wl2: e.scalar_tensor_tensor(sf_all, vtf[:, 2:OWN + 2], w_, sf_all,
                                                         ALU.mult, ALU.add))
            cfin = vop(lambda e, b_=bl: e.tensor_scalar(sf_all, sf_all, b_, None, ALU.add))
            for tt in range(NTO):
                ct = top(lambda e, t=tt: e.transpose(
                    psT[:, 0:128], sfm[:, t * 128:(t + 1) * 128], idf[:]),
                    wait=(("VE", cfin), ("SE", S.c("SE"))))
                cse = sop(lambda e: e.copy(sstg[:, 0:128], psT[:, 0:128]), wait=(("TE", ct),))
                stq(s_d[tt * 128:(tt + 1) * 128, g * 128:(g + 1) * 128], sstg[:, 0:128],
                    wait=(("SE", cse),))
                S.emit("scalar", None, wait=st_all())


        while p0_state["widx"] < len(p0_work):
            weave_slot()
        emit_p0_flush()

        S.barrier()

        cur2 = Cursor()
        xqTs = cur2.bf16(CC * OWN)
        hid = cur2.f32(2 * I)
        hq_t = cur2.bf16(I)
        hqT = cur2.bf16(2 * NI * 128)
        gstg = cur2.f32(2 * 1024)
        wg_a = cur2.bf16(CC * 512); wg_b = cur2.bf16(CC * 512)
        wu_a = cur2.bf16(CC * 512); wu_b = cur2.bf16(CC * 512)
        wd_a = cur2.bf16(NI * 128); wd_b = cur2.bf16(NI * 128)
        sgn2 = cur2.f32(I)
        blk2 = cur2.f32(256)
        chain2 = cur2.off

        xq3 = xqTs.rearrange("p (q s) -> p q s", q=CC)
        cxq = ld(xqTs, xqT_d[:])
        hid3 = hid.rearrange("p (t w) -> p t w", t=2)
        hqT3 = hqT.rearrange("p (t ic w) -> p t ic w", t=2, ic=NI)
        g3 = gstg.rearrange("p (t w) -> p t w", t=2)
        NDSL = 8
        gu_pref = {}
        p2s = {"wbuf_mm": {0: None, 1: None}, "se_pP0": 0, "se_pP1": 0,
               "ge_hd": 0}
        wdbuf_mm = {0: None, 1: None}
        wd_pref = {}
        se_pS = {0: 0, 1: 0}
        q_done = {0: 0, 1: 0}
        sp_guard = {0: None, 1: None}

        def issue_gu_load(k, isl):
            par = (k * NISL + isl) % 2
            wg = wg_a if par == 0 else wg_b
            wu = wu_a if par == 0 else wu_b
            ww = (p2s["wbuf_mm"][par],) if p2s["wbuf_mm"][par] else ()
            clg = ld(wg, wgq_d[:, isl * CC * 512:(isl + 1) * CC * 512], wait=ww)
            clu = ld(wu, wuq_d[:, isl * CC * 512:(isl + 1) * CC * 512], wait=ww)
            gu_pref[(k, isl)] = (clg, clu, wg, wu)

        def issue_wd_load(k, csl):
            par = csl % 2
            wd = wd_a if par == 0 else wd_b
            ww = (wdbuf_mm[par],) if wdbuf_mm[par] else ()
            cld = ld(wd, wdq_d[:, csl * NI * 128:(csl + 1) * NI * 128], wait=ww)
            wd_pref[(k, csl)] = (cld, wd)

        issue_gu_load(0, 0)
        issue_wd_load(0, 0)
        issue_wd_load(0, 1)
        hd_last = {0: 0, 1: 0}

        def emit_gu(k):
            par = k % 2
            for isl in range(NISL):
                clg, clu, wg, wu = gu_pref.pop((k, isl))
                wg3 = wg.rearrange("p (q s) -> p q s", q=CC)
                wu3 = wu.rearrange("p (q s) -> p q s", q=CC)
                wtm = [clg, clu, cxq, ("SE", p2s["se_pP1"])]
                cg = None
                for q in range(CC):
                    cg = top(lambda e, qq=q, k_=k, w3=wg3: e.matmul(
                        psP0[:, 0:512], xq3[:, qq, k_ * 128:(k_ + 1) * 128],
                        w3[:, qq, :], start=(qq == 0), stop=(qq == CC - 1)),
                        wait=tuple(wtm) if q == 0 else (),
                        inc="TE" if q == CC - 1 else None)
                cu = None
                for q in range(CC):
                    cu = top(lambda e, qq=q, k_=k, w3=wu3: e.matmul(
                        psP1[:, 0:512], xq3[:, qq, k_ * 128:(k_ + 1) * 128],
                        w3[:, qq, :], start=(qq == 0), stop=(qq == CC - 1)),
                        inc="TE" if q == CC - 1 else None)
                wsl = [("TE", cg)]
                if isl == 0 and q_done[par]:
                    wsl.append(("VE", q_done[par]))
                csl = sop(lambda e, p_=par, i_=isl: e.activation(
                    hid3[:, p_, i_ * 512:(i_ + 1) * 512], psP0[:, 0:512],
                    ACT.Silu, bias=0.0, scale=scal[:, 12:13]), wait=tuple(wsl))
                p2s["se_pP0"] = csl
                cup = sop(lambda e: e.activation(rcpr[:, 0:512], psP1[:, 0:512],
                                                 ACT.Copy, bias=0.0,
                                                 scale=scal[:, 13:14]),
                          wait=(("TE", cu), ("GE", p2s["ge_hd"])))
                p2s["se_pP1"] = cup
                hd = gop(lambda e, p_=par, i_=isl: e.tensor_tensor(
                    hid3[:, p_, i_ * 512:(i_ + 1) * 512],
                    hid3[:, p_, i_ * 512:(i_ + 1) * 512], rcpr[:, 0:512], ALU.mult),
                    wait=(("SE", cup),))
                p2s["ge_hd"] = hd
                p2s["wbuf_mm"][(k * NISL + isl) % 2] = ("TE", cu)
                if isl + 1 < NISL:
                    issue_gu_load(k, isl + 1)
                elif k + 1 < NTO:
                    issue_gu_load(k + 1, 0)
            hd_last[par] = hd

        emit_gu(0)
        for k in range(NTO):
            par = k % 2
            csg = sop(lambda e, p_=par: e.sign(sgn2, hid3[:, p_, :]),
                      wait=(("GE", hd_last[par]), ("VE", q_done[1 - par])))
            S.emit("vector", None, wait=(("SE", csg), ("TE", S.c("TE"))))
            cq = quant_tile(chain2, blk2, hid3[:, par, :], sgn2, hq_t, I,
                            scal[:, 4:5], scal[:, 5:6])
            q_done[par] = cq
            if k + 1 < NTO:
                emit_gu(k + 1)
            for ic in range(NI):
                hh = ic % 2
                ct = top(lambda e, i_=ic, h_=hh: e.transpose(
                    psB[:, h_ * 128:(h_ + 1) * 128],
                    hq_t[:, i_ * 128:(i_ + 1) * 128], idb[:]),
                    wait=(("VE", cq), ("SE", S.c("SE"))))
                sop(lambda e, p_=par, i_=ic, h_=hh: e.copy(
                    hqT3[:, p_, i_, :], psB[:, h_ * 128:(h_ + 1) * 128]),
                    wait=(("TE", ct),))
            se_cp = S.c("SE")
            for csl_i in range(NDSL):
                cld, wd = wd_pref.pop((k, csl_i))
                wd3 = wd.rearrange("p (ic s) -> p ic s", ic=NI)
                psS = psS0 if csl_i % 2 == 0 else psS1
                wtd = [cld, ("SE", se_cp), ("SE", se_pS[csl_i % 2])]
                cd = None
                for ic in range(NI):
                    cd = top(lambda e, p_=par, i_=ic, w3=wd3, ps=psS: e.matmul(
                        ps[:, 0:128], hqT3[:, p_, i_, :], w3[:, i_, :],
                        start=(i_ == 0), stop=(i_ == NI - 1)),
                        wait=tuple(wtd) if ic == 0 else (),
                        inc="TE" if ic == NI - 1 else None)
                wdbuf_mm[csl_i % 2] = ("TE", cd)
                if csl_i + 2 < NDSL:
                    issue_wd_load(k, csl_i + 2)
                elif k + 1 < NTO:
                    issue_wd_load(k + 1, csl_i + 2 - NDSL)
                wds = [("TE", cd)]
                if csl_i == 0 and sp_guard[par]:
                    wds.append(sp_guard[par])
                cds = sop(lambda e, p_=par, c_=csl_i, ps=psS: e.activation(
                    g3[:, p_, c_ * 128:(c_ + 1) * 128], ps[:, 0:128],
                    ACT.Silu, bias=0.0, scale=scal[:, 14:15]), wait=tuple(wds))
                se_pS[csl_i % 2] = cds
            sp_guard[par] = stq(g_d[k * 128:(k + 1) * 128, :], g3[:, par, :],
                                wait=(("SE", S.c("SE")),))

        S.barrier()

        cur4 = Cursor(0)
        st_a = cur4.f32(1024); st_b = cur4.f32(1024)
        xt4_a = cur4.f32(1024); xt4_b = cur4.f32(1024)
        ot4_a = cur4.f32(1024); ot4_b = cur4.f32(1024)
        gt4_a = cur4.f32(1024); gt4_b = cur4.f32(1024)
        for tt in range(NTO):
            sbuf_s = st_a if tt % 2 == 0 else st_b
            sbuf_x = xt4_a if tt % 2 == 0 else xt4_b
            obuf = ot4_a if tt % 2 == 0 else ot4_b
            gbuf4 = gt4_a if tt % 2 == 0 else gt4_b
            wv = (("VE", S.c("VE")),) if tt >= 2 else ()
            cl1 = ld(sbuf_s, s_d[tt * 128:(tt + 1) * 128, :], wait=wv)
            cl3 = ld(gbuf4, g_d[tt * 128:(tt + 1) * 128, :], wait=wv)
            cl2 = ld(sbuf_x, x_full[tt * 128:(tt + 1) * 128, :], wait=wv)
            wsp = st_all() if tt >= 2 else ()
            vop(lambda e, s=sbuf_s, g_=gbuf4: e.tensor_tensor(s, s, g_, ALU.mult),
                wait=(cl1, cl3) + tuple(wsp))
            cfin = vop(lambda e, s=sbuf_s, x=sbuf_x, o=obuf: e.tensor_tensor(
                o, x, s, ALU.add), wait=(cl2,))
            stq(out_t[tt * 128:(tt + 1) * 128, :], obuf, wait=(("VE", cfin),))

        S.emit("gpsimd", None, wait=st_all())

        nc._script = S
        with nc.Block() as block:
            def make_runner(eng_name):
                def runner(eng):
                    for fn, waits, inc, inc_by, drain in S.ops[eng_name]:
                        for sname, val in waits:
                            if val > 0:
                                eng.wait_ge(sems[sname], val)
                        if drain and fn is not None:
                            eng.drain()
                        if fn is None:
                            continue
                        ins = fn(eng)
                        if inc is not None and ins is not None:
                            ins.then_inc(sems[inc], inc_by)
                return runner
            block.sync(make_runner("sync"))
            block.tensor(make_runner("tensor"))
            block.vector(make_runner("vector"))
            block.scalar(make_runner("scalar"))
            block.gpsimd(make_runner("gpsimd"))

    return nc


def _rope_tables(n, d=64, base=10000.0):
    try:
        import jax
        import jax.numpy as jnp
        cpu = jax.devices("cpu")[0]
        with jax.default_device(cpu):
            position = jnp.arange(n, dtype=jnp.float32)[:, None]
            div_term = jnp.exp(jnp.arange(0, d, 2, dtype=jnp.float32)
                               * (-np.log(base) / d))
            ang = position * div_term
            return np.asarray(jnp.sin(ang)), np.asarray(jnp.cos(ang))
    except Exception:
        position = np.arange(n, dtype=np.float32)[:, None]
        div_term = np.exp(np.arange(0, d, 2, dtype=np.float32)
                          * np.float32(-np.log(base) / d)).astype(np.float32)
        ang = (position * div_term).astype(np.float32)
        return np.sin(ang).astype(np.float32), np.cos(ang).astype(np.float32)


def _tile_tables(tab, NT):
    return np.ascontiguousarray(
        tab.reshape(NT, 128, 32).transpose(1, 0, 2).reshape(128, NT * 32))


def make_core_inputs(x, w_qkv, b_qkv, w_lepe, b_lepe, gate_w, up_w, down_w,
                     s_in, s_in_down, b, j, seq):
    OWN = seq // 2
    NT = seq // 128
    roll = j * OWN
    xr = np.ascontiguousarray(np.roll(np.asarray(x[b]), -roll, axis=0)).astype(np.float32)
    sin_full, cos_full = _rope_tables(seq)
    sin_r = np.roll(sin_full, -roll, axis=0)
    cos_r = np.roll(cos_full, -roll, axis=0)
    D = 1024
    f32 = np.float32
    wq = np.ascontiguousarray(np.asarray(w_qkv[0:D]).T).astype(f32)
    wk = np.ascontiguousarray(np.asarray(w_qkv[D:2 * D]).T).astype(f32)
    wv = np.ascontiguousarray(np.asarray(w_qkv[2 * D:3 * D]).T).astype(f32)
    lw = np.ascontiguousarray(
        np.asarray(w_lepe).reshape(8, 128, 3).transpose(1, 0, 2).reshape(128, 24)).astype(f32)
    lb = np.ascontiguousarray(
        np.asarray(b_lepe).reshape(8, 128).T).astype(f32)
    maskL = 1.0 if j == 1 else 0.0
    maskR = 1.0 if j == 0 else 0.0
    return {
        "x_full": xr,
        "wqT": wq, "wkT": wk, "wvT": wv,
        "bias_b": np.ascontiguousarray(np.broadcast_to(np.asarray(b_qkv, f32), (128, 3 * D))),
        "gate_w": np.asarray(gate_w, f32), "up_w": np.asarray(up_w, f32),
        "down_w": np.asarray(down_w, f32),
        "s_in_b": np.full((128, 1), np.asarray(s_in).reshape(-1)[0], f32),
        "s_ind_b": np.full((128, 1), np.asarray(s_in_down).reshape(-1)[0], f32),
        "lepe_w": lw, "lepe_b": lb,
        "sin_t": _tile_tables(sin_r, NT), "cos_t": _tile_tables(cos_r, NT),
        "masks": np.ascontiguousarray(np.broadcast_to(
            np.array([maskL, maskR], f32), (128, 2))),
        "ident_f": np.eye(128, dtype=f32),
        "ident_b": np.eye(128).astype(ml_dtypes.bfloat16),
        "ones_in": np.ones((128, 128), f32),
    }


def kernel(L0_attn_x, L0_attn_w_qkv, L0_attn_b_qkv, L0_attn_w_lepe, L0_attn_b_lepe,
           L0_ffn_gate_w, L0_ffn_up_w, L0_ffn_down_w, L0_ffn_s_in, L0_ffn_s_in_down):
    SEQ = 4096
    OWN = SEQ // 2
    x = np.asarray(L0_attn_x, np.float32)
    in_maps = []
    for core in range(8):
        b, j = core // 2, core % 2
        in_maps.append(make_core_inputs(
            x, L0_attn_w_qkv, L0_attn_b_qkv, L0_attn_w_lepe, L0_attn_b_lepe,
            L0_ffn_gate_w, L0_ffn_up_w, L0_ffn_down_w, L0_ffn_s_in, L0_ffn_s_in_down,
            b, j, SEQ))
    nc = build_nc(seq=SEQ, drains=True)
    res = run_bass_kernel_spmd(nc, in_maps, core_ids=list(range(8)))
    out = np.empty((4, SEQ, 1024), np.float32)
    for core in range(8):
        b, j = core // 2, core % 2
        out[b, j * OWN:(j + 1) * OWN, :] = np.asarray(res.results[core]["out_s"])
    return out


# version 31
# speedup vs baseline: 1.0383x; 1.0087x over previous
import numpy as np
import ml_dtypes
import concourse.bass as bass
import concourse.mybir as mybir
from concourse.bass_utils import run_bass_kernel_spmd
from contextlib import ExitStack

F32 = mybir.dt.float32
BF16 = mybir.dt.bfloat16
ALU = mybir.AluOpType
ACT = mybir.ActivationFunctionType
AX = mybir.AxisListType

FP8_MAX = 448.0
FP4_MAX = 6.0
GSC = FP8_MAX * FP4_MAX
EXP_BIAS = -8.0

ENGINES = ("sync", "tensor", "vector", "scalar", "gpsimd")


class Script:

    def __init__(self):
        self.ops = {e: [] for e in ENGINES}
        self.counts = {}

    def emit(self, eng, fn, wait=(), inc=None, inc_by=1, drain=False):
        self.ops[eng].append((fn, tuple(wait), inc, inc_by, drain))
        if inc is not None:
            self.counts[inc] = self.counts.get(inc, 0) + inc_by
            return self.counts[inc]
        return None

    def c(self, sem):
        return self.counts.get(sem, 0)

    def barrier(self):
        snap = tuple((s, v) for s, v in self.counts.items() if v > 0)
        for e in ENGINES:
            self.emit(e, None, wait=snap)


def build_nc(seq=4096, drains=False, dbg=False):
    SEQ = seq
    OWN = SEQ // 2
    NT = SEQ // 128
    NTO = OWN // 128
    D = 1024
    CC = 8
    NP = 8
    I = 4096
    NI = I // 128
    NISL = I // 512
    CHT = min(2, NTO)
    NCH = NTO // CHT
    NQT = max(OWN // 512, 1)
    QTW = min(512, OWN)
    GRP = 4 if NT % 4 == 0 else 1

    nc = bass.Bass("TRN2", target_bir_lowering=False)

    def din(name, shape, dt=F32):
        return nc.dram_tensor(name, shape, dt, kind="ExternalInput").ap()

    x_full = din("x_full", [SEQ, D])
    wqT = din("wqT", [D, D]); wkT = din("wkT", [D, D]); wvT = din("wvT", [D, D])
    bias_b = din("bias_b", [128, 3 * D])
    gate_w = din("gate_w", [I, D]); up_w = din("up_w", [I, D]); down_w = din("down_w", [D, I])
    s_in_b = din("s_in_b", [128, 1]); s_ind_b = din("s_ind_b", [128, 1])
    lepe_w = din("lepe_w", [128, NP * 3]); lepe_b = din("lepe_b", [128, NP])
    sin_t = din("sin_t", [128, NT * 32]); cos_t = din("cos_t", [128, NT * 32])
    masks = din("masks", [128, 2])
    ident_f = din("ident_f", [128, 128]); ident_b = din("ident_b", [128, 128], BF16)
    ones_in = din("ones_in", [128, 128])

    out_t = nc.dram_tensor("out_s", [OWN, D], F32, kind="ExternalOutput").ap()

    KIND = "ExternalOutput" if dbg else "Internal"
    wgq_d = nc.dram_tensor("wgq_d", [128, CC * I], BF16, kind=KIND).ap()
    wuq_d = nc.dram_tensor("wuq_d", [128, CC * I], BF16, kind=KIND).ap()
    wdq_d = nc.dram_tensor("wdq_d", [128, NI * D], BF16, kind=KIND).ap()
    xqT_d = nc.dram_tensor("xqT_d", [128, CC * OWN], BF16, kind=KIND).ap()
    xnT_d = nc.dram_tensor("xnT_d", [128, CC * SEQ], F32, kind="Internal").ap()
    s_d = nc.dram_tensor("s_d", [OWN, D], F32, kind=KIND).ap()
    g_d = nc.dram_tensor("g_d", [OWN, D], F32, kind=KIND).ap()
    if dbg:
        dbg_xnt = nc.dram_tensor("dbg_xnt", [128, CC * SEQ], F32, kind="ExternalOutput").ap()

    ctx = ExitStack()
    with ctx:
        def sbuf(name, shape, dt=F32):
            return ctx.enter_context(nc.sbuf_tensor(name, shape, dt))

        sinc = sbuf("sinc", [128, NT * 32]); cosc = sbuf("cosc", [128, NT * 32])
        lepw = sbuf("lepw", [128, NP * 3]); lepb = sbuf("lepb", [128, NP])
        maskc = sbuf("maskc", [128, 2])
        idf = sbuf("idf", [128, 128]); idb = sbuf("idb", [128, 128], BF16)
        onesc = sbuf("onesc", [128, 128])
        scal = sbuf("scal", [128, 32])
        expbuf = sbuf("expbuf", [128, 1024])
        biasg = sbuf("biasg", [128, 384])
        rcpr = sbuf("rcpr", [128, 512])

        rem = nc.sbuf_bytes_remaining
        acols = (rem // 4 // 128) * 128 - 128
        arena = sbuf("arena", [128, acols])
        arena_b = arena[:].bitcast(BF16)

        def psum(name, dt=F32, cols=512):
            return ctx.enter_context(nc.psum_tensor(name, [128, cols], dt))
        psS0 = psum("psS0"); psS1 = psum("psS1")
        psA0 = psum("psA0"); psA1 = psum("psA1")
        psP0 = psum("psP0"); psP1 = psum("psP1")
        psT = psum("psT")
        psB = psum("psB", BF16, 1024)

        sem_names = (["TE", "VE", "SE", "GE"] + [f"L{i}" for i in range(8)]
                     + [f"S{i}" for i in range(4)])
        sems = {s: ctx.enter_context(nc.semaphore(s)) for s in sem_names}

        S = Script()

        rr = {"ld": 0, "st": 0}

        def ld(dst_ap, src_ap, wait=()):
            sem = f"L{rr['ld'] % 8}"
            rr["ld"] += 1
            w = tuple(wait)
            prev = S.c(sem)
            if prev:
                w = w + ((sem, prev),)
            v = S.emit("sync", lambda e, d=dst_ap, s=src_ap: e.dma_start(out=d, in_=s),
                       wait=w, inc=sem, inc_by=16)
            return (sem, v)

        def ld_all():
            return tuple((f"L{i}", S.c(f"L{i}")) for i in range(8) if S.c(f"L{i}"))

        def stq(dst_ap, src_ap, wait=()):
            sem = f"S{rr['st'] % 4}"
            rr["st"] += 1
            w = tuple(wait)
            prev = S.c(sem)
            if prev:
                w = w + ((sem, prev),)
            v = S.emit("gpsimd", lambda e, d=dst_ap, s=src_ap: e.dma_start(out=d, in_=s),
                       wait=w, inc=sem, inc_by=16)
            return (sem, v)

        def st_all():
            return tuple((f"S{i}", S.c(f"S{i}")) for i in range(4) if S.c(f"S{i}"))

        dr_v = drains in (True, "all")
        dr_s = drains in (True, "all", "scalar")

        def vop(fn, wait=(), inc="VE", drain=None):
            return S.emit("vector", fn, wait=wait, inc=inc,
                          drain=dr_v if drain is None else drain)

        def sop(fn, wait=(), inc="SE", drain=None):
            return S.emit("scalar", fn, wait=wait, inc=inc,
                          drain=dr_s if drain is None else drain)

        def top(fn, wait=(), inc="TE"):
            return S.emit("tensor", fn, wait=wait, inc=inc)

        def gop(fn, wait=(), inc="GE", drain=None):
            return S.emit("gpsimd", fn, wait=wait, inc=inc,
                          drain=dr_v if drain is None else drain)

        class Cursor:
            def __init__(self, off=0):
                self.off = off
            def f32(self, n):
                assert self.off + n <= acols, f"arena overflow: {self.off}+{n} > {acols}"
                ap = arena[:, self.off:self.off + n]
                self.off += n
                return ap
            def bf16(self, n):
                assert n % 2 == 0
                assert self.off + n // 2 <= acols, "arena overflow (bf16)"
                ap = arena_b[:, 2 * self.off:2 * self.off + n]
                self.off += n // 2
                return ap

        ld(sinc[:], sin_t[:]); ld(cosc[:], cos_t[:])
        ld(lepw[:], lepe_w[:]); ld(lepb[:], lepe_b[:])
        ld(maskc[:], masks[:]); ld(idf[:], ident_f[:]); ld(idb[:], ident_b[:])
        ld(onesc[:], ones_in[:])
        ld(scal[:, 0:1], s_in_b[:]); ld(scal[:, 1:2], s_ind_b[:])
        c0all = ld_all()
        vop(lambda e: e.tensor_scalar(scal[:, 2:3], scal[:, 0:1], 1.0 / 6.0, None, ALU.mult),
            wait=c0all)
        vop(lambda e: e.tensor_scalar(scal[:, 3:4], scal[:, 0:1], 2.0, None, ALU.mult))
        vop(lambda e: e.tensor_scalar(scal[:, 4:5], scal[:, 1:2], 1.0 / 6.0, None, ALU.mult))
        vop(lambda e: e.tensor_scalar(scal[:, 5:6], scal[:, 1:2], 2.0, None, ALU.mult))
        vop(lambda e: e.memset(scal[:, 18:19], 1e-5))
        vop(lambda e: e.memset(scal[:, 19:20], EXP_BIAS))
        c_consts = ld_all()
        for _e in ("tensor", "scalar", "gpsimd", "vector"):
            S.emit(_e, None, wait=c_consts)

        def quant_block_chain(cur, am, gs6_ap, gs2_ap, nb):
            r2 = cur.f32(nb); s8h = cur.f32(nb)
            sc = cur.f32(nb); p = cur.f32(nb); pinv = cur.f32(nb)
            t1 = cur.f32(nb); t2 = cur.f32(nb); fl = cur.f32(nb)
            vop(lambda e: e.tensor_scalar(sc, am, gs6_ap, 448.0, ALU.mult, ALU.min))
            vop(lambda e: e.tensor_scalar(t1, sc, 4.0, None, ALU.is_ge))
            vop(lambda e: e.tensor_scalar(p, t1, 255.0, 1.0, ALU.mult, ALU.add))
            vop(lambda e: e.tensor_scalar(pinv, t1, (2.0 ** -8) - 1.0, 1.0, ALU.mult, ALU.add))
            for mul, thr in ((15.0, 0.25), (3.0, 2.0 ** -4), (1.0, 2.0 ** -5)):
                vop(lambda e, th=thr: e.tensor_scalar(t2, p, th, None, ALU.mult))
                vop(lambda e: e.tensor_tensor(t1, sc, t2, ALU.is_ge))
                vop(lambda e, m=mul: e.tensor_scalar(t2, t1, m, 1.0, ALU.mult, ALU.add))
                vop(lambda e: e.tensor_tensor(p, p, t2, ALU.mult))
                vop(lambda e, m=mul: e.tensor_scalar(t2, t1, 1.0 / (m + 1.0) - 1.0, 1.0,
                                                     ALU.mult, ALU.add))
                vop(lambda e: e.tensor_tensor(pinv, pinv, t2, ALU.mult))
            vop(lambda e: e.scalar_tensor_tensor(t1, sc, 512.0, pinv, ALU.mult, ALU.mult))
            vop(lambda e: e.tensor_scalar(t1, t1, 0.5, None, ALU.add))
            vop(lambda e: e.tensor_scalar(t2, t1, 16.0, 16.0, ALU.is_ge, ALU.mult))
            vop(lambda e: e.tensor_tensor(t1, t1, t2, ALU.subtract))
            vop(lambda e: e.tensor_copy(fl, t2))
            vop(lambda e: e.tensor_scalar(t2, t1, 8.0, 8.0, ALU.is_ge, ALU.mult))
            vop(lambda e: e.tensor_tensor(t1, t1, t2, ALU.subtract))
            vop(lambda e: e.tensor_tensor(fl, fl, t2, ALU.add))
            vop(lambda e: e.tensor_scalar(t2, t1, 4.0, 4.0, ALU.is_ge, ALU.mult))
            vop(lambda e: e.tensor_tensor(t1, t1, t2, ALU.subtract))
            vop(lambda e: e.tensor_tensor(fl, fl, t2, ALU.add))
            vop(lambda e: e.tensor_scalar(t2, t1, 2.0, 2.0, ALU.is_ge, ALU.mult))
            vop(lambda e: e.tensor_tensor(t1, t1, t2, ALU.subtract))
            vop(lambda e: e.tensor_tensor(fl, fl, t2, ALU.add))
            vop(lambda e: e.scalar_tensor_tensor(fl, t1, 1.0, fl, ALU.is_ge, ALU.add))
            vop(lambda e: e.scalar_tensor_tensor(sc, fl, 2.0 ** -9, p, ALU.mult, ALU.mult))
            vop(lambda e: e.tensor_scalar(s8h, sc, 0.5, None, ALU.mult))
            vop(lambda e: e.tensor_single_scalar(t1, sc, 0.0, ALU.is_equal))
            vop(lambda e: e.tensor_tensor(t2, sc, t1, ALU.max))
            vop(lambda e: e.reciprocal(t2, t2))
            vop(lambda e: e.tensor_scalar(t2, t2, gs2_ap, None, ALU.mult))
            vop(lambda e: e.tensor_scalar(fl, t1, -1.0, 1.0, ALU.mult, ALU.add))
            vop(lambda e: e.tensor_tensor(t2, t2, fl, ALU.mult))
            c = vop(lambda e: e.scalar_tensor_tensor(r2, t1, 2.0, t2, ALU.mult, ALU.add))
            return r2, s8h, c

        def quant_elem_chain(cur, w_ap, sgn_ap, r2, s8h, out_ap, ncols, nb):
            u = cur.f32(ncols); A = cur.f32(ncols); Bv = cur.f32(ncols)
            t = cur.f32(ncols); h = cur.f32(ncols); c = cur.f32(ncols)
            r2b = r2.unsqueeze(2).broadcast_to([128, nb, 16])
            s8hb = s8h.unsqueeze(2).broadcast_to([128, nb, 16])
            u3 = u.rearrange("p (g b) -> p g b", b=16)
            vop(lambda e: e.scalar_tensor_tensor(u, w_ap, -1.0, w_ap, ALU.mult, ALU.max))
            vop(lambda e: e.tensor_tensor(u3, u3, r2b, ALU.mult))
            vop(lambda e: e.tensor_scalar(u, u, 12.0, None, ALU.min))
            vop(lambda e: e.tensor_scalar(A, u, 4.0, 0.5, ALU.is_ge, ALU.mult))
            vop(lambda e: e.tensor_scalar(Bv, u, 8.0, 0.25, ALU.is_ge, ALU.mult))
            vop(lambda e: e.tensor_tensor(t, A, Bv, ALU.add))
            vop(lambda e: e.scalar_tensor_tensor(h, Bv, 3.0, t, ALU.mult, ALU.add))
            vop(lambda e: e.tensor_scalar(t, t, -1.0, 1.0, ALU.mult, ALU.add))
            vop(lambda e: e.tensor_scalar(A, h, 2.0, 1.0, ALU.mult, ALU.add))
            vop(lambda e: e.scalar_tensor_tensor(h, u, 0.5, h, ALU.add, ALU.add))
            vop(lambda e: e.tensor_tensor(h, h, t, ALU.mult))
            vop(lambda e: e.tensor_scalar(c, h, 1.0, None, ALU.is_ge))
            vop(lambda e: e.scalar_tensor_tensor(c, h, 2.0, c, ALU.is_ge, ALU.add))
            vop(lambda e: e.scalar_tensor_tensor(c, h, 3.0, c, ALU.is_ge, ALU.add))
            vop(lambda e: e.scalar_tensor_tensor(c, h, 4.0, c, ALU.is_ge, ALU.add))
            vop(lambda e: e.tensor_tensor(c, c, A, ALU.mult))
            vop(lambda e: e.tensor_tensor(c, c, sgn_ap, ALU.mult))
            o3 = out_ap.rearrange("p (g b) -> p g b", b=16)
            c3 = c.rearrange("p (g b) -> p g b", b=16)
            return vop(lambda e: e.tensor_tensor(o3, c3, s8hb, ALU.mult))

        def quant_tile(chain_base, blk_ap, src_ap, sgn_ap, out_ap, C, gs6_ap, gs2_ap):
            nb = C // 16
            vop(lambda e: e.reduce_max(
                blk_ap[:, 0:nb], src_ap[:, 0:C].rearrange("p (g k) -> p g k", k=16),
                axis=AX.X, apply_absolute_value=True))
            curq = Cursor(chain_base)
            r2, s8h, _ = quant_block_chain(curq, blk_ap[:, 0:nb], gs6_ap, gs2_ap, nb)
            elem_base = chain_base + 2 * nb
            cq = None
            for sub in range(C // 1024):
                sl = slice(sub * 1024, (sub + 1) * 1024)
                bsl = slice(sub * 64, (sub + 1) * 64)
                cq = quant_elem_chain(Cursor(elem_base),
                                      src_ap[:, sl], sgn_ap[:, sl],
                                      r2[:, bsl], s8h[:, bsl],
                                      out_ap[:, sl], 1024, 64)
            return cq

        def cross_partition_max(src_col, dst_col, tmp_row):
            ct = top(lambda e: e.transpose(psT[0:1, 0:128], src_col, idf[:]),
                     wait=(("VE", S.c("VE")), ("SE", S.c("SE"))))
            cv = vop(lambda e: e.tensor_copy(tmp_row[0:1, 0:128], psT[0:1, 0:128]),
                     wait=(("TE", ct),))
            cv = vop(lambda e: e.reduce_max(tmp_row[0:1, 0:1], tmp_row[0:1, 0:128], axis=AX.X))
            ct = top(lambda e: e.matmul(psT[:, 0:1], onesc[0:1, :], tmp_row[0:1, 0:1],
                                        start=True, stop=True), wait=(("VE", cv),))
            return vop(lambda e: e.tensor_copy(dst_col, psT[:, 0:1]), wait=(("TE", ct),))

        P0_NEED = 4096 + 4096 + 2048 + 2048 + 512 + 2 * 256 + 6 * 1024
        p0_cur = Cursor(acols - P0_NEED)
        wt_a = p0_cur.f32(4096)
        sgn0 = p0_cur.f32(4096)
        wq_a = p0_cur.bf16(4096)
        stage0 = p0_cur.bf16(4096)
        blk0 = p0_cur.f32(512)
        chain_base0 = p0_cur.off

        wcfg = [
            (gate_w, wgq_d, I, D, 6, 9, 15, 12, 0),
            (up_w, wuq_d, I, D, 7, 10, 16, 13, 0),
            (down_w, wdq_d, D, I, 8, 11, 17, 14, 1),
        ]
        p0_state = {"last_sp": None, "last_q": None, "last_ct": None,
                    "pending": None, "widx": 0}

        def emit_p0_passA():
            for (src_, dst, rows, C, sl6, sl2, slm, slr, sli) in wcfg:
                nrt = rows // 128
                gmax = scal[:, slm:slm + 1]
                last_v = S.c("VE")
                for ot in range(nrt):
                    cl = ld(wt_a[:, 0:C], src_[ot * 128:(ot + 1) * 128, :],
                            wait=(("VE", last_v),))
                    if ot == 0:
                        last_v = vop(lambda e, cc=C, g_=gmax: e.reduce_max(
                            g_, wt_a[:, 0:cc], axis=AX.X, apply_absolute_value=True),
                            wait=(cl,))
                    else:
                        vop(lambda e, cc=C: e.reduce_max(
                            blk0[:, 0:1], wt_a[:, 0:cc], axis=AX.X,
                            apply_absolute_value=True), wait=(cl,))
                        last_v = vop(lambda e, g_=gmax: e.tensor_tensor(
                            g_, g_, blk0[:, 0:1], ALU.max))
                cross_partition_max(gmax, gmax, rcpr)
                vop(lambda e, g_=gmax: e.reciprocal(blk0[:, 0:1], g_))
                vop(lambda e, a=sl6: e.tensor_scalar(scal[:, a:a + 1], blk0[:, 0:1],
                                                     GSC / 6.0, None, ALU.mult))
                vop(lambda e, a=sl2: e.tensor_scalar(scal[:, a:a + 1], blk0[:, 0:1],
                                                     GSC * 2.0, None, ALU.mult))
                vop(lambda e, s=sli: e.reciprocal(blk0[:, 1:2], scal[:, s:s + 1]))
                vop(lambda e, r=slr, g_=gmax: e.scalar_tensor_tensor(
                    scal[:, r:r + 1], g_, 1.0 / GSC, blk0[:, 1:2], ALU.mult, ALU.mult))

        p0_work = []
        for wi, (src_, dst, rows, C, sl6, sl2, slm, slr, sli) in enumerate(wcfg):
            for ot in range(rows // 128):
                p0_work.append((src_, dst, C, sl6, sl2, ot))

        def emit_p0_quant(src_, dst, C, sl6, sl2, ot):
            last_sp = p0_state["last_sp"]
            wsp0 = (last_sp,) if last_sp else ()
            wq = (("VE", p0_state["last_q"]),) if p0_state["last_q"] else (("VE", S.c("VE")),)
            cl = ld(wt_a[:, 0:C], src_[ot * 128:(ot + 1) * 128, :], wait=wq + wsp0)
            csg = sop(lambda e, cc=C: e.sign(sgn0[:, 0:cc], wt_a[:, 0:cc]), wait=(cl,))
            wtr = (("TE", p0_state["last_ct"]),) if p0_state["last_ct"] else ()
            S.emit("vector", None, wait=(("SE", csg), cl) + wtr)
            cq = quant_tile(chain_base0, blk0, wt_a, sgn0, wq_a, C,
                            scal[:, sl6:sl6 + 1], scal[:, sl2:sl2 + 1])
            p0_state["last_q"] = cq
            S.emit("scalar", None, wait=(("VE", cq),))
            p0_state["pending"] = (dst, C, ot, cq)

        def emit_p0_flush():
            if p0_state["pending"] is None:
                return
            dst, C, ot, cq = p0_state["pending"]
            p0_state["pending"] = None
            last_sp = p0_state["last_sp"]
            wsp0 = (last_sp,) if last_sp else ()
            ncc = C // 128
            ct = None
            for q in range(ncc):
                hh = q % 2
                ct = top(lambda e, qq=q, h_=hh: e.transpose(
                    psB[:, h_ * 128:(h_ + 1) * 128],
                    wq_a[:, qq * 128:(qq + 1) * 128], idb[:]),
                    wait=(("VE", cq), ("SE", S.c("SE"))) + wsp0)
                sop(lambda e, qq=q, h_=hh: e.copy(
                    stage0[:, qq * 128:(qq + 1) * 128],
                    psB[:, h_ * 128:(h_ + 1) * 128]), wait=(("TE", ct),))
            p0_state["last_ct"] = ct
            if C == 1024:
                dv4 = dst.rearrange("p (il q r) -> p il q r", il=NISL, q=ncc, r=512)
                dview = dv4[:, ot // 4, :, (ot % 4) * 128:(ot % 4 + 1) * 128]
            else:
                dview = dst.rearrange("p (db ic r) -> p db ic r",
                                      db=8, ic=NI, r=128)[:, ot, :, :]
            sview = stage0[:, 0:ncc * 128].rearrange("p (q r) -> p q r", q=ncc)
            p0_state["last_sp"] = stq(dview, sview, wait=(("SE", S.c("SE")),))

        def weave_slot():
            emit_p0_flush()
            if p0_state["widx"] < len(p0_work):
                emit_p0_quant(*p0_work[p0_state["widx"]])
                p0_state["widx"] += 1

        S.barrier()

        cur1 = Cursor()
        p3_base = 0
        xnstage = cur1.f32(1024)
        xt_a = cur1.f32(1024); xt_b = cur1.f32(1024)
        xn_t = cur1.f32(1024); sq_t = cur1.f32(1024)
        mv = cur1.f32(4)
        xq_t = cur1.bf16(1024)
        xqstage = cur1.bf16(1024)
        sgn1 = cur1.f32(1024)
        blk1 = cur1.f32(64)
        chain1 = cur1.off

        xnT3d = xnT_d.rearrange("p (q s) -> p q s", q=CC)
        xnst3 = xnstage.rearrange("p (q s) -> p q s", q=CC)
        prev_done = S.c("VE")
        prev_sp = None
        prev_sp_xn = None
        xn_guard = ()
        for tt in range(NT):
            buf = xt_a if tt % 2 == 0 else xt_b
            cl = ld(buf, x_full[tt * 128:(tt + 1) * 128, :],
                    wait=(("VE", prev_done),) + ((prev_sp,) if prev_sp else ()))
            vop(lambda e, b=buf: e.reduce_sum(mv[:, 0:1], b, axis=AX.X), wait=(cl,))
            vop(lambda e: e.tensor_scalar(mv[:, 0:1], mv[:, 0:1], 1.0 / D, None, ALU.mult))
            cve = vop(lambda e, b=buf: e.tensor_scalar(xn_t, b, mv[:, 0:1], None, ALU.subtract),
                       wait=xn_guard)
            sop(lambda e: e.activation(sq_t, xn_t, ACT.Square, bias=0.0, scale=1.0,
                                       accum_out=mv[:, 2:3]), wait=(("VE", cve),))
            cs2 = sop(lambda e: e.activation(mv[:, 3:4], mv[:, 2:3], ACT.Sqrt,
                                             bias=scal[:, 18:19], scale=1.0 / D))
            vop(lambda e: e.reciprocal(mv[:, 1:2], mv[:, 3:4]), wait=(("SE", cs2),))
            cxn = vop(lambda e: e.tensor_scalar(xn_t, xn_t, mv[:, 1:2], None, ALU.mult))
            for q in range(CC):
                wtr = [("VE", cxn), ("SE", S.c("SE"))]
                if q == 0 and prev_sp_xn:
                    wtr.append(prev_sp_xn)
                ct = top(lambda e, qq=q: e.transpose(
                    psT[:, 0:128], xn_t[:, qq * 128:(qq + 1) * 128], idf[:]),
                    wait=tuple(wtr))
                sop(lambda e, qq=q: e.copy(xnst3[:, qq, :], psT[:, 0:128]),
                    wait=(("TE", ct),))
            prev_sp_xn = stq(xnT3d[:, :, tt * 128:(tt + 1) * 128], xnst3,
                             wait=(("SE", S.c("SE")),))
            prev_done = cxn
            if tt < NTO:
                csg = sop(lambda e: e.sign(sgn1, xn_t), wait=(("VE", cxn),))
                S.emit("vector", None, wait=(("SE", csg),))
                cq = quant_tile(chain1, blk1, xn_t, sgn1, xq_t, 1024,
                                scal[:, 2:3], scal[:, 3:4])
                S.emit("scalar", None, wait=(("VE", cq),))
                for q in range(CC):
                    hh = q % 2
                    ct = top(lambda e, qq=q, h_=hh: e.transpose(
                        psB[:, h_ * 128:(h_ + 1) * 128],
                        xq_t[:, qq * 128:(qq + 1) * 128], idb[:]),
                        wait=(("VE", cq), ("SE", S.c("SE"))) + ((prev_sp,) if prev_sp else ()))
                    sop(lambda e, qq=q, h_=hh: e.copy(
                        xqstage[:, qq * 128:(qq + 1) * 128],
                        psB[:, h_ * 128:(h_ + 1) * 128]), wait=(("TE", ct),))
                dv = xqT_d[:, tt * 1024:(tt + 1) * 1024].rearrange(
                    "p (q s) -> p q s", q=CC)
                sv = xqstage.rearrange("p (q s) -> p q s", q=CC)
                prev_sp = stq(dv, sv, wait=(("SE", S.c("SE")),))
                prev_done = cq
            xn_guard = (("TE", S.c("TE")), ("SE", S.c("SE")))

        S.barrier()
        if dbg:
            stq(dbg_xnt[:], xnT)
            S.barrier()

        cur3 = Cursor(p3_base)
        xnb_a = cur3.f32(CC * GRP * 128); xnb_b = cur3.f32(CC * GRP * 128)
        KS1 = cur3.bf16(SEQ); KS2 = cur3.bf16(SEQ)
        QS1 = cur3.bf16(OWN); QS2 = cur3.bf16(OWN)
        QW1 = cur3.bf16(OWN); QW2 = cur3.bf16(OWN)
        VH1 = cur3.bf16(NT * 65); VL1 = cur3.bf16(NT * 65)
        VH2 = cur3.bf16(NT * 65); VL2 = cur3.bf16(NT * 65)
        phr = cur3.f32(1024)
        PhiB = phr[:, 0:512].bitcast(BF16)
        PloB = phr[:, 512:1024].bitcast(BF16)
        vtf = cur3.f32(OWN + 2)
        kvg_off = cur3.off
        kvg = cur3.f32(GRP * 256)
        ket = cur3.f32(GRP * 128)
        rtw = cur3.f32(GRP * 128)
        rt1 = rtw[:, 0:GRP * 64]; rt2 = rtw[:, GRP * 64:GRP * 128]
        sstg = rcpr[:, 0:128]
        assert cur3.off <= acols - P0_NEED, f"P3 region {cur3.off} overlaps P0 at {acols - P0_NEED}"
        assert GRP * 256 + GRP * 128 + 2 * GRP * 64 >= OWN, "sfm alias too small"
        sfm = arena[:, kvg_off:kvg_off + OWN]

        def kv_slot(q):
            if q < 4:
                return expbuf[:, q * 256:(q + 1) * 256]
            if q < 6:
                return rcpr[:, (q - 4) * 256:(q - 3) * 256]
            return phr[:, (q - 6) * 256:(q - 5) * 256]

        def wq_slot(q):
            return expbuf[:, q * 128:(q + 1) * 128]

        def proc_group(tiles, is_q):
            n = len(tiles)
            if is_q:
                src = kvg.rearrange("p (t w) -> p t w", t=GRP)[:, 0:n, 0:128]
                bslc = biasg[:, 0:128]
            else:
                src = kvg.rearrange("p (t w) -> p t w", t=GRP)[:, 0:n, 0:128]
                bslc = biasg[:, 128:256]
            ketv = ket.rearrange("p (t w) -> p t w", t=GRP)[:, 0:n, :]
            bb = bslc.unsqueeze(1).broadcast_to([128, n, 128])
            vop(lambda e: e.tensor_tensor(ketv, src, bb, ALU.add))
            kkv = kvg.rearrange("p (t w) -> p t w", t=GRP)[:, 0:n, 0:128]
            rtv = rtw[:, 0:n * 128]
            ketf = ket[:, 0:n * 128]
            ket3 = ket.rearrange("p (t w) -> p t w", t=GRP)[:, 0:n, :]
            cv = vop(lambda e: e.tensor_scalar(kkv, ket3, 0.0, None, ALU.min))
            cs = sop(lambda e: e.activation(rtv, kkv, ACT.Exp, bias=0.0, scale=1.0),
                     wait=(("VE", cv),))
            vop(lambda e: e.tensor_scalar(kkv, ket3, 0.0, None, ALU.max),
                wait=(("SE", cs),))
            rtv3 = rtv.rearrange("p (t w) -> p t w", t=n)
            vop(lambda e: e.tensor_tensor(ket3, kkv, rtv3, ALU.add))
            t0 = tiles[0]
            ket5 = ket.rearrange("p (t h j u) -> p t h j u", h=2, j=32, u=2)
            x1 = ket5[:, 0:n, :, :, 0]
            x2 = ket5[:, 0:n, :, :, 1]
            sv = sinc[:, t0 * 32:(t0 + n) * 32].rearrange("p (t j) -> p t j", t=n)
            svb = sv.unsqueeze(2).broadcast_to([128, n, 2, 32])
            cvv = cosc[:, t0 * 32:(t0 + n) * 32].rearrange("p (t j) -> p t j", t=n)
            cvb = cvv.unsqueeze(2).broadcast_to([128, n, 2, 32])
            r1v3 = rt1.rearrange("p (t h j) -> p t h j", t=GRP, h=2)[:, 0:n]
            r2v3 = rt2.rearrange("p (t h j) -> p t h j", t=GRP, h=2)[:, 0:n]
            dd = kvg.rearrange("p (t q) -> p t q", t=GRP)[:, 0:n, 0:128]
            de = dd.rearrange("p t (h u j) -> p t h u j", h=2, u=2, j=32)[:, :, :, 0, :]
            do = dd.rearrange("p t (h u j) -> p t h u j", h=2, u=2, j=32)[:, :, :, 1, :]
            vop(lambda e: e.tensor_tensor(r1v3, x1, cvb, ALU.mult))
            vop(lambda e: e.tensor_tensor(r2v3, x2, svb, ALU.mult))
            vop(lambda e: e.tensor_tensor(de, r1v3, r2v3, ALU.subtract))
            vop(lambda e: e.tensor_tensor(r1v3, x1, svb, ALU.mult))
            vop(lambda e: e.tensor_tensor(r2v3, x2, cvb, ALU.mult))
            cfin = vop(lambda e: e.tensor_tensor(do, r1v3, r2v3, ALU.add))
            stg = (rt1.bitcast(BF16).rearrange("p (t w) -> p t w", t=GRP)[:, 0:n, 0:128],
                   rt2.bitcast(BF16).rearrange("p (t w) -> p t w", t=GRP)[:, 0:n, 0:128])
            kr4 = kvg.rearrange("p (t w) -> p t w", t=GRP)[:, 0:n, 0:128]

            def tr_into(sg, dstT, tiles, dep):
                for i, tt in enumerate(tiles):
                    hh = i % 2
                    ct = top(lambda e, ii=i, s_=sg, h_=hh: e.transpose(
                        psB[:, h_ * 128:(h_ + 1) * 128], s_[:, ii, :], idb[:]),
                        wait=(("VE", dep), ("SE", S.c("SE"))))
                    sop(lambda e, t=tt, d=dstT, h_=hh: e.copy(
                        d[:, t * 128:(t + 1) * 128],
                        psB[:, h_ * 128:(h_ + 1) * 128]), wait=(("TE", ct),))

            for h in range(2):
                sg = stg[0]
                hs = slice(h * 64, (h + 1) * 64)
                vop(lambda e, s_=sg, k_=hs: e.tensor_copy(s_[:, :, 0:64], kr4[:, :, k_]),
                    wait=(("TE", S.c("TE")),))
                cfin2 = vop(lambda e, s_=sg, k_=hs: e.tensor_tensor(
                    s_[:, :, 64:128], kr4[:, :, k_], s_[:, :, 0:64], ALU.subtract))
                dstT = (QS1 if h == 0 else QS2) if is_q else (KS1 if h == 0 else KS2)
                tr_into(sg, dstT, tiles, cfin2)
                if is_q:
                    sw = stg[1]
                    vop(lambda e, a=sw, b=sg: e.tensor_copy(a[:, :, 0:64], b[:, :, 64:128]),
                        wait=(("TE", S.c("TE")),))
                    cfin3 = vop(lambda e, a=sw, b=sg: e.tensor_copy(a[:, :, 64:128],
                                                                    b[:, :, 0:64]))
                    tr_into(sw, QW1 if h == 0 else QW2, tiles, cfin3)
            return S.c("SE")

        emit_p0_passA()
        p0_per_pair = (len(p0_work) + NP - 1) // NP
        for g in range(NP):
            pb_te = S.c("TE"); pb_ve = S.c("VE"); pb_se = S.c("SE")
            for wi, wsrc in enumerate((wkT, wvT)):
                for q in range(CC):
                    ld(kv_slot(q)[:, wi * 128:(wi + 1) * 128],
                       wsrc[q * 128:(q + 1) * 128, g * 128:(g + 1) * 128],
                       wait=(("TE", pb_te), ("VE", pb_ve), ("SE", pb_se)) + st_all())
            ld(biasg[:, 0:128], bias_b[:, g * 128:(g + 1) * 128])
            ld(biasg[:, 128:256], bias_b[:, D + g * 128:D + (g + 1) * 128])
            cbl = ld(biasg[:, 256:384], bias_b[:, 2 * D + g * 128:2 * D + (g + 1) * 128])
            wld = ld_all()
            cv1 = vop(lambda e: e.memset(VH1, 1.0), wait=(("TE", pb_te),))
            vop(lambda e: e.memset(VH2, 1.0))
            vop(lambda e: e.memset(VL1, 0.0))
            cv2 = vop(lambda e: e.memset(VL2, 0.0))

            pbanks = (psP0, psP1, psS0, psS1)
            pb_evac = {0: ("VE", pb_ve), 1: ("VE", pb_ve),
                       2: ("VE", pb_ve), 3: ("VE", pb_ve)}
            xnb_use = {0: 0, 1: 0}
            for gi, g0 in enumerate(range(0, NT, GRP)):
                gts = list(range(g0, min(g0 + GRP, NT)))
                xnb = xnb_a if gi % 2 == 0 else xnb_b
                xnb3 = xnb.rearrange("p (q s) -> p q s", q=CC)
                cxl = ld(xnb3[:, :, 0:len(gts) * 128],
                         xnT3d[:, :, g0 * 128:(g0 + len(gts)) * 128],
                         wait=(("TE", xnb_use[gi % 2]),) + ((prev_sp_xn,) if gi == 0 else ()))
                for i, tt in enumerate(gts):
                    bi = i % 4
                    pp = pbanks[bi]
                    wt = list(wld) + [pb_evac[bi], cxl]
                    ctev = None
                    for q in range(CC):
                        ctev = top(lambda e, qq=q, ii=i, p_=pp, x_=xnb3: e.matmul(
                            p_[:, 0:256], x_[:, qq, ii * 128:(ii + 1) * 128],
                            kv_slot(qq), start=(qq == 0), stop=(qq == CC - 1)),
                            wait=tuple(wt) if q == 0 else (),
                            inc="TE" if q == CC - 1 else None)
                    pb_evac[bi] = ("VE", vop(lambda e, ii=i, p_=pp: e.tensor_copy(
                        kvg[:, ii * 256:(ii + 1) * 256], p_[:, 0:256]),
                        wait=(("TE", S.c("TE")),)))
                xnb_use[gi % 2] = S.c("TE")
                ng = len(gts)
                kv4 = kvg.rearrange("p (t w) -> p t w", t=GRP)[:, 0:ng, :]
                proc_group(gts, is_q=False)
                ketv = ket.rearrange("p (t w) -> p t w", t=GRP)[:, 0:ng, :]
                bvp = biasg[:, 256:384].unsqueeze(1).broadcast_to([128, ng, 128])
                vop(lambda e: e.tensor_tensor(ketv, kv4[:, :, 128:256], bvp, ALU.add),
                    wait=(("TE", S.c("TE")), ("SE", S.c("SE"))))
                for h, (VH, VL) in enumerate(((VH1, VL1), (VH2, VL2))):
                    hd = slice(h * 64, (h + 1) * 64)
                    vhd = VH.rearrange("p (t w) -> p t w", t=NT)[:, g0:g0 + ng, 0:64]
                    vld = VL.rearrange("p (t w) -> p t w", t=NT)[:, g0:g0 + ng, 0:64]
                    vop(lambda e, d_=vhd, s_=hd: e.tensor_copy(d_, ketv[:, :, s_]))
                    vop(lambda e, d_=vld, s_=hd, h_=vhd: e.tensor_tensor(
                        d_, ketv[:, :, s_], h_, ALU.subtract))
                for i, tt in enumerate(gts):
                    if not (tt <= NTO or tt == NT - 1):
                        continue
                    ctv = top(lambda e, ii=i: e.transpose(
                        psT[:, 0:128], ketv[:, ii, :], idf[:]),
                        wait=(("VE", S.c("VE")), ("SE", S.c("SE"))))
                    if tt < NTO:
                        sop(lambda e, t=tt: e.copy(
                            vtf[:, 1 + t * 128:1 + (t + 1) * 128], psT[:, 0:128]),
                            wait=(("TE", ctv),))
                    elif tt == NTO:
                        sop(lambda e: e.copy(vtf[:, OWN + 1:OWN + 2], psT[:, 0:1]),
                            wait=(("TE", ctv),))
                    if tt == NT - 1:
                        sop(lambda e: e.copy(vtf[:, 0:1], psT[:, 127:128]),
                            wait=(("TE", ctv),))
            kse = S.c("SE")

            for q in range(CC):
                ld(wq_slot(q), wqT[q * 128:(q + 1) * 128, g * 128:(g + 1) * 128],
                   wait=(("TE", S.c("TE")), ("SE", S.c("SE"))))
            wldq = ld_all()
            for gi, g0 in enumerate(range(0, NTO, GRP)):
                gts = list(range(g0, min(g0 + GRP, NTO)))
                xnb = xnb_a if gi % 2 == 0 else xnb_b
                xnb3 = xnb.rearrange("p (q s) -> p q s", q=CC)
                cxl = ld(xnb3[:, :, 0:len(gts) * 128],
                         xnT3d[:, :, g0 * 128:(g0 + len(gts)) * 128],
                         wait=(("TE", xnb_use[gi % 2]),))
                for i, tt in enumerate(gts):
                    bi = i % 4
                    pp = pbanks[bi]
                    wt = list(wldq) + [pb_evac[bi], cxl]
                    ctev = None
                    for q in range(CC):
                        ctev = top(lambda e, qq=q, ii=i, p_=pp, x_=xnb3: e.matmul(
                            p_[:, 0:128], x_[:, qq, ii * 128:(ii + 1) * 128],
                            wq_slot(qq), start=(qq == 0), stop=(qq == CC - 1)),
                            wait=tuple(wt) if q == 0 else (),
                            inc="TE" if q == CC - 1 else None)
                    pb_evac[bi] = ("VE", vop(lambda e, ii=i, p_=pp: e.tensor_copy(
                        kvg[:, ii * 256:ii * 256 + 128], p_[:, 0:128]),
                        wait=(("TE", S.c("TE")),)))
                xnb_use[gi % 2] = S.c("TE")
                proc_group(gts, is_q=True)

            chalo = S.c("SE")
            vop(lambda e: e.tensor_scalar(vtf[:, 0:1], vtf[:, 0:1], maskc[:, 0:1],
                                          None, ALU.mult), wait=(("SE", chalo),))
            vop(lambda e: e.tensor_scalar(vtf[:, OWN + 1:OWN + 2], vtf[:, OWN + 1:OWN + 2],
                                          maskc[:, 1:2], None, ALU.mult))

            qkT_done = S.c("SE")
            prev_exp = [qkT_done, qkT_done]
            prev_av = [S.c("TE"), S.c("TE")]
            prev_plo = [S.c("VE"), S.c("VE")]
            psA_free = {0: S.c("VE"), 1: S.c("VE")}
            for h in range(2):
                psA = psA0 if h == 0 else psA1
                VH = VH1 if h == 0 else VH2
                VL = VL1 if h == 0 else VL2
                for qt in range(NQT):
                    exp_c = {}
                    pf_c = {}
                    av_c = {}
                    plo_c = {}
                    hr = slice(h * 64, (h + 1) * 64)

                    KS = KS1 if h == 0 else KS2
                    QS = QS1 if h == 0 else QS2
                    QW = QW1 if h == 0 else QW2

                    def emit_s(kb):
                        pss = psS0 if kb % 2 == 0 else psS1
                        w = [("SE", pf_c[kb - 2] if kb >= 2 else prev_exp[kb % 2])]
                        top(lambda e, k=kb, p_=pss, q_=qt, K_=KS, Q_=QS: e.matmul(
                            p_[:, 0:QTW], K_[:, k * 128:(k + 1) * 128],
                            Q_[:, q_ * QTW:(q_ + 1) * QTW],
                            start=True, stop=False), wait=tuple(w), inc=None)
                        cs_mm = top(lambda e, k=kb, p_=pss, q_=qt, K_=KS, Q_=QW: e.matmul(
                            p_[:, 0:QTW], K_[:, k * 128:(k + 1) * 128],
                            Q_[:, q_ * QTW:(q_ + 1) * QTW],
                            start=False, stop=True))
                        sl = slice((kb % 2) * 512, (kb % 2) * 512 + QTW)
                        w2b = [("TE", cs_mm),
                               ("TE", av_c[kb - 2] if kb >= 2 else prev_av[kb % 2])]
                        exp_c[kb] = sop(lambda e, p_=pss, s_=sl: e.activation(
                            PhiB[:, s_], p_[:, 0:QTW],
                            ACT.Exp, bias=scal[:, 19:20], scale=0.125), wait=tuple(w2b),
                            drain=False)
                        w2 = [("VE", plo_c[kb - 2] if kb >= 2 else prev_plo[kb % 2])]
                        pf_c[kb] = sop(lambda e, p_=pss, s_=sl: e.activation(
                            expbuf[:, s_], p_[:, 0:QTW],
                            ACT.Exp, bias=scal[:, 19:20], scale=0.125), wait=tuple(w2),
                            drain=False)
                        plo_c[kb] = vop(lambda e, s_=sl: e.tensor_tensor(
                            PloB[:, s_], expbuf[:, s_], PhiB[:, s_], ALU.subtract),
                            wait=(("SE", pf_c[kb]),), drain=False)

                    def emit_av(kb):
                        sl = slice((kb % 2) * 512, (kb % 2) * 512 + QTW)
                        wav = [("SE", exp_c[kb])]
                        if kb == 0:
                            wav.append(("VE", psA_free[h]))
                        top(lambda e, k=kb, pA=psA, V_=VH, s_=sl: e.matmul(
                            pA[0:65, 0:QTW], V_[:, k * 65:(k + 1) * 65],
                            PhiB[:, s_], start=(k == 0), stop=False),
                            wait=tuple(wav), inc=None)
                        top(lambda e, k=kb, pA=psA, V_=VL, s_=sl: e.matmul(
                            pA[0:65, 0:QTW], V_[:, k * 65:(k + 1) * 65],
                            PhiB[:, s_], start=False, stop=False), inc=None)
                        av_c[kb] = top(lambda e, k=kb, pA=psA, V_=VH, s_=sl: e.matmul(
                            pA[0:65, 0:QTW], V_[:, k * 65:(k + 1) * 65],
                            PloB[:, s_], start=False, stop=(k == NT - 1)),
                            wait=(("VE", plo_c[kb]),))

                    emit_s(0)
                    for kb in range(1, NT):
                        emit_s(kb)
                        emit_av(kb - 1)
                    emit_av(NT - 1)
                    prev_exp = [pf_c[NT - 2], pf_c[NT - 1]]
                    prev_av = [av_c[NT - 2], av_c[NT - 1]]
                    prev_plo = [plo_c[NT - 2], plo_c[NT - 1]]
                    crc = vop(lambda e, pA=psA: e.reciprocal(rcpr[0:1, 0:QTW], pA[64:65, 0:QTW]),
                              wait=(("TE", av_c[NT - 1]), ("VE", S.c("VE"))))
                    crx = top(lambda e: e.matmul(psT[0:64, 0:QTW], onesc[0:1, 0:64],
                                                 rcpr[0:1, 0:QTW], start=True, stop=True),
                              wait=(("VE", crc),))
                    sf = sfm[h * 64:(h + 1) * 64, qt * QTW:(qt + 1) * QTW]
                    cev = vop(lambda e, s=sf, pA=psA: e.tensor_copy(s, pA[0:64, 0:QTW]))
                    psA_free[h] = vop(
                        lambda e, s=sf: e.tensor_tensor(s, s, psT[0:64, 0:QTW], ALU.mult),
                        wait=(("TE", crx),))
                    s_idx = h * NQT + qt
                    wtarget = ((g * 2 * NQT + s_idx + 1) * len(p0_work)) // (NP * 2 * NQT)
                    while p0_state["widx"] < wtarget:
                        weave_slot()

            wl0 = lepw[:, g * 3:g * 3 + 1]; wl1 = lepw[:, g * 3 + 1:g * 3 + 2]
            wl2 = lepw[:, g * 3 + 2:g * 3 + 3]; bl = lepb[:, g:g + 1]
            sf_all = sfm[:, 0:OWN]
            vop(lambda e, w_=wl0: e.scalar_tensor_tensor(sf_all, vtf[:, 0:OWN], w_, sf_all,
                                                         ALU.mult, ALU.add),
                wait=(("TE", S.c("TE")),))
            vop(lambda e, w_=wl1: e.scalar_tensor_tensor(sf_all, vtf[:, 1:OWN + 1], w_, sf_all,
                                                         ALU.mult, ALU.add))
            vop(lambda e, w_=wl2: e.scalar_tensor_tensor(sf_all, vtf[:, 2:OWN + 2], w_, sf_all,
                                                         ALU.mult, ALU.add))
            cfin = vop(lambda e, b_=bl: e.tensor_scalar(sf_all, sf_all, b_, None, ALU.add))
            for tt in range(NTO):
                ct = top(lambda e, t=tt: e.transpose(
                    psT[:, 0:128], sfm[:, t * 128:(t + 1) * 128], idf[:]),
                    wait=(("VE", cfin), ("SE", S.c("SE"))))
                cse = sop(lambda e: e.copy(sstg[:, 0:128], psT[:, 0:128]), wait=(("TE", ct),))
                stq(s_d[tt * 128:(tt + 1) * 128, g * 128:(g + 1) * 128], sstg[:, 0:128],
                    wait=(("SE", cse),))
                S.emit("scalar", None, wait=st_all())


        while p0_state["widx"] < len(p0_work):
            weave_slot()
        emit_p0_flush()

        S.barrier()

        cur2 = Cursor()
        xq_t2 = cur2.bf16(2 * CC * 128)
        hid = cur2.f32(2 * I)
        hq_t = cur2.bf16(I)
        hqT = cur2.bf16(2 * NI * 128)
        gstg = cur2.f32(2 * 1024)
        wg_a = cur2.bf16(CC * 512); wg_b = cur2.bf16(CC * 512)
        wu_a = cur2.bf16(CC * 512); wu_b = cur2.bf16(CC * 512)
        wd_a = cur2.bf16(NI * 128); wd_b = cur2.bf16(NI * 128)
        sgn2 = cur2.f32(I)
        st_s = cur2.f32(1024)
        st_x = cur2.f32(1024)
        blk2 = cur2.f32(256)
        chain2 = cur2.off

        xqv = xq_t2.rearrange("p (t q s) -> p t q s", t=2, q=CC)
        xq_pref = {}
        xq_guard = {0: None, 1: None}

        def issue_xq_load(k):
            par_ = k % 2
            ww = (xq_guard[par_],) if xq_guard[par_] else ()
            xq_pref[k] = ld(xq_t2.rearrange("p (t w) -> p t w", t=2)[:, par_, :],
                            xqT_d[:, k * 1024:(k + 1) * 1024], wait=ww)
        hid3 = hid.rearrange("p (t w) -> p t w", t=2)
        hqT3 = hqT.rearrange("p (t ic w) -> p t ic w", t=2, ic=NI)
        g3 = gstg.rearrange("p (t w) -> p t w", t=2)
        NDSL = 8
        gu_pref = {}
        p2s = {"wbuf_mm": {0: None, 1: None}, "se_pP0": 0, "se_pP1": 0,
               "ge_hd": 0}
        wdbuf_mm = {0: None, 1: None}
        wd_pref = {}
        se_pS = {0: 0, 1: 0}
        q_done = {0: 0, 1: 0}
        sp_guard = {0: None, 1: None}
        ge_c1 = {"c1": 0, "c2": 0}
        sx_pref = {}

        def issue_sx_load(k):
            w1 = (("GE", ge_c1["c1"]),) if ge_c1["c1"] else ()
            w2 = (("GE", ge_c1["c2"]),) if ge_c1["c2"] else ()
            cls_ = ld(st_s, s_d[k * 128:(k + 1) * 128, :], wait=w1)
            clx_ = ld(st_x, x_full[k * 128:(k + 1) * 128, :], wait=w2)
            sx_pref[k] = (cls_, clx_)

        def issue_gu_load(k, isl):
            par = (k * NISL + isl) % 2
            wg = wg_a if par == 0 else wg_b
            wu = wu_a if par == 0 else wu_b
            ww = (p2s["wbuf_mm"][par],) if p2s["wbuf_mm"][par] else ()
            clg = ld(wg, wgq_d[:, isl * CC * 512:(isl + 1) * CC * 512], wait=ww)
            clu = ld(wu, wuq_d[:, isl * CC * 512:(isl + 1) * CC * 512], wait=ww)
            gu_pref[(k, isl)] = (clg, clu, wg, wu)

        def issue_wd_load(k, csl):
            par = csl % 2
            wd = wd_a if par == 0 else wd_b
            ww = (wdbuf_mm[par],) if wdbuf_mm[par] else ()
            cld = ld(wd, wdq_d[:, csl * NI * 128:(csl + 1) * NI * 128], wait=ww)
            wd_pref[(k, csl)] = (cld, wd)

        issue_gu_load(0, 0)
        issue_wd_load(0, 0)
        issue_wd_load(0, 1)
        hd_last = {0: 0, 1: 0}

        def emit_gu(k):
            par = k % 2
            cu = None
            for isl in range(NISL):
                clg, clu, wg, wu = gu_pref.pop((k, isl))
                wg3 = wg.rearrange("p (q s) -> p q s", q=CC)
                wu3 = wu.rearrange("p (q s) -> p q s", q=CC)
                wtm = [clg, clu, xq_pref[k], ("SE", p2s["se_pP1"])]
                cg = None
                for q in range(CC):
                    cg = top(lambda e, qq=q, p2_=par, w3=wg3: e.matmul(
                        psP0[:, 0:512], xqv[:, p2_, qq, :],
                        w3[:, qq, :], start=(qq == 0), stop=(qq == CC - 1)),
                        wait=tuple(wtm) if q == 0 else (),
                        inc="TE" if q == CC - 1 else None)
                cu = None
                for q in range(CC):
                    cu = top(lambda e, qq=q, p2_=par, w3=wu3: e.matmul(
                        psP1[:, 0:512], xqv[:, p2_, qq, :],
                        w3[:, qq, :], start=(qq == 0), stop=(qq == CC - 1)),
                        inc="TE" if q == CC - 1 else None)
                wsl = [("TE", cg)]
                if isl == 0 and q_done[par]:
                    wsl.append(("VE", q_done[par]))
                csl = sop(lambda e, p_=par, i_=isl: e.activation(
                    hid3[:, p_, i_ * 512:(i_ + 1) * 512], psP0[:, 0:512],
                    ACT.Silu, bias=0.0, scale=scal[:, 12:13]), wait=tuple(wsl))
                p2s["se_pP0"] = csl
                cup = sop(lambda e: e.activation(rcpr[:, 0:512], psP1[:, 0:512],
                                                 ACT.Copy, bias=0.0,
                                                 scale=scal[:, 13:14]),
                          wait=(("TE", cu), ("GE", p2s["ge_hd"])))
                p2s["se_pP1"] = cup
                hd = gop(lambda e, p_=par, i_=isl: e.tensor_tensor(
                    hid3[:, p_, i_ * 512:(i_ + 1) * 512],
                    hid3[:, p_, i_ * 512:(i_ + 1) * 512], rcpr[:, 0:512], ALU.mult),
                    wait=(("SE", cup),))
                p2s["ge_hd"] = hd
                p2s["wbuf_mm"][(k * NISL + isl) % 2] = ("TE", cu)
                if isl + 1 < NISL:
                    issue_gu_load(k, isl + 1)
                elif k + 1 < NTO:
                    issue_gu_load(k + 1, 0)
            hd_last[par] = hd
            xq_guard[par] = ("TE", cu)
            if k + 2 < NTO:
                issue_xq_load(k + 2)

        issue_xq_load(0)
        issue_xq_load(1)
        emit_gu(0)
        issue_sx_load(0)
        for k in range(NTO):
            par = k % 2
            csg = sop(lambda e, p_=par: e.sign(sgn2, hid3[:, p_, :]),
                      wait=(("GE", hd_last[par]), ("VE", q_done[1 - par])))
            S.emit("vector", None, wait=(("SE", csg), ("TE", S.c("TE"))))
            cq = quant_tile(chain2, blk2, hid3[:, par, :], sgn2, hq_t, I,
                            scal[:, 4:5], scal[:, 5:6])
            q_done[par] = cq
            if k + 1 < NTO:
                emit_gu(k + 1)
            for ic in range(NI):
                hh = ic % 2
                ct = top(lambda e, i_=ic, h_=hh: e.transpose(
                    psB[:, h_ * 128:(h_ + 1) * 128],
                    hq_t[:, i_ * 128:(i_ + 1) * 128], idb[:]),
                    wait=(("VE", cq), ("SE", S.c("SE"))))
                sop(lambda e, p_=par, i_=ic, h_=hh: e.copy(
                    hqT3[:, p_, i_, :], psB[:, h_ * 128:(h_ + 1) * 128]),
                    wait=(("TE", ct),))
            se_cp = S.c("SE")
            for csl_i in range(NDSL):
                cld, wd = wd_pref.pop((k, csl_i))
                wd3 = wd.rearrange("p (ic s) -> p ic s", ic=NI)
                psS = psS0 if csl_i % 2 == 0 else psS1
                wtd = [cld, ("SE", se_cp), ("SE", se_pS[csl_i % 2])]
                cd = None
                for ic in range(NI):
                    cd = top(lambda e, p_=par, i_=ic, w3=wd3, ps=psS: e.matmul(
                        ps[:, 0:128], hqT3[:, p_, i_, :], w3[:, i_, :],
                        start=(i_ == 0), stop=(i_ == NI - 1)),
                        wait=tuple(wtd) if ic == 0 else (),
                        inc="TE" if ic == NI - 1 else None)
                wdbuf_mm[csl_i % 2] = ("TE", cd)
                if csl_i + 2 < NDSL:
                    issue_wd_load(k, csl_i + 2)
                elif k + 1 < NTO:
                    issue_wd_load(k + 1, csl_i + 2 - NDSL)
                wds = [("TE", cd)]
                if csl_i == 0 and sp_guard[par]:
                    wds.append(sp_guard[par])
                cds = sop(lambda e, p_=par, c_=csl_i, ps=psS: e.activation(
                    g3[:, p_, c_ * 128:(c_ + 1) * 128], ps[:, 0:128],
                    ACT.Silu, bias=0.0, scale=scal[:, 14:15]), wait=tuple(wds))
                se_pS[csl_i % 2] = cds
            cls_, clx_ = sx_pref.pop(k)
            c1 = gop(lambda e, p_=par: e.tensor_tensor(
                g3[:, p_, :], g3[:, p_, :], st_s, ALU.mult),
                wait=(("SE", S.c("SE")), cls_))
            ge_c1["c1"] = c1
            c2 = gop(lambda e, p_=par: e.tensor_tensor(
                g3[:, p_, :], g3[:, p_, :], st_x, ALU.add),
                wait=(clx_,))
            ge_c1["c2"] = c2
            sp_guard[par] = stq(out_t[k * 128:(k + 1) * 128, :], g3[:, par, :],
                                wait=(("GE", c2),))
            if k + 1 < NTO:
                issue_sx_load(k + 1)

        S.barrier()

        S.emit("gpsimd", None, wait=st_all())

        nc._script = S
        with nc.Block() as block:
            def make_runner(eng_name):
                def runner(eng):
                    for fn, waits, inc, inc_by, drain in S.ops[eng_name]:
                        for sname, val in waits:
                            if val > 0:
                                eng.wait_ge(sems[sname], val)
                        if drain and fn is not None:
                            eng.drain()
                        if fn is None:
                            continue
                        ins = fn(eng)
                        if inc is not None and ins is not None:
                            ins.then_inc(sems[inc], inc_by)
                return runner
            block.sync(make_runner("sync"))
            block.tensor(make_runner("tensor"))
            block.vector(make_runner("vector"))
            block.scalar(make_runner("scalar"))
            block.gpsimd(make_runner("gpsimd"))

    return nc


def _rope_tables(n, d=64, base=10000.0):
    try:
        import jax
        import jax.numpy as jnp
        cpu = jax.devices("cpu")[0]
        with jax.default_device(cpu):
            position = jnp.arange(n, dtype=jnp.float32)[:, None]
            div_term = jnp.exp(jnp.arange(0, d, 2, dtype=jnp.float32)
                               * (-np.log(base) / d))
            ang = position * div_term
            return np.asarray(jnp.sin(ang)), np.asarray(jnp.cos(ang))
    except Exception:
        position = np.arange(n, dtype=np.float32)[:, None]
        div_term = np.exp(np.arange(0, d, 2, dtype=np.float32)
                          * np.float32(-np.log(base) / d)).astype(np.float32)
        ang = (position * div_term).astype(np.float32)
        return np.sin(ang).astype(np.float32), np.cos(ang).astype(np.float32)


def _tile_tables(tab, NT):
    return np.ascontiguousarray(
        tab.reshape(NT, 128, 32).transpose(1, 0, 2).reshape(128, NT * 32))


def make_core_inputs(x, w_qkv, b_qkv, w_lepe, b_lepe, gate_w, up_w, down_w,
                     s_in, s_in_down, b, j, seq):
    OWN = seq // 2
    NT = seq // 128
    roll = j * OWN
    xr = np.ascontiguousarray(np.roll(np.asarray(x[b]), -roll, axis=0)).astype(np.float32)
    sin_full, cos_full = _rope_tables(seq)
    sin_r = np.roll(sin_full, -roll, axis=0)
    cos_r = np.roll(cos_full, -roll, axis=0)
    D = 1024
    f32 = np.float32
    wq = np.ascontiguousarray(np.asarray(w_qkv[0:D]).T).astype(f32)
    wk = np.ascontiguousarray(np.asarray(w_qkv[D:2 * D]).T).astype(f32)
    wv = np.ascontiguousarray(np.asarray(w_qkv[2 * D:3 * D]).T).astype(f32)
    lw = np.ascontiguousarray(
        np.asarray(w_lepe).reshape(8, 128, 3).transpose(1, 0, 2).reshape(128, 24)).astype(f32)
    lb = np.ascontiguousarray(
        np.asarray(b_lepe).reshape(8, 128).T).astype(f32)
    maskL = 1.0 if j == 1 else 0.0
    maskR = 1.0 if j == 0 else 0.0
    return {
        "x_full": xr,
        "wqT": wq, "wkT": wk, "wvT": wv,
        "bias_b": np.ascontiguousarray(np.broadcast_to(np.asarray(b_qkv, f32), (128, 3 * D))),
        "gate_w": np.asarray(gate_w, f32), "up_w": np.asarray(up_w, f32),
        "down_w": np.asarray(down_w, f32),
        "s_in_b": np.full((128, 1), np.asarray(s_in).reshape(-1)[0], f32),
        "s_ind_b": np.full((128, 1), np.asarray(s_in_down).reshape(-1)[0], f32),
        "lepe_w": lw, "lepe_b": lb,
        "sin_t": _tile_tables(sin_r, NT), "cos_t": _tile_tables(cos_r, NT),
        "masks": np.ascontiguousarray(np.broadcast_to(
            np.array([maskL, maskR], f32), (128, 2))),
        "ident_f": np.eye(128, dtype=f32),
        "ident_b": np.eye(128).astype(ml_dtypes.bfloat16),
        "ones_in": np.ones((128, 128), f32),
    }


def kernel(L0_attn_x, L0_attn_w_qkv, L0_attn_b_qkv, L0_attn_w_lepe, L0_attn_b_lepe,
           L0_ffn_gate_w, L0_ffn_up_w, L0_ffn_down_w, L0_ffn_s_in, L0_ffn_s_in_down):
    SEQ = 4096
    OWN = SEQ // 2
    x = np.asarray(L0_attn_x, np.float32)
    in_maps = []
    for core in range(8):
        b, j = core // 2, core % 2
        in_maps.append(make_core_inputs(
            x, L0_attn_w_qkv, L0_attn_b_qkv, L0_attn_w_lepe, L0_attn_b_lepe,
            L0_ffn_gate_w, L0_ffn_up_w, L0_ffn_down_w, L0_ffn_s_in, L0_ffn_s_in_down,
            b, j, SEQ))
    nc = build_nc(seq=SEQ, drains=True)
    res = run_bass_kernel_spmd(nc, in_maps, core_ids=list(range(8)))
    out = np.empty((4, SEQ, 1024), np.float32)
    for core in range(8):
        b, j = core // 2, core % 2
        out[b, j * OWN:(j + 1) * OWN, :] = np.asarray(res.results[core]["out_s"])
    return out


# version 32 (final)
# speedup vs baseline: 1.0619x; 1.0227x over previous
import numpy as np
import ml_dtypes
import concourse.bass as bass
import concourse.mybir as mybir
from concourse.bass_utils import run_bass_kernel_spmd
from contextlib import ExitStack

F32 = mybir.dt.float32
BF16 = mybir.dt.bfloat16
ALU = mybir.AluOpType
ACT = mybir.ActivationFunctionType
AX = mybir.AxisListType

FP8_MAX = 448.0
FP4_MAX = 6.0
GSC = FP8_MAX * FP4_MAX
EXP_BIAS = -8.0

ENGINES = ("sync", "tensor", "vector", "scalar", "gpsimd")


class Script:

    def __init__(self):
        self.ops = {e: [] for e in ENGINES}
        self.counts = {}

    def emit(self, eng, fn, wait=(), inc=None, inc_by=1, drain=False):
        self.ops[eng].append((fn, tuple(wait), inc, inc_by, drain))
        if inc is not None:
            self.counts[inc] = self.counts.get(inc, 0) + inc_by
            return self.counts[inc]
        return None

    def c(self, sem):
        return self.counts.get(sem, 0)

    def barrier(self):
        snap = tuple((s, v) for s, v in self.counts.items() if v > 0)
        for e in ENGINES:
            self.emit(e, None, wait=snap)


def build_nc(seq=4096, drains=False, dbg=False):
    SEQ = seq
    OWN = SEQ // 2
    NT = SEQ // 128
    NTO = OWN // 128
    D = 1024
    CC = 8
    NP = 8
    I = 4096
    NI = I // 128
    NISL = I // 512
    CHT = min(2, NTO)
    NCH = NTO // CHT
    NQT = max(OWN // 512, 1)
    QTW = min(512, OWN)
    GRP = 4 if NT % 4 == 0 else 1

    nc = bass.Bass("TRN2", target_bir_lowering=False)

    def din(name, shape, dt=F32):
        return nc.dram_tensor(name, shape, dt, kind="ExternalInput").ap()

    x_full = din("x_full", [SEQ, D])
    wqT = din("wqT", [D, D]); wkT = din("wkT", [D, D]); wvT = din("wvT", [D, D])
    bias_b = din("bias_b", [128, 3 * D])
    gate_w = din("gate_w", [I, D]); up_w = din("up_w", [I, D]); down_w = din("down_w", [D, I])
    s_in_b = din("s_in_b", [128, 1]); s_ind_b = din("s_ind_b", [128, 1])
    lepe_w = din("lepe_w", [128, NP * 3]); lepe_b = din("lepe_b", [128, NP])
    sin_t = din("sin_t", [128, NT * 32]); cos_t = din("cos_t", [128, NT * 32])
    masks = din("masks", [128, 2])
    ident_f = din("ident_f", [128, 128]); ident_b = din("ident_b", [128, 128], BF16)
    ones_in = din("ones_in", [128, 128])

    out_t = nc.dram_tensor("out_s", [OWN, D], F32, kind="ExternalOutput").ap()

    KIND = "ExternalOutput" if dbg else "Internal"
    wgq_d = nc.dram_tensor("wgq_d", [128, CC * I], BF16, kind=KIND).ap()
    wuq_d = nc.dram_tensor("wuq_d", [128, CC * I], BF16, kind=KIND).ap()
    wdq_d = nc.dram_tensor("wdq_d", [128, NI * D], BF16, kind=KIND).ap()
    xqT_d = nc.dram_tensor("xqT_d", [128, CC * OWN], BF16, kind=KIND).ap()
    xnT_d = nc.dram_tensor("xnT_d", [128, CC * SEQ], F32, kind="Internal").ap()
    s_d = nc.dram_tensor("s_d", [OWN, D], F32, kind=KIND).ap()
    g_d = nc.dram_tensor("g_d", [OWN, D], F32, kind=KIND).ap()
    if dbg:
        dbg_xnt = nc.dram_tensor("dbg_xnt", [128, CC * SEQ], F32, kind="ExternalOutput").ap()

    ctx = ExitStack()
    with ctx:
        def sbuf(name, shape, dt=F32):
            return ctx.enter_context(nc.sbuf_tensor(name, shape, dt))

        sinc = sbuf("sinc", [128, NT * 32]); cosc = sbuf("cosc", [128, NT * 32])
        lepw = sbuf("lepw", [128, NP * 3]); lepb = sbuf("lepb", [128, NP])
        maskc = sbuf("maskc", [128, 2])
        idf = sbuf("idf", [128, 128]); idb = sbuf("idb", [128, 128], BF16)
        onesc = sbuf("onesc", [128, 128])
        scal = sbuf("scal", [128, 32])
        expbuf = sbuf("expbuf", [128, 1024])
        biasg = sbuf("biasg", [128, 384])
        rcpr = sbuf("rcpr", [128, 512])

        rem = nc.sbuf_bytes_remaining
        acols = (rem // 4 // 128) * 128 - 128
        arena = sbuf("arena", [128, acols])
        arena_b = arena[:].bitcast(BF16)

        def psum(name, dt=F32, cols=512):
            return ctx.enter_context(nc.psum_tensor(name, [128, cols], dt))
        psS0 = psum("psS0"); psS1 = psum("psS1")
        psA0 = psum("psA0"); psA1 = psum("psA1")
        psP0 = psum("psP0"); psP1 = psum("psP1")
        psT = psum("psT")
        psB = psum("psB", BF16, 1024)

        sem_names = (["TE", "VE", "SE", "GE"] + [f"L{i}" for i in range(8)]
                     + [f"S{i}" for i in range(4)])
        sems = {s: ctx.enter_context(nc.semaphore(s)) for s in sem_names}

        S = Script()

        rr = {"ld": 0, "st": 0}

        def ld(dst_ap, src_ap, wait=()):
            sem = f"L{rr['ld'] % 8}"
            rr["ld"] += 1
            w = tuple(wait)
            prev = S.c(sem)
            if prev:
                w = w + ((sem, prev),)
            v = S.emit("sync", lambda e, d=dst_ap, s=src_ap: e.dma_start(out=d, in_=s),
                       wait=w, inc=sem, inc_by=16)
            return (sem, v)

        def ld_all():
            return tuple((f"L{i}", S.c(f"L{i}")) for i in range(8) if S.c(f"L{i}"))

        def stq(dst_ap, src_ap, wait=()):
            sem = f"S{rr['st'] % 4}"
            rr["st"] += 1
            w = tuple(wait)
            prev = S.c(sem)
            if prev:
                w = w + ((sem, prev),)
            v = S.emit("gpsimd", lambda e, d=dst_ap, s=src_ap: e.dma_start(out=d, in_=s),
                       wait=w, inc=sem, inc_by=16)
            return (sem, v)

        def st_all():
            return tuple((f"S{i}", S.c(f"S{i}")) for i in range(4) if S.c(f"S{i}"))

        dr_v = drains in (True, "all")
        dr_s = drains in (True, "all", "scalar")

        def vop(fn, wait=(), inc="VE", drain=None):
            return S.emit("vector", fn, wait=wait, inc=inc,
                          drain=dr_v if drain is None else drain)

        def sop(fn, wait=(), inc="SE", drain=None):
            return S.emit("scalar", fn, wait=wait, inc=inc,
                          drain=dr_s if drain is None else drain)

        def top(fn, wait=(), inc="TE"):
            return S.emit("tensor", fn, wait=wait, inc=inc)

        def gop(fn, wait=(), inc="GE", drain=None):
            return S.emit("gpsimd", fn, wait=wait, inc=inc,
                          drain=dr_v if drain is None else drain)

        class Cursor:
            def __init__(self, off=0):
                self.off = off
            def f32(self, n):
                assert self.off + n <= acols, f"arena overflow: {self.off}+{n} > {acols}"
                ap = arena[:, self.off:self.off + n]
                self.off += n
                return ap
            def bf16(self, n):
                assert n % 2 == 0
                assert self.off + n // 2 <= acols, "arena overflow (bf16)"
                ap = arena_b[:, 2 * self.off:2 * self.off + n]
                self.off += n // 2
                return ap

        ld(sinc[:], sin_t[:]); ld(cosc[:], cos_t[:])
        ld(lepw[:], lepe_w[:]); ld(lepb[:], lepe_b[:])
        ld(maskc[:], masks[:]); ld(idf[:], ident_f[:]); ld(idb[:], ident_b[:])
        ld(onesc[:], ones_in[:])
        ld(scal[:, 0:1], s_in_b[:]); ld(scal[:, 1:2], s_ind_b[:])
        c0all = ld_all()
        vop(lambda e: e.tensor_scalar(scal[:, 2:3], scal[:, 0:1], 1.0 / 6.0, None, ALU.mult),
            wait=c0all)
        vop(lambda e: e.tensor_scalar(scal[:, 3:4], scal[:, 0:1], 2.0, None, ALU.mult))
        vop(lambda e: e.tensor_scalar(scal[:, 4:5], scal[:, 1:2], 1.0 / 6.0, None, ALU.mult))
        vop(lambda e: e.tensor_scalar(scal[:, 5:6], scal[:, 1:2], 2.0, None, ALU.mult))
        vop(lambda e: e.memset(scal[:, 18:19], 1e-5))
        vop(lambda e: e.memset(scal[:, 19:20], EXP_BIAS))
        c_consts = ld_all()
        for _e in ("tensor", "scalar", "gpsimd", "vector"):
            S.emit(_e, None, wait=c_consts)

        def quant_block_chain(cur, am, gs6_ap, gs2_ap, nb):
            r2 = cur.f32(nb); s8h = cur.f32(nb)
            sc = cur.f32(nb); p = cur.f32(nb); pinv = cur.f32(nb)
            t1 = cur.f32(nb); t2 = cur.f32(nb); fl = cur.f32(nb)
            vop(lambda e: e.tensor_scalar(sc, am, gs6_ap, 448.0, ALU.mult, ALU.min))
            vop(lambda e: e.tensor_scalar(t1, sc, 4.0, None, ALU.is_ge))
            vop(lambda e: e.tensor_scalar(p, t1, 255.0, 1.0, ALU.mult, ALU.add))
            vop(lambda e: e.tensor_scalar(pinv, t1, (2.0 ** -8) - 1.0, 1.0, ALU.mult, ALU.add))
            for mul, thr in ((15.0, 0.25), (3.0, 2.0 ** -4), (1.0, 2.0 ** -5)):
                vop(lambda e, th=thr: e.tensor_scalar(t2, p, th, None, ALU.mult))
                vop(lambda e: e.tensor_tensor(t1, sc, t2, ALU.is_ge))
                vop(lambda e, m=mul: e.tensor_scalar(t2, t1, m, 1.0, ALU.mult, ALU.add))
                vop(lambda e: e.tensor_tensor(p, p, t2, ALU.mult))
                vop(lambda e, m=mul: e.tensor_scalar(t2, t1, 1.0 / (m + 1.0) - 1.0, 1.0,
                                                     ALU.mult, ALU.add))
                vop(lambda e: e.tensor_tensor(pinv, pinv, t2, ALU.mult))
            vop(lambda e: e.scalar_tensor_tensor(t1, sc, 512.0, pinv, ALU.mult, ALU.mult))
            vop(lambda e: e.tensor_scalar(t1, t1, 0.5, None, ALU.add))
            vop(lambda e: e.tensor_scalar(t2, t1, 16.0, 16.0, ALU.is_ge, ALU.mult))
            vop(lambda e: e.tensor_tensor(t1, t1, t2, ALU.subtract))
            vop(lambda e: e.tensor_copy(fl, t2))
            vop(lambda e: e.tensor_scalar(t2, t1, 8.0, 8.0, ALU.is_ge, ALU.mult))
            vop(lambda e: e.tensor_tensor(t1, t1, t2, ALU.subtract))
            vop(lambda e: e.tensor_tensor(fl, fl, t2, ALU.add))
            vop(lambda e: e.tensor_scalar(t2, t1, 4.0, 4.0, ALU.is_ge, ALU.mult))
            vop(lambda e: e.tensor_tensor(t1, t1, t2, ALU.subtract))
            vop(lambda e: e.tensor_tensor(fl, fl, t2, ALU.add))
            vop(lambda e: e.tensor_scalar(t2, t1, 2.0, 2.0, ALU.is_ge, ALU.mult))
            vop(lambda e: e.tensor_tensor(t1, t1, t2, ALU.subtract))
            vop(lambda e: e.tensor_tensor(fl, fl, t2, ALU.add))
            vop(lambda e: e.scalar_tensor_tensor(fl, t1, 1.0, fl, ALU.is_ge, ALU.add))
            vop(lambda e: e.scalar_tensor_tensor(sc, fl, 2.0 ** -9, p, ALU.mult, ALU.mult))
            vop(lambda e: e.tensor_scalar(s8h, sc, 0.5, None, ALU.mult))
            vop(lambda e: e.tensor_single_scalar(t1, sc, 0.0, ALU.is_equal))
            vop(lambda e: e.tensor_tensor(t2, sc, t1, ALU.max))
            vop(lambda e: e.reciprocal(t2, t2))
            vop(lambda e: e.tensor_scalar(t2, t2, gs2_ap, None, ALU.mult))
            vop(lambda e: e.tensor_scalar(fl, t1, -1.0, 1.0, ALU.mult, ALU.add))
            vop(lambda e: e.tensor_tensor(t2, t2, fl, ALU.mult))
            c = vop(lambda e: e.scalar_tensor_tensor(r2, t1, 2.0, t2, ALU.mult, ALU.add))
            return r2, s8h, c

        def quant_elem_chain(cur, w_ap, sgn_ap, r2, s8h, out_ap, ncols, nb):
            u = cur.f32(ncols); A = cur.f32(ncols); Bv = cur.f32(ncols)
            t = cur.f32(ncols); h = cur.f32(ncols); c = cur.f32(ncols)
            r2b = r2.unsqueeze(2).broadcast_to([128, nb, 16])
            s8hb = s8h.unsqueeze(2).broadcast_to([128, nb, 16])
            u3 = u.rearrange("p (g b) -> p g b", b=16)
            vop(lambda e: e.scalar_tensor_tensor(u, w_ap, -1.0, w_ap, ALU.mult, ALU.max))
            vop(lambda e: e.tensor_tensor(u3, u3, r2b, ALU.mult))
            vop(lambda e: e.tensor_scalar(u, u, 12.0, None, ALU.min))
            vop(lambda e: e.tensor_scalar(A, u, 4.0, 0.5, ALU.is_ge, ALU.mult))
            vop(lambda e: e.tensor_scalar(Bv, u, 8.0, 0.25, ALU.is_ge, ALU.mult))
            vop(lambda e: e.tensor_tensor(t, A, Bv, ALU.add))
            vop(lambda e: e.scalar_tensor_tensor(h, Bv, 3.0, t, ALU.mult, ALU.add))
            vop(lambda e: e.tensor_scalar(t, t, -1.0, 1.0, ALU.mult, ALU.add))
            vop(lambda e: e.tensor_scalar(A, h, 2.0, 1.0, ALU.mult, ALU.add))
            vop(lambda e: e.scalar_tensor_tensor(h, u, 0.5, h, ALU.add, ALU.add))
            vop(lambda e: e.tensor_tensor(h, h, t, ALU.mult))
            vop(lambda e: e.tensor_scalar(c, h, 1.0, None, ALU.is_ge))
            vop(lambda e: e.scalar_tensor_tensor(c, h, 2.0, c, ALU.is_ge, ALU.add))
            vop(lambda e: e.scalar_tensor_tensor(c, h, 3.0, c, ALU.is_ge, ALU.add))
            vop(lambda e: e.scalar_tensor_tensor(c, h, 4.0, c, ALU.is_ge, ALU.add))
            vop(lambda e: e.tensor_tensor(c, c, A, ALU.mult))
            vop(lambda e: e.tensor_tensor(c, c, sgn_ap, ALU.mult))
            o3 = out_ap.rearrange("p (g b) -> p g b", b=16)
            c3 = c.rearrange("p (g b) -> p g b", b=16)
            return vop(lambda e: e.tensor_tensor(o3, c3, s8hb, ALU.mult))

        def quant_tile(chain_base, blk_ap, src_ap, sgn_ap, out_ap, C, gs6_ap, gs2_ap):
            nb = C // 16
            vop(lambda e: e.reduce_max(
                blk_ap[:, 0:nb], src_ap[:, 0:C].rearrange("p (g k) -> p g k", k=16),
                axis=AX.X, apply_absolute_value=True))
            curq = Cursor(chain_base)
            r2, s8h, _ = quant_block_chain(curq, blk_ap[:, 0:nb], gs6_ap, gs2_ap, nb)
            elem_base = chain_base + 2 * nb
            cq = None
            for sub in range(C // 1024):
                sl = slice(sub * 1024, (sub + 1) * 1024)
                bsl = slice(sub * 64, (sub + 1) * 64)
                cq = quant_elem_chain(Cursor(elem_base),
                                      src_ap[:, sl], sgn_ap[:, sl],
                                      r2[:, bsl], s8h[:, bsl],
                                      out_ap[:, sl], 1024, 64)
            return cq

        def cross_partition_max(src_col, dst_col, tmp_row):
            ct = top(lambda e: e.transpose(psT[0:1, 0:128], src_col, idf[:]),
                     wait=(("VE", S.c("VE")), ("SE", S.c("SE"))))
            cv = vop(lambda e: e.tensor_copy(tmp_row[0:1, 0:128], psT[0:1, 0:128]),
                     wait=(("TE", ct),))
            cv = vop(lambda e: e.reduce_max(tmp_row[0:1, 0:1], tmp_row[0:1, 0:128], axis=AX.X))
            ct = top(lambda e: e.matmul(psT[:, 0:1], onesc[0:1, :], tmp_row[0:1, 0:1],
                                        start=True, stop=True), wait=(("VE", cv),))
            return vop(lambda e: e.tensor_copy(dst_col, psT[:, 0:1]), wait=(("TE", ct),))

        P0_NEED = 4096 + 4096 + 2048 + 2048 + 512 + 2 * 256 + 6 * 1024
        p0_cur = Cursor(acols - P0_NEED)
        wt_a = p0_cur.f32(4096)
        sgn0 = p0_cur.f32(4096)
        wq_a = p0_cur.bf16(4096)
        stage0 = p0_cur.bf16(4096)
        blk0 = p0_cur.f32(512)
        chain_base0 = p0_cur.off

        wcfg = [
            (gate_w, wgq_d, I, D, 6, 9, 15, 12, 0),
            (up_w, wuq_d, I, D, 7, 10, 16, 13, 0),
            (down_w, wdq_d, D, I, 8, 11, 17, 14, 1),
        ]
        p0_state = {"last_sp": None, "last_q": None, "last_ct": None,
                    "pending": None, "widx": 0}

        pa_steps = []
        for _w in wcfg:
            _src, _dst, _rows, _C, _sl6, _sl2, _slm, _slr, _sli = _w
            _gmax = scal[:, _slm:_slm + 1]
            _st = {"last_v": None}

            def _tile_step(src_=_src, C=_C, gmax=_gmax, st0=_st, nrt=_rows // 128):
                ot = st0.setdefault("ot", 0)
                st0["ot"] = ot + 1
                lv = st0["last_v"] if st0["last_v"] is not None else S.c("VE")
                cl = ld(wt_a[:, 0:C], src_[ot * 128:(ot + 1) * 128, :],
                        wait=(("VE", lv),))
                if ot == 0:
                    st0["last_v"] = vop(lambda e, cc=C, g_=gmax: e.reduce_max(
                        g_, wt_a[:, 0:cc], axis=AX.X, apply_absolute_value=True),
                        wait=(cl,))
                else:
                    vop(lambda e, cc=C: e.reduce_max(
                        blk0[:, 0:1], wt_a[:, 0:cc], axis=AX.X,
                        apply_absolute_value=True), wait=(cl,))
                    st0["last_v"] = vop(lambda e, g_=gmax: e.tensor_tensor(
                        g_, g_, blk0[:, 0:1], ALU.max))
            for _ot in range(_rows // 128):
                pa_steps.append(_tile_step)

            def _fin_step(gmax=_gmax, sl6=_sl6, sl2=_sl2, slr=_slr, sli=_sli):
                cross_partition_max(gmax, gmax, rcpr)
                vop(lambda e, g_=gmax: e.reciprocal(blk0[:, 0:1], g_))
                vop(lambda e, a=sl6: e.tensor_scalar(scal[:, a:a + 1], blk0[:, 0:1],
                                                     GSC / 6.0, None, ALU.mult))
                vop(lambda e, a=sl2: e.tensor_scalar(scal[:, a:a + 1], blk0[:, 0:1],
                                                     GSC * 2.0, None, ALU.mult))
                vop(lambda e, s=sli: e.reciprocal(blk0[:, 1:2], scal[:, s:s + 1]))
                vop(lambda e, r=slr, g_=gmax: e.scalar_tensor_tensor(
                    scal[:, r:r + 1], g_, 1.0 / GSC, blk0[:, 1:2], ALU.mult, ALU.mult))
            pa_steps.append(_fin_step)
        pa_state = {"i": 0}

        def pa_pump(n):
            while pa_state["i"] < len(pa_steps) and n > 0:
                pa_steps[pa_state["i"]]()
                pa_state["i"] += 1
                n -= 1

        p0_work = []
        for wi, (src_, dst, rows, C, sl6, sl2, slm, slr, sli) in enumerate(wcfg):
            for ot in range(rows // 128):
                p0_work.append((src_, dst, C, sl6, sl2, ot))

        def emit_p0_quant(src_, dst, C, sl6, sl2, ot):
            last_sp = p0_state["last_sp"]
            wsp0 = (last_sp,) if last_sp else ()
            wq = (("VE", p0_state["last_q"]),) if p0_state["last_q"] else (("VE", S.c("VE")),)
            cl = ld(wt_a[:, 0:C], src_[ot * 128:(ot + 1) * 128, :], wait=wq + wsp0)
            csg = sop(lambda e, cc=C: e.sign(sgn0[:, 0:cc], wt_a[:, 0:cc]), wait=(cl,))
            wtr = (("TE", p0_state["last_ct"]),) if p0_state["last_ct"] else ()
            S.emit("vector", None, wait=(("SE", csg), cl) + wtr)
            cq = quant_tile(chain_base0, blk0, wt_a, sgn0, wq_a, C,
                            scal[:, sl6:sl6 + 1], scal[:, sl2:sl2 + 1])
            p0_state["last_q"] = cq
            S.emit("scalar", None, wait=(("VE", cq),))
            p0_state["pending"] = (dst, C, ot, cq)

        def emit_p0_flush():
            if p0_state["pending"] is None:
                return
            dst, C, ot, cq = p0_state["pending"]
            p0_state["pending"] = None
            last_sp = p0_state["last_sp"]
            wsp0 = (last_sp,) if last_sp else ()
            ncc = C // 128
            ct = None
            for q in range(ncc):
                hh = q % 2
                ct = top(lambda e, qq=q, h_=hh: e.transpose(
                    psB[:, h_ * 128:(h_ + 1) * 128],
                    wq_a[:, qq * 128:(qq + 1) * 128], idb[:]),
                    wait=(("VE", cq), ("SE", S.c("SE"))) + wsp0)
                sop(lambda e, qq=q, h_=hh: e.copy(
                    stage0[:, qq * 128:(qq + 1) * 128],
                    psB[:, h_ * 128:(h_ + 1) * 128]), wait=(("TE", ct),))
            p0_state["last_ct"] = ct
            if C == 1024:
                dv4 = dst.rearrange("p (il q r) -> p il q r", il=NISL, q=ncc, r=512)
                dview = dv4[:, ot // 4, :, (ot % 4) * 128:(ot % 4 + 1) * 128]
            else:
                dview = dst.rearrange("p (db ic r) -> p db ic r",
                                      db=8, ic=NI, r=128)[:, ot, :, :]
            sview = stage0[:, 0:ncc * 128].rearrange("p (q r) -> p q r", q=ncc)
            p0_state["last_sp"] = stq(dview, sview, wait=(("SE", S.c("SE")),))

        def weave_slot():
            emit_p0_flush()
            if p0_state["widx"] < len(p0_work):
                emit_p0_quant(*p0_work[p0_state["widx"]])
                p0_state["widx"] += 1

        S.barrier()

        cur1 = Cursor()
        p3_base = 0
        xnstage = cur1.f32(1024)
        xt_a = cur1.f32(1024); xt_b = cur1.f32(1024)
        xn_t = cur1.f32(1024); sq_t = cur1.f32(1024)
        mv = cur1.f32(4)
        xq_t = cur1.bf16(1024)
        xqstage = cur1.bf16(1024)
        sgn1 = cur1.f32(1024)
        blk1 = cur1.f32(64)
        chain1 = cur1.off

        xnT3d = xnT_d.rearrange("p (q s) -> p q s", q=CC)
        xnst3 = xnstage.rearrange("p (q s) -> p q s", q=CC)
        prev_done = S.c("VE")
        prev_sp = None
        prev_sp_xn = None
        xn_guard = ()
        for tt in range(NT):
            buf = xt_a if tt % 2 == 0 else xt_b
            cl = ld(buf, x_full[tt * 128:(tt + 1) * 128, :],
                    wait=(("VE", prev_done),) + ((prev_sp,) if prev_sp else ()))
            vop(lambda e, b=buf: e.reduce_sum(mv[:, 0:1], b, axis=AX.X), wait=(cl,))
            vop(lambda e: e.tensor_scalar(mv[:, 0:1], mv[:, 0:1], 1.0 / D, None, ALU.mult))
            cve = vop(lambda e, b=buf: e.tensor_scalar(xn_t, b, mv[:, 0:1], None, ALU.subtract),
                       wait=xn_guard)
            sop(lambda e: e.activation(sq_t, xn_t, ACT.Square, bias=0.0, scale=1.0,
                                       accum_out=mv[:, 2:3]), wait=(("VE", cve),))
            cs2 = sop(lambda e: e.activation(mv[:, 3:4], mv[:, 2:3], ACT.Sqrt,
                                             bias=scal[:, 18:19], scale=1.0 / D))
            vop(lambda e: e.reciprocal(mv[:, 1:2], mv[:, 3:4]), wait=(("SE", cs2),))
            cxn = vop(lambda e: e.tensor_scalar(xn_t, xn_t, mv[:, 1:2], None, ALU.mult))
            for q in range(CC):
                wtr = [("VE", cxn), ("SE", S.c("SE"))]
                if q == 0 and prev_sp_xn:
                    wtr.append(prev_sp_xn)
                ct = top(lambda e, qq=q: e.transpose(
                    psT[:, 0:128], xn_t[:, qq * 128:(qq + 1) * 128], idf[:]),
                    wait=tuple(wtr))
                sop(lambda e, qq=q: e.copy(xnst3[:, qq, :], psT[:, 0:128]),
                    wait=(("TE", ct),))
            prev_sp_xn = stq(xnT3d[:, :, tt * 128:(tt + 1) * 128], xnst3,
                             wait=(("SE", S.c("SE")),))
            prev_done = cxn
            if tt < NTO:
                csg = sop(lambda e: e.sign(sgn1, xn_t), wait=(("VE", cxn),))
                S.emit("vector", None, wait=(("SE", csg),))
                cq = quant_tile(chain1, blk1, xn_t, sgn1, xq_t, 1024,
                                scal[:, 2:3], scal[:, 3:4])
                S.emit("scalar", None, wait=(("VE", cq),))
                for q in range(CC):
                    hh = q % 2
                    ct = top(lambda e, qq=q, h_=hh: e.transpose(
                        psB[:, h_ * 128:(h_ + 1) * 128],
                        xq_t[:, qq * 128:(qq + 1) * 128], idb[:]),
                        wait=(("VE", cq), ("SE", S.c("SE"))) + ((prev_sp,) if prev_sp else ()))
                    sop(lambda e, qq=q, h_=hh: e.copy(
                        xqstage[:, qq * 128:(qq + 1) * 128],
                        psB[:, h_ * 128:(h_ + 1) * 128]), wait=(("TE", ct),))
                dv = xqT_d[:, tt * 1024:(tt + 1) * 1024].rearrange(
                    "p (q s) -> p q s", q=CC)
                sv = xqstage.rearrange("p (q s) -> p q s", q=CC)
                prev_sp = stq(dv, sv, wait=(("SE", S.c("SE")),))
                prev_done = cq
            xn_guard = (("TE", S.c("TE")), ("SE", S.c("SE")))
            pa_pump(3)

        S.barrier()
        if dbg:
            stq(dbg_xnt[:], xnT)
            S.barrier()

        cur3 = Cursor(p3_base)
        xnb_a = cur3.f32(CC * GRP * 128); xnb_b = cur3.f32(CC * GRP * 128)
        KS1 = cur3.bf16(SEQ); KS2 = cur3.bf16(SEQ)
        QS1 = cur3.bf16(OWN); QS2 = cur3.bf16(OWN)
        QW1 = cur3.bf16(OWN); QW2 = cur3.bf16(OWN)
        VH1 = cur3.bf16(NT * 65); VL1 = cur3.bf16(NT * 65)
        VH2 = cur3.bf16(NT * 65); VL2 = cur3.bf16(NT * 65)
        phr = cur3.f32(1024)
        PhiB = phr[:, 0:512].bitcast(BF16)
        PloB = phr[:, 512:1024].bitcast(BF16)
        vtf = cur3.f32(OWN + 2)
        kvg_off = cur3.off
        kvg = cur3.f32(GRP * 256)
        ket = cur3.f32(GRP * 128)
        rtw = cur3.f32(GRP * 128)
        rt1 = rtw[:, 0:GRP * 64]; rt2 = rtw[:, GRP * 64:GRP * 128]
        sstg = rcpr[:, 0:128]
        assert cur3.off <= acols - P0_NEED, f"P3 region {cur3.off} overlaps P0 at {acols - P0_NEED}"
        assert GRP * 256 + GRP * 128 + 2 * GRP * 64 >= OWN, "sfm alias too small"
        sfm = arena[:, kvg_off:kvg_off + OWN]

        def kv_slot(q):
            if q < 4:
                return expbuf[:, q * 256:(q + 1) * 256]
            if q < 6:
                return rcpr[:, (q - 4) * 256:(q - 3) * 256]
            return phr[:, (q - 6) * 256:(q - 5) * 256]

        def wq_slot(q):
            return expbuf[:, q * 128:(q + 1) * 128]

        def proc_group(tiles, is_q):
            n = len(tiles)
            if is_q:
                src = kvg.rearrange("p (t w) -> p t w", t=GRP)[:, 0:n, 0:128]
                bslc = biasg[:, 0:128]
            else:
                src = kvg.rearrange("p (t w) -> p t w", t=GRP)[:, 0:n, 0:128]
                bslc = biasg[:, 128:256]
            ketv = ket.rearrange("p (t w) -> p t w", t=GRP)[:, 0:n, :]
            bb = bslc.unsqueeze(1).broadcast_to([128, n, 128])
            vop(lambda e: e.tensor_tensor(ketv, src, bb, ALU.add))
            kkv = kvg.rearrange("p (t w) -> p t w", t=GRP)[:, 0:n, 0:128]
            rtv = rtw[:, 0:n * 128]
            ketf = ket[:, 0:n * 128]
            ket3 = ket.rearrange("p (t w) -> p t w", t=GRP)[:, 0:n, :]
            cv = vop(lambda e: e.tensor_scalar(kkv, ket3, 0.0, None, ALU.min))
            cs = sop(lambda e: e.activation(rtv, kkv, ACT.Exp, bias=0.0, scale=1.0),
                     wait=(("VE", cv),))
            vop(lambda e: e.tensor_scalar(kkv, ket3, 0.0, None, ALU.max),
                wait=(("SE", cs),))
            rtv3 = rtv.rearrange("p (t w) -> p t w", t=n)
            vop(lambda e: e.tensor_tensor(ket3, kkv, rtv3, ALU.add))
            t0 = tiles[0]
            ket5 = ket.rearrange("p (t h j u) -> p t h j u", h=2, j=32, u=2)
            x1 = ket5[:, 0:n, :, :, 0]
            x2 = ket5[:, 0:n, :, :, 1]
            sv = sinc[:, t0 * 32:(t0 + n) * 32].rearrange("p (t j) -> p t j", t=n)
            svb = sv.unsqueeze(2).broadcast_to([128, n, 2, 32])
            cvv = cosc[:, t0 * 32:(t0 + n) * 32].rearrange("p (t j) -> p t j", t=n)
            cvb = cvv.unsqueeze(2).broadcast_to([128, n, 2, 32])
            r1v3 = rt1.rearrange("p (t h j) -> p t h j", t=GRP, h=2)[:, 0:n]
            r2v3 = rt2.rearrange("p (t h j) -> p t h j", t=GRP, h=2)[:, 0:n]
            dd = kvg.rearrange("p (t q) -> p t q", t=GRP)[:, 0:n, 0:128]
            de = dd.rearrange("p t (h u j) -> p t h u j", h=2, u=2, j=32)[:, :, :, 0, :]
            do = dd.rearrange("p t (h u j) -> p t h u j", h=2, u=2, j=32)[:, :, :, 1, :]
            vop(lambda e: e.tensor_tensor(r1v3, x1, cvb, ALU.mult))
            vop(lambda e: e.tensor_tensor(r2v3, x2, svb, ALU.mult))
            vop(lambda e: e.tensor_tensor(de, r1v3, r2v3, ALU.subtract))
            vop(lambda e: e.tensor_tensor(r1v3, x1, svb, ALU.mult))
            vop(lambda e: e.tensor_tensor(r2v3, x2, cvb, ALU.mult))
            cfin = vop(lambda e: e.tensor_tensor(do, r1v3, r2v3, ALU.add))
            stg = (rt1.bitcast(BF16).rearrange("p (t w) -> p t w", t=GRP)[:, 0:n, 0:128],
                   rt2.bitcast(BF16).rearrange("p (t w) -> p t w", t=GRP)[:, 0:n, 0:128])
            kr4 = kvg.rearrange("p (t w) -> p t w", t=GRP)[:, 0:n, 0:128]

            def tr_into(sg, dstT, tiles, dep):
                for i, tt in enumerate(tiles):
                    hh = i % 2
                    ct = top(lambda e, ii=i, s_=sg, h_=hh: e.transpose(
                        psB[:, h_ * 128:(h_ + 1) * 128], s_[:, ii, :], idb[:]),
                        wait=(("VE", dep), ("SE", S.c("SE"))))
                    sop(lambda e, t=tt, d=dstT, h_=hh: e.copy(
                        d[:, t * 128:(t + 1) * 128],
                        psB[:, h_ * 128:(h_ + 1) * 128]), wait=(("TE", ct),))

            for h in range(2):
                sg = stg[0]
                hs = slice(h * 64, (h + 1) * 64)
                vop(lambda e, s_=sg, k_=hs: e.tensor_copy(s_[:, :, 0:64], kr4[:, :, k_]),
                    wait=(("TE", S.c("TE")),))
                cfin2 = vop(lambda e, s_=sg, k_=hs: e.tensor_tensor(
                    s_[:, :, 64:128], kr4[:, :, k_], s_[:, :, 0:64], ALU.subtract))
                dstT = (QS1 if h == 0 else QS2) if is_q else (KS1 if h == 0 else KS2)
                tr_into(sg, dstT, tiles, cfin2)
                if is_q:
                    sw = stg[1]
                    vop(lambda e, a=sw, b=sg: e.tensor_copy(a[:, :, 0:64], b[:, :, 64:128]),
                        wait=(("TE", S.c("TE")),))
                    cfin3 = vop(lambda e, a=sw, b=sg: e.tensor_copy(a[:, :, 64:128],
                                                                    b[:, :, 0:64]))
                    tr_into(sw, QW1 if h == 0 else QW2, tiles, cfin3)
            return S.c("SE")

        pa_pump(10 ** 9)
        p0_per_pair = (len(p0_work) + NP - 1) // NP
        for g in range(NP):
            pb_te = S.c("TE"); pb_ve = S.c("VE"); pb_se = S.c("SE")
            for wi, wsrc in enumerate((wkT, wvT)):
                for q in range(CC):
                    ld(kv_slot(q)[:, wi * 128:(wi + 1) * 128],
                       wsrc[q * 128:(q + 1) * 128, g * 128:(g + 1) * 128],
                       wait=(("TE", pb_te), ("VE", pb_ve), ("SE", pb_se)) + st_all())
            ld(biasg[:, 0:128], bias_b[:, g * 128:(g + 1) * 128])
            ld(biasg[:, 128:256], bias_b[:, D + g * 128:D + (g + 1) * 128])
            cbl = ld(biasg[:, 256:384], bias_b[:, 2 * D + g * 128:2 * D + (g + 1) * 128])
            wld = ld_all()
            cv1 = vop(lambda e: e.memset(VH1, 1.0), wait=(("TE", pb_te),))
            vop(lambda e: e.memset(VH2, 1.0))
            vop(lambda e: e.memset(VL1, 0.0))
            cv2 = vop(lambda e: e.memset(VL2, 0.0))

            pbanks = (psP0, psP1, psS0, psS1)
            pb_evac = {0: ("VE", pb_ve), 1: ("VE", pb_ve),
                       2: ("VE", pb_ve), 3: ("VE", pb_ve)}
            xnb_use = {0: 0, 1: 0}
            for gi, g0 in enumerate(range(0, NT, GRP)):
                gts = list(range(g0, min(g0 + GRP, NT)))
                xnb = xnb_a if gi % 2 == 0 else xnb_b
                xnb3 = xnb.rearrange("p (q s) -> p q s", q=CC)
                cxl = ld(xnb3[:, :, 0:len(gts) * 128],
                         xnT3d[:, :, g0 * 128:(g0 + len(gts)) * 128],
                         wait=(("TE", xnb_use[gi % 2]),) + ((prev_sp_xn,) if gi == 0 else ()))
                for i, tt in enumerate(gts):
                    bi = i % 4
                    pp = pbanks[bi]
                    wt = list(wld) + [pb_evac[bi], cxl]
                    ctev = None
                    for q in range(CC):
                        ctev = top(lambda e, qq=q, ii=i, p_=pp, x_=xnb3: e.matmul(
                            p_[:, 0:256], x_[:, qq, ii * 128:(ii + 1) * 128],
                            kv_slot(qq), start=(qq == 0), stop=(qq == CC - 1)),
                            wait=tuple(wt) if q == 0 else (),
                            inc="TE" if q == CC - 1 else None)
                    pb_evac[bi] = ("VE", vop(lambda e, ii=i, p_=pp: e.tensor_copy(
                        kvg[:, ii * 256:(ii + 1) * 256], p_[:, 0:256]),
                        wait=(("TE", S.c("TE")),)))
                xnb_use[gi % 2] = S.c("TE")
                ng = len(gts)
                kv4 = kvg.rearrange("p (t w) -> p t w", t=GRP)[:, 0:ng, :]
                proc_group(gts, is_q=False)
                ketv = ket.rearrange("p (t w) -> p t w", t=GRP)[:, 0:ng, :]
                bvp = biasg[:, 256:384].unsqueeze(1).broadcast_to([128, ng, 128])
                vop(lambda e: e.tensor_tensor(ketv, kv4[:, :, 128:256], bvp, ALU.add),
                    wait=(("TE", S.c("TE")), ("SE", S.c("SE"))))
                for h, (VH, VL) in enumerate(((VH1, VL1), (VH2, VL2))):
                    hd = slice(h * 64, (h + 1) * 64)
                    vhd = VH.rearrange("p (t w) -> p t w", t=NT)[:, g0:g0 + ng, 0:64]
                    vld = VL.rearrange("p (t w) -> p t w", t=NT)[:, g0:g0 + ng, 0:64]
                    vop(lambda e, d_=vhd, s_=hd: e.tensor_copy(d_, ketv[:, :, s_]))
                    vop(lambda e, d_=vld, s_=hd, h_=vhd: e.tensor_tensor(
                        d_, ketv[:, :, s_], h_, ALU.subtract))
                for i, tt in enumerate(gts):
                    if not (tt <= NTO or tt == NT - 1):
                        continue
                    ctv = top(lambda e, ii=i: e.transpose(
                        psT[:, 0:128], ketv[:, ii, :], idf[:]),
                        wait=(("VE", S.c("VE")), ("SE", S.c("SE"))))
                    if tt < NTO:
                        sop(lambda e, t=tt: e.copy(
                            vtf[:, 1 + t * 128:1 + (t + 1) * 128], psT[:, 0:128]),
                            wait=(("TE", ctv),))
                    elif tt == NTO:
                        sop(lambda e: e.copy(vtf[:, OWN + 1:OWN + 2], psT[:, 0:1]),
                            wait=(("TE", ctv),))
                    if tt == NT - 1:
                        sop(lambda e: e.copy(vtf[:, 0:1], psT[:, 127:128]),
                            wait=(("TE", ctv),))
            kse = S.c("SE")

            for q in range(CC):
                ld(wq_slot(q), wqT[q * 128:(q + 1) * 128, g * 128:(g + 1) * 128],
                   wait=(("TE", S.c("TE")), ("SE", S.c("SE"))))
            wldq = ld_all()
            for gi, g0 in enumerate(range(0, NTO, GRP)):
                gts = list(range(g0, min(g0 + GRP, NTO)))
                xnb = xnb_a if gi % 2 == 0 else xnb_b
                xnb3 = xnb.rearrange("p (q s) -> p q s", q=CC)
                cxl = ld(xnb3[:, :, 0:len(gts) * 128],
                         xnT3d[:, :, g0 * 128:(g0 + len(gts)) * 128],
                         wait=(("TE", xnb_use[gi % 2]),))
                for i, tt in enumerate(gts):
                    bi = i % 4
                    pp = pbanks[bi]
                    wt = list(wldq) + [pb_evac[bi], cxl]
                    ctev = None
                    for q in range(CC):
                        ctev = top(lambda e, qq=q, ii=i, p_=pp, x_=xnb3: e.matmul(
                            p_[:, 0:128], x_[:, qq, ii * 128:(ii + 1) * 128],
                            wq_slot(qq), start=(qq == 0), stop=(qq == CC - 1)),
                            wait=tuple(wt) if q == 0 else (),
                            inc="TE" if q == CC - 1 else None)
                    pb_evac[bi] = ("VE", vop(lambda e, ii=i, p_=pp: e.tensor_copy(
                        kvg[:, ii * 256:ii * 256 + 128], p_[:, 0:128]),
                        wait=(("TE", S.c("TE")),)))
                xnb_use[gi % 2] = S.c("TE")
                proc_group(gts, is_q=True)

            chalo = S.c("SE")
            vop(lambda e: e.tensor_scalar(vtf[:, 0:1], vtf[:, 0:1], maskc[:, 0:1],
                                          None, ALU.mult), wait=(("SE", chalo),))
            vop(lambda e: e.tensor_scalar(vtf[:, OWN + 1:OWN + 2], vtf[:, OWN + 1:OWN + 2],
                                          maskc[:, 1:2], None, ALU.mult))

            qkT_done = S.c("SE")
            prev_exp = [qkT_done, qkT_done]
            prev_av = [S.c("TE"), S.c("TE")]
            prev_plo = [S.c("VE"), S.c("VE")]
            psA_free = {0: S.c("VE"), 1: S.c("VE")}
            for h in range(2):
                psA = psA0 if h == 0 else psA1
                VH = VH1 if h == 0 else VH2
                VL = VL1 if h == 0 else VL2
                for qt in range(NQT):
                    exp_c = {}
                    pf_c = {}
                    av_c = {}
                    plo_c = {}
                    hr = slice(h * 64, (h + 1) * 64)

                    KS = KS1 if h == 0 else KS2
                    QS = QS1 if h == 0 else QS2
                    QW = QW1 if h == 0 else QW2

                    def emit_s(kb):
                        pss = psS0 if kb % 2 == 0 else psS1
                        w = [("SE", pf_c[kb - 2] if kb >= 2 else prev_exp[kb % 2])]
                        top(lambda e, k=kb, p_=pss, q_=qt, K_=KS, Q_=QS: e.matmul(
                            p_[:, 0:QTW], K_[:, k * 128:(k + 1) * 128],
                            Q_[:, q_ * QTW:(q_ + 1) * QTW],
                            start=True, stop=False), wait=tuple(w), inc=None)
                        cs_mm = top(lambda e, k=kb, p_=pss, q_=qt, K_=KS, Q_=QW: e.matmul(
                            p_[:, 0:QTW], K_[:, k * 128:(k + 1) * 128],
                            Q_[:, q_ * QTW:(q_ + 1) * QTW],
                            start=False, stop=True))
                        sl = slice((kb % 2) * 512, (kb % 2) * 512 + QTW)
                        w2b = [("TE", cs_mm),
                               ("TE", av_c[kb - 2] if kb >= 2 else prev_av[kb % 2])]
                        exp_c[kb] = sop(lambda e, p_=pss, s_=sl: e.activation(
                            PhiB[:, s_], p_[:, 0:QTW],
                            ACT.Exp, bias=scal[:, 19:20], scale=0.125), wait=tuple(w2b),
                            drain=False)
                        w2 = [("VE", plo_c[kb - 2] if kb >= 2 else prev_plo[kb % 2])]
                        pf_c[kb] = sop(lambda e, p_=pss, s_=sl: e.activation(
                            expbuf[:, s_], p_[:, 0:QTW],
                            ACT.Exp, bias=scal[:, 19:20], scale=0.125), wait=tuple(w2),
                            drain=False)
                        plo_c[kb] = vop(lambda e, s_=sl: e.tensor_tensor(
                            PloB[:, s_], expbuf[:, s_], PhiB[:, s_], ALU.subtract),
                            wait=(("SE", pf_c[kb]),), drain=False)

                    def emit_av(kb):
                        sl = slice((kb % 2) * 512, (kb % 2) * 512 + QTW)
                        wav = [("SE", exp_c[kb])]
                        if kb == 0:
                            wav.append(("VE", psA_free[h]))
                        top(lambda e, k=kb, pA=psA, V_=VH, s_=sl: e.matmul(
                            pA[0:65, 0:QTW], V_[:, k * 65:(k + 1) * 65],
                            PhiB[:, s_], start=(k == 0), stop=False),
                            wait=tuple(wav), inc=None)
                        top(lambda e, k=kb, pA=psA, V_=VL, s_=sl: e.matmul(
                            pA[0:65, 0:QTW], V_[:, k * 65:(k + 1) * 65],
                            PhiB[:, s_], start=False, stop=False), inc=None)
                        av_c[kb] = top(lambda e, k=kb, pA=psA, V_=VH, s_=sl: e.matmul(
                            pA[0:65, 0:QTW], V_[:, k * 65:(k + 1) * 65],
                            PloB[:, s_], start=False, stop=(k == NT - 1)),
                            wait=(("VE", plo_c[kb]),))

                    emit_s(0)
                    for kb in range(1, NT):
                        emit_s(kb)
                        emit_av(kb - 1)
                    emit_av(NT - 1)
                    prev_exp = [pf_c[NT - 2], pf_c[NT - 1]]
                    prev_av = [av_c[NT - 2], av_c[NT - 1]]
                    prev_plo = [plo_c[NT - 2], plo_c[NT - 1]]
                    crc = vop(lambda e, pA=psA: e.reciprocal(rcpr[0:1, 0:QTW], pA[64:65, 0:QTW]),
                              wait=(("TE", av_c[NT - 1]), ("VE", S.c("VE"))))
                    crx = top(lambda e: e.matmul(psT[0:64, 0:QTW], onesc[0:1, 0:64],
                                                 rcpr[0:1, 0:QTW], start=True, stop=True),
                              wait=(("VE", crc),))
                    sf = sfm[h * 64:(h + 1) * 64, qt * QTW:(qt + 1) * QTW]
                    cev = vop(lambda e, s=sf, pA=psA: e.tensor_copy(s, pA[0:64, 0:QTW]))
                    psA_free[h] = vop(
                        lambda e, s=sf: e.tensor_tensor(s, s, psT[0:64, 0:QTW], ALU.mult),
                        wait=(("TE", crx),))
                    s_idx = h * NQT + qt
                    wtarget = ((g * 2 * NQT + s_idx + 1) * len(p0_work)) // (NP * 2 * NQT)
                    while p0_state["widx"] < wtarget:
                        weave_slot()

            wl0 = lepw[:, g * 3:g * 3 + 1]; wl1 = lepw[:, g * 3 + 1:g * 3 + 2]
            wl2 = lepw[:, g * 3 + 2:g * 3 + 3]; bl = lepb[:, g:g + 1]
            sf_all = sfm[:, 0:OWN]
            vop(lambda e, w_=wl0: e.scalar_tensor_tensor(sf_all, vtf[:, 0:OWN], w_, sf_all,
                                                         ALU.mult, ALU.add),
                wait=(("TE", S.c("TE")),))
            vop(lambda e, w_=wl1: e.scalar_tensor_tensor(sf_all, vtf[:, 1:OWN + 1], w_, sf_all,
                                                         ALU.mult, ALU.add))
            vop(lambda e, w_=wl2: e.scalar_tensor_tensor(sf_all, vtf[:, 2:OWN + 2], w_, sf_all,
                                                         ALU.mult, ALU.add))
            cfin = vop(lambda e, b_=bl: e.tensor_scalar(sf_all, sf_all, b_, None, ALU.add))
            for tt in range(NTO):
                ct = top(lambda e, t=tt: e.transpose(
                    psT[:, 0:128], sfm[:, t * 128:(t + 1) * 128], idf[:]),
                    wait=(("VE", cfin), ("SE", S.c("SE"))))
                cse = sop(lambda e: e.copy(sstg[:, 0:128], psT[:, 0:128]), wait=(("TE", ct),))
                stq(s_d[tt * 128:(tt + 1) * 128, g * 128:(g + 1) * 128], sstg[:, 0:128],
                    wait=(("SE", cse),))
                S.emit("scalar", None, wait=st_all())


        while p0_state["widx"] < len(p0_work):
            weave_slot()
        emit_p0_flush()

        S.barrier()

        cur2 = Cursor()
        xq_t2 = cur2.bf16(2 * CC * 128)
        hid = cur2.f32(2 * I)
        hq_t = cur2.bf16(I)
        hqT = cur2.bf16(2 * NI * 128)
        gstg = cur2.f32(2 * 1024)
        wg_a = cur2.bf16(CC * 512); wg_b = cur2.bf16(CC * 512)
        wu_a = cur2.bf16(CC * 512); wu_b = cur2.bf16(CC * 512)
        wd_a = cur2.bf16(NI * 128); wd_b = cur2.bf16(NI * 128)
        sgn2 = cur2.f32(I)
        st_s = cur2.f32(1024)
        st_x = cur2.f32(1024)
        blk2 = cur2.f32(256)
        chain2 = cur2.off

        xqv = xq_t2.rearrange("p (t q s) -> p t q s", t=2, q=CC)
        xq_pref = {}
        xq_guard = {0: None, 1: None}

        def issue_xq_load(k):
            par_ = k % 2
            ww = (xq_guard[par_],) if xq_guard[par_] else ()
            xq_pref[k] = ld(xq_t2.rearrange("p (t w) -> p t w", t=2)[:, par_, :],
                            xqT_d[:, k * 1024:(k + 1) * 1024], wait=ww)
        hid3 = hid.rearrange("p (t w) -> p t w", t=2)
        hqT3 = hqT.rearrange("p (t ic w) -> p t ic w", t=2, ic=NI)
        g3 = gstg.rearrange("p (t w) -> p t w", t=2)
        NDSL = 8
        gu_pref = {}
        p2s = {"wbuf_mm": {0: None, 1: None}, "se_pP0": 0, "se_pP1": 0,
               "ge_hd": 0}
        wdbuf_mm = {0: None, 1: None}
        wd_pref = {}
        se_pS = {0: 0, 1: 0}
        q_done = {0: 0, 1: 0}
        sp_guard = {0: None, 1: None}
        ge_c1 = {"c1": 0, "c2": 0}
        sx_pref = {}

        def issue_sx_load(k):
            w1 = (("GE", ge_c1["c1"]),) if ge_c1["c1"] else ()
            w2 = (("GE", ge_c1["c2"]),) if ge_c1["c2"] else ()
            cls_ = ld(st_s, s_d[k * 128:(k + 1) * 128, :], wait=w1)
            clx_ = ld(st_x, x_full[k * 128:(k + 1) * 128, :], wait=w2)
            sx_pref[k] = (cls_, clx_)

        def issue_gu_load(k, isl):
            par = (k * NISL + isl) % 2
            wg = wg_a if par == 0 else wg_b
            wu = wu_a if par == 0 else wu_b
            ww = (p2s["wbuf_mm"][par],) if p2s["wbuf_mm"][par] else ()
            clg = ld(wg, wgq_d[:, isl * CC * 512:(isl + 1) * CC * 512], wait=ww)
            clu = ld(wu, wuq_d[:, isl * CC * 512:(isl + 1) * CC * 512], wait=ww)
            gu_pref[(k, isl)] = (clg, clu, wg, wu)

        def issue_wd_load(k, csl):
            par = csl % 2
            wd = wd_a if par == 0 else wd_b
            ww = (wdbuf_mm[par],) if wdbuf_mm[par] else ()
            cld = ld(wd, wdq_d[:, csl * NI * 128:(csl + 1) * NI * 128], wait=ww)
            wd_pref[(k, csl)] = (cld, wd)

        issue_gu_load(0, 0)
        issue_wd_load(0, 0)
        issue_wd_load(0, 1)
        hd_last = {0: 0, 1: 0}

        def emit_gu(k):
            par = k % 2
            cu = None
            for isl in range(NISL):
                clg, clu, wg, wu = gu_pref.pop((k, isl))
                wg3 = wg.rearrange("p (q s) -> p q s", q=CC)
                wu3 = wu.rearrange("p (q s) -> p q s", q=CC)
                wtm = [clg, clu, xq_pref[k], ("SE", p2s["se_pP1"])]
                cg = None
                for q in range(CC):
                    cg = top(lambda e, qq=q, p2_=par, w3=wg3: e.matmul(
                        psP0[:, 0:512], xqv[:, p2_, qq, :],
                        w3[:, qq, :], start=(qq == 0), stop=(qq == CC - 1)),
                        wait=tuple(wtm) if q == 0 else (),
                        inc="TE" if q == CC - 1 else None)
                cu = None
                for q in range(CC):
                    cu = top(lambda e, qq=q, p2_=par, w3=wu3: e.matmul(
                        psP1[:, 0:512], xqv[:, p2_, qq, :],
                        w3[:, qq, :], start=(qq == 0), stop=(qq == CC - 1)),
                        inc="TE" if q == CC - 1 else None)
                wsl = [("TE", cg)]
                if isl == 0 and q_done[par]:
                    wsl.append(("VE", q_done[par]))
                csl = sop(lambda e, p_=par, i_=isl: e.activation(
                    hid3[:, p_, i_ * 512:(i_ + 1) * 512], psP0[:, 0:512],
                    ACT.Silu, bias=0.0, scale=scal[:, 12:13]), wait=tuple(wsl))
                p2s["se_pP0"] = csl
                cup = sop(lambda e: e.activation(rcpr[:, 0:512], psP1[:, 0:512],
                                                 ACT.Copy, bias=0.0,
                                                 scale=scal[:, 13:14]),
                          wait=(("TE", cu), ("GE", p2s["ge_hd"])))
                p2s["se_pP1"] = cup
                hd = gop(lambda e, p_=par, i_=isl: e.tensor_tensor(
                    hid3[:, p_, i_ * 512:(i_ + 1) * 512],
                    hid3[:, p_, i_ * 512:(i_ + 1) * 512], rcpr[:, 0:512], ALU.mult),
                    wait=(("SE", cup),))
                p2s["ge_hd"] = hd
                p2s["wbuf_mm"][(k * NISL + isl) % 2] = ("TE", cu)
                if isl + 1 < NISL:
                    issue_gu_load(k, isl + 1)
                elif k + 1 < NTO:
                    issue_gu_load(k + 1, 0)
            hd_last[par] = hd
            xq_guard[par] = ("TE", cu)
            if k + 2 < NTO:
                issue_xq_load(k + 2)

        issue_xq_load(0)
        issue_xq_load(1)
        emit_gu(0)
        issue_sx_load(0)
        for k in range(NTO):
            par = k % 2
            csg = sop(lambda e, p_=par: e.sign(sgn2, hid3[:, p_, :]),
                      wait=(("GE", hd_last[par]), ("VE", q_done[1 - par])))
            S.emit("vector", None, wait=(("SE", csg), ("TE", S.c("TE"))))
            cq = quant_tile(chain2, blk2, hid3[:, par, :], sgn2, hq_t, I,
                            scal[:, 4:5], scal[:, 5:6])
            q_done[par] = cq
            if k + 1 < NTO:
                emit_gu(k + 1)
            for ic in range(NI):
                hh = ic % 2
                ct = top(lambda e, i_=ic, h_=hh: e.transpose(
                    psB[:, h_ * 128:(h_ + 1) * 128],
                    hq_t[:, i_ * 128:(i_ + 1) * 128], idb[:]),
                    wait=(("VE", cq), ("SE", S.c("SE"))))
                sop(lambda e, p_=par, i_=ic, h_=hh: e.copy(
                    hqT3[:, p_, i_, :], psB[:, h_ * 128:(h_ + 1) * 128]),
                    wait=(("TE", ct),))
            se_cp = S.c("SE")
            for csl_i in range(NDSL):
                cld, wd = wd_pref.pop((k, csl_i))
                wd3 = wd.rearrange("p (ic s) -> p ic s", ic=NI)
                psS = psS0 if csl_i % 2 == 0 else psS1
                wtd = [cld, ("SE", se_cp), ("SE", se_pS[csl_i % 2])]
                cd = None
                for ic in range(NI):
                    cd = top(lambda e, p_=par, i_=ic, w3=wd3, ps=psS: e.matmul(
                        ps[:, 0:128], hqT3[:, p_, i_, :], w3[:, i_, :],
                        start=(i_ == 0), stop=(i_ == NI - 1)),
                        wait=tuple(wtd) if ic == 0 else (),
                        inc="TE" if ic == NI - 1 else None)
                wdbuf_mm[csl_i % 2] = ("TE", cd)
                if csl_i + 2 < NDSL:
                    issue_wd_load(k, csl_i + 2)
                elif k + 1 < NTO:
                    issue_wd_load(k + 1, csl_i + 2 - NDSL)
                wds = [("TE", cd)]
                if csl_i == 0 and sp_guard[par]:
                    wds.append(sp_guard[par])
                cds = sop(lambda e, p_=par, c_=csl_i, ps=psS: e.activation(
                    g3[:, p_, c_ * 128:(c_ + 1) * 128], ps[:, 0:128],
                    ACT.Silu, bias=0.0, scale=scal[:, 14:15]), wait=tuple(wds))
                se_pS[csl_i % 2] = cds
            cls_, clx_ = sx_pref.pop(k)
            c1 = gop(lambda e, p_=par: e.tensor_tensor(
                g3[:, p_, :], g3[:, p_, :], st_s, ALU.mult),
                wait=(("SE", S.c("SE")), cls_))
            ge_c1["c1"] = c1
            c2 = gop(lambda e, p_=par: e.tensor_tensor(
                g3[:, p_, :], g3[:, p_, :], st_x, ALU.add),
                wait=(clx_,))
            ge_c1["c2"] = c2
            sp_guard[par] = stq(out_t[k * 128:(k + 1) * 128, :], g3[:, par, :],
                                wait=(("GE", c2),))
            if k + 1 < NTO:
                issue_sx_load(k + 1)

        S.barrier()

        S.emit("gpsimd", None, wait=st_all())

        nc._script = S
        with nc.Block() as block:
            def make_runner(eng_name):
                def runner(eng):
                    for fn, waits, inc, inc_by, drain in S.ops[eng_name]:
                        for sname, val in waits:
                            if val > 0:
                                eng.wait_ge(sems[sname], val)
                        if drain and fn is not None:
                            eng.drain()
                        if fn is None:
                            continue
                        ins = fn(eng)
                        if inc is not None and ins is not None:
                            ins.then_inc(sems[inc], inc_by)
                return runner
            block.sync(make_runner("sync"))
            block.tensor(make_runner("tensor"))
            block.vector(make_runner("vector"))
            block.scalar(make_runner("scalar"))
            block.gpsimd(make_runner("gpsimd"))

    return nc


def _rope_tables(n, d=64, base=10000.0):
    try:
        import jax
        import jax.numpy as jnp
        cpu = jax.devices("cpu")[0]
        with jax.default_device(cpu):
            position = jnp.arange(n, dtype=jnp.float32)[:, None]
            div_term = jnp.exp(jnp.arange(0, d, 2, dtype=jnp.float32)
                               * (-np.log(base) / d))
            ang = position * div_term
            return np.asarray(jnp.sin(ang)), np.asarray(jnp.cos(ang))
    except Exception:
        position = np.arange(n, dtype=np.float32)[:, None]
        div_term = np.exp(np.arange(0, d, 2, dtype=np.float32)
                          * np.float32(-np.log(base) / d)).astype(np.float32)
        ang = (position * div_term).astype(np.float32)
        return np.sin(ang).astype(np.float32), np.cos(ang).astype(np.float32)


def _tile_tables(tab, NT):
    return np.ascontiguousarray(
        tab.reshape(NT, 128, 32).transpose(1, 0, 2).reshape(128, NT * 32))


def make_core_inputs(x, w_qkv, b_qkv, w_lepe, b_lepe, gate_w, up_w, down_w,
                     s_in, s_in_down, b, j, seq):
    OWN = seq // 2
    NT = seq // 128
    roll = j * OWN
    xr = np.ascontiguousarray(np.roll(np.asarray(x[b]), -roll, axis=0)).astype(np.float32)
    sin_full, cos_full = _rope_tables(seq)
    sin_r = np.roll(sin_full, -roll, axis=0)
    cos_r = np.roll(cos_full, -roll, axis=0)
    D = 1024
    f32 = np.float32
    wq = np.ascontiguousarray(np.asarray(w_qkv[0:D]).T).astype(f32)
    wk = np.ascontiguousarray(np.asarray(w_qkv[D:2 * D]).T).astype(f32)
    wv = np.ascontiguousarray(np.asarray(w_qkv[2 * D:3 * D]).T).astype(f32)
    lw = np.ascontiguousarray(
        np.asarray(w_lepe).reshape(8, 128, 3).transpose(1, 0, 2).reshape(128, 24)).astype(f32)
    lb = np.ascontiguousarray(
        np.asarray(b_lepe).reshape(8, 128).T).astype(f32)
    maskL = 1.0 if j == 1 else 0.0
    maskR = 1.0 if j == 0 else 0.0
    return {
        "x_full": xr,
        "wqT": wq, "wkT": wk, "wvT": wv,
        "bias_b": np.ascontiguousarray(np.broadcast_to(np.asarray(b_qkv, f32), (128, 3 * D))),
        "gate_w": np.asarray(gate_w, f32), "up_w": np.asarray(up_w, f32),
        "down_w": np.asarray(down_w, f32),
        "s_in_b": np.full((128, 1), np.asarray(s_in).reshape(-1)[0], f32),
        "s_ind_b": np.full((128, 1), np.asarray(s_in_down).reshape(-1)[0], f32),
        "lepe_w": lw, "lepe_b": lb,
        "sin_t": _tile_tables(sin_r, NT), "cos_t": _tile_tables(cos_r, NT),
        "masks": np.ascontiguousarray(np.broadcast_to(
            np.array([maskL, maskR], f32), (128, 2))),
        "ident_f": np.eye(128, dtype=f32),
        "ident_b": np.eye(128).astype(ml_dtypes.bfloat16),
        "ones_in": np.ones((128, 128), f32),
    }


def kernel(L0_attn_x, L0_attn_w_qkv, L0_attn_b_qkv, L0_attn_w_lepe, L0_attn_b_lepe,
           L0_ffn_gate_w, L0_ffn_up_w, L0_ffn_down_w, L0_ffn_s_in, L0_ffn_s_in_down):
    SEQ = 4096
    OWN = SEQ // 2
    x = np.asarray(L0_attn_x, np.float32)
    in_maps = []
    for core in range(8):
        b, j = core // 2, core % 2
        in_maps.append(make_core_inputs(
            x, L0_attn_w_qkv, L0_attn_b_qkv, L0_attn_w_lepe, L0_attn_b_lepe,
            L0_ffn_gate_w, L0_ffn_up_w, L0_ffn_down_w, L0_ffn_s_in, L0_ffn_s_in_down,
            b, j, SEQ))
    nc = build_nc(seq=SEQ, drains=True)
    res = run_bass_kernel_spmd(nc, in_maps, core_ids=list(range(8)))
    out = np.empty((4, SEQ, 1024), np.float32)
    for core in range(8):
        b, j = core // 2, core % 2
        out[b, j * OWN:(j + 1) * OWN, :] = np.asarray(res.results[core]["out_s"])
    return out
